# version 19
# baseline (speedup 1.0000x reference)
import numpy as np
import ml_dtypes
from contextlib import ExitStack

import concourse.bass as bass
import concourse.mybir as mybir
from concourse.bass_utils import run_bass_kernel_spmd

F32 = mybir.dt.float32
BF16 = mybir.dt.bfloat16
AF = mybir.ActivationFunctionType
ALU = mybir.AluOpType
AX = mybir.AxisListType

T = 2048
D = 1024
NC = 8
LAT = 256
RD = 32
NEXP = 8
NTT = 16
NTG = 4
BIGNEG = -1e30
EPS = 1e-5

BUILD_UPTO = "C"
TAPS = ()
CORES = None

_CACHE = {}


class B:
    def __init__(self, nc):
        self.nc = nc
        self.q = {k: [] for k in "tvsgy"}
        self.cnt = {}
        self.ctx = ExitStack()

    def sem(self, name):
        h = self.ctx.enter_context(self.nc.semaphore(name))
        self.cnt[h] = 0
        return h

    def sb(self, name, shape, dt, stack=None):
        return (stack or self.ctx).enter_context(self.nc.sbuf_tensor(name, shape, dt))

    def ps(self, name, shape, dt=F32, stack=None):
        return (stack or self.ctx).enter_context(self.nc.psum_tensor(name, shape, dt))

    def bump(self, sem, by=1):
        self.cnt[sem] += by
        return self.cnt[sem]

    def emit(self, eng, fn):
        self.q[eng].append(fn)

    def wait(self, eng, sem, val):
        if val and val > 0:
            self.emit(eng, lambda e, s=sem, v=val: e.wait_ge(s, v))

    def op(self, eng, meth, *args, inc=None, incby=1, **kw):
        v = self.bump(inc, incby) if inc is not None else None
        def f(e, m=meth, a=args, k=kw, i=inc, ib=incby):
            r = getattr(e, m)(*a, **k)
            if i is not None:
                r.then_inc(i, ib)
        self.emit(eng, f)
        return v

    def mm(self, out, lhsT, rhs, start, stop, inc=None):
        return self.op("t", "matmul", out, lhsT, rhs, start=start, stop=stop, inc=inc)


def _emit_block(bld):
    with bld.nc.Block() as block:
        @block.tensor
        def _(e):
            for f in bld.q["t"]:
                f(e)
        @block.vector
        def _(e):
            for f in bld.q["v"]:
                f(e)
        @block.scalar
        def _(e):
            for f in bld.q["s"]:
                f(e)
        @block.gpsimd
        def _(e):
            for f in bld.q["g"]:
                f(e)
        @block.sync
        def _(e):
            for f in bld.q["y"]:
                f(e)


def _host_constants():
    c = {}
    p = np.arange(128)[:, None]
    j = np.arange(128)[None, :]
    mA = np.where(p >= j, 0.0, BIGNEG).astype(np.float32)
    mB = np.where(p <= j, 0.0, BIGNEG).astype(np.float32)
    c["mask_band"] = np.concatenate([mA, mB, mA, mB], axis=1).astype(ml_dtypes.bfloat16)
    c["mask_inner"] = np.concatenate(
        [np.full((128, 384), BIGNEG, np.float32), mB,
         np.zeros((128, 512), np.float32)], axis=1).astype(ml_dtypes.bfloat16)
    ms = np.full((32, 16, 128), BIGNEG, np.float32)
    qv = np.arange(128)
    for qb in range(16):
        for jj in range(32):
            ms[jj, qb, :] = np.where((64 * jj) < (128 * qb + qv - 128), 0.0, BIGNEG)
    c["mask_str"] = ms.reshape(32, 16 * 128).astype(ml_dtypes.bfloat16)
    c["mask_allneg"] = np.full((128, 256), BIGNEG, np.float32).astype(ml_dtypes.bfloat16)
    c["iden_bf"] = np.eye(128, dtype=ml_dtypes.bfloat16)
    c["iden_f"] = np.eye(128, dtype=np.float32)
    c["ones_row_bf"] = np.ones((33, 128), dtype=ml_dtypes.bfloat16)
    c["ones_blk_bf"] = np.ones((65, 128), dtype=ml_dtypes.bfloat16)
    s2m = np.zeros((128, 2), np.float32)
    s2m[0:64, 0] = 1.0
    s2m[64:128, 1] = 1.0
    c["sel2"] = s2m.astype(ml_dtypes.bfloat16)
    c["ones_col_bf"] = np.ones((128, 1), dtype=ml_dtypes.bfloat16)
    c["ones_row_f"] = np.ones((65, 512), dtype=np.float32)
    c["ones_col_f"] = np.ones((128, 1), dtype=np.float32)
    sel = np.zeros((128, NTT, 32), np.float32)
    for tile in range(NTT):
        for i in range(2):
            sel[i * 64, tile, tile * 2 + i] = 1.0
    c["vsel"] = sel.reshape(128, NTT * 32).astype(ml_dtypes.bfloat16)
    inv = 1.0 / (10000.0 ** (np.arange(0, RD, 2, dtype=np.float64) / RD))
    tpos = np.arange(T, dtype=np.float64)
    fr = tpos[:, None] * inv[None, :]
    emb = np.concatenate([fr, fr], axis=-1)
    cos = np.cos(emb).astype(np.float32)
    sin = np.sin(emb).astype(np.float32)
    c["cosE"] = np.ascontiguousarray(np.concatenate([cos, cos], axis=1))
    c["sinE"] = np.ascontiguousarray(np.concatenate([sin, sin], axis=1))
    return c


def _host_prep(inputs):
    t = lambda a: np.ascontiguousarray(np.asarray(a, dtype=np.float32).T)
    w = {}
    wqkv = np.asarray(inputs["L0_attn_w_qkv"], dtype=np.float32)
    w["wqT"] = t(wqkv[0:D])
    w["wkT"] = t(wqkv[D:2 * D])
    w["wvT"] = t(wqkv[2 * D:3 * D])
    w["woT"] = t(inputs["L0_attn_w_out"])
    w["g1"] = np.ascontiguousarray(
        np.asarray(inputs["L0_ffn_ln1_w"], dtype=np.float32).reshape(NC, 128).T)
    w["g2"] = np.ascontiguousarray(
        np.asarray(inputs["L0_ffn_ln2_w"], dtype=np.float32).reshape(NC, 128).T)
    w["kvdT"] = t(inputs["L0_ffn_kv_d"])
    w["qdT"] = t(inputs["L0_ffn_q_d"])
    w["kuT"] = t(inputs["L0_ffn_k_u"])
    w["quT"] = t(inputs["L0_ffn_q_u"])
    w["vuT"] = t(inputs["L0_ffn_v_u"])
    w["rqT"] = t(inputs["L0_ffn_rope_q_w"])
    w["rkT"] = t(inputs["L0_ffn_rope_k_w"])
    w["owT2"] = t(inputs["L0_ffn_o_w"])
    w["rwT"] = t(inputs["L0_ffn_router_w"])
    w["rbias"] = np.asarray(inputs["L0_ffn_routing_bias"], dtype=np.float32).reshape(1, 7)
    sg = np.asarray(inputs["L0_ffn_shared_gate"], dtype=np.float32)
    su = np.asarray(inputs["L0_ffn_shared_up"], dtype=np.float32)
    sd = np.asarray(inputs["L0_ffn_shared_down"], dtype=np.float32)
    rg = np.asarray(inputs["L0_ffn_routed_gate"], dtype=np.float32)
    ru = np.asarray(inputs["L0_ffn_routed_up"], dtype=np.float32)
    rd = np.asarray(inputs["L0_ffn_routed_down"], dtype=np.float32)
    w["gatesT"] = np.ascontiguousarray(np.concatenate([sg, rg], 0).transpose(0, 2, 1))
    w["upsT"] = np.ascontiguousarray(np.concatenate([su, ru], 0).transpose(0, 2, 1))
    w["downsT"] = np.ascontiguousarray(np.concatenate([sd, rd], 0).transpose(0, 2, 1))
    return w


def build_program(upto="C", taps=()):
    nc = bass.Bass("TRN2", target_bir_lowering=False)
    bld = B(nc)
    taps = set(taps)

    dr = {}
    def din(name, shape, dt=F32):
        dr[name] = nc.dram_tensor(name, list(shape), dt, kind="ExternalInput").ap()
    din("x", [D, T])
    din("wqT", [D, D]); din("wkT", [D, D]); din("wvT", [D, D]); din("woT", [D, D])
    din("g1", [128, NC]); din("g2", [128, NC])
    din("kvdT", [D, LAT]); din("qdT", [D, LAT])
    din("kuT", [LAT, D]); din("quT", [LAT, D]); din("vuT", [LAT, D])
    din("rqT", [LAT, 512]); din("rkT", [D, 512])
    din("owT2", [D, D])
    din("rwT", [D, 7]); din("rbias", [1, 7])
    din("gatesT", [NEXP, D, 2048]); din("upsT", [NEXP, D, 2048])
    din("downsT", [NEXP, 2048, D])
    din("cosE", [T, 64]); din("sinE", [T, 64])
    din("mask_band", [128, 512], BF16)
    din("mask_inner", [128, 1024], BF16)
    din("mask_str", [32, 16 * 128], BF16)
    din("mask_allneg", [128, 256], BF16)
    din("iden_bf", [128, 128], BF16); din("iden_f", [128, 128], F32)
    din("ones_row_bf", [33, 128], BF16); din("ones_col_bf", [128, 1], BF16)
    din("ones_blk_bf", [65, 128], BF16)
    din("sel2", [128, 2], BF16)
    din("ones_row_f", [65, 512], F32); din("ones_col_f", [128, 1], F32)
    din("vsel", [128, NTT * 32], BF16)

    out = nc.dram_tensor("out", [T, D], F32, kind="ExternalOutput").ap()
    tap_out = {}
    def mktap(name, shape, dt=F32):
        if name in taps:
            tap_out[name] = nc.dram_tensor("tap_" + name, list(shape), dt,
                                           kind="ExternalOutput").ap()
    mktap("x1", [128, NC * T]); mktap("h", [128, NC * T], BF16)
    mktap("x2", [128, NC * T]); mktap("h2", [128, NC * T], BF16)
    mktap("wgt", [128, NTT * 8]); mktap("x3", [128, NTT * D])
    mktap("prb", [128, NTT * 8])
    mktap("ao", [128, NC * T], BF16); mktap("yb", [128, NC * T], BF16)
    mktap("yraw", [64, T]); mktap("sums", [1, T])
    mktap("sumsb", [1, T]); mktap("mbb", [128, 2])
    mktap("yraw2", [64, T]); mktap("sums2", [1, T])
    for _c in range(NC):
        mktap("kc%d" % _c, [128, T], BF16)
        mktap("mb%d" % _c, [128, 2])
    mktap("rtb3", [128, 512], BF16)
    mktap("wrk3", [128, 512], BF16)
    mktap("sumsb2", [1, T]); mktap("mbb2", [128, 2])
    mktap("q0", [128, T], BF16); mktap("k0", [128, T], BF16)
    mktap("v0", [128, NTT * 130], BF16)
    mktap("lat", [128, 2 * T], BF16)

    LD = bld.LD = bld.sem("LD")
    LG = bld.LG = bld.sem("LG")
    LO = bld.LO = bld.sem("LO")
    PT = bld.PT = bld.sem("PT")
    VT = bld.VT = bld.sem("VT")
    ST = bld.ST = bld.sem("ST")
    GT = bld.GT = bld.sem("GT")
    LU = bld.LU = bld.sem("LU")
    CSLD = bld.CSLD = bld.sem("CSLD")
    XLD = bld.XLD = bld.sem("XLD")

    XBBF = bld.sb("XBBF", [128, NC * T], BF16)
    STG = bld.sb("STG", [128, 2 * 1024], F32)
    IDF = bld.sb("IDF", [128, 128], F32)
    ONER_BF = bld.sb("ONER_BF", [33, 128], BF16)
    ONEC_BF = bld.sb("ONEC_BF", [128, 1], BF16)
    ONER_F = bld.sb("ONER_F", [65, 512], F32)
    ONEC_F = bld.sb("ONEC_F", [128, 1], F32)
    G1 = bld.sb("G1", [128, NC], F32)
    G2 = bld.sb("G2", [128, NC], F32)
    STA = bld.sb("STA", [128, 512], F32)
    STB = bld.sb("STB", [128, 512], F32)
    STC = bld.sb("STC", [128, 512], F32)
    BCR = bld.sb("BCR", [1, 512], F32)
    LNSB = bld.sb("LNSB", [128, 512], BF16)
    LNSQ = bld.sb("LNSQ", [128, 512], BF16)
    RWT = bld.sb("RWT", [128, NC * 7], F32)
    RBIAS = bld.sb("RBIAS", [1, 7], F32)
    SGR = bld.sb("SGR", [128, 512], F32)

    abctx = ExitStack()
    RES = bld.sb("RES", [128, NC * T], F32, stack=abctx)
    cglobal = ExitStack()
    LGT = bld.sb("LGT", [7, 512], F32, stack=cglobal)
    PRBp = bld.sb("PRBp", [128, NTT * 8], F32, stack=cglobal)
    WGT = bld.sb("WGT", [128, NTT * 8], F32, stack=cglobal)
    TOPS = bld.sb("TOPS", [128, 8], F32, stack=cglobal)
    aoctx = ExitStack()
    AOBF = bld.sb("AOBF", [128, NC * T], BF16, stack=aoctx)
    atc = ExitStack()
    IDB = bld.sb("IDB", [128, 128], BF16, stack=atc)
    PTRB = bld.sb("PTRB", [128, 3 * 512], BF16, stack=atc)
    MBAND2 = bld.sb("MBAND2", [128, 512], BF16, stack=atc)
    MBI = bld.sb("MBI", [128, 1024], BF16, stack=atc)
    YS = bld.sb("YS", [65, T], F32, stack=atc)
    PTSB = bld.sb("PTSB", [33, 2 * 512], BF16, stack=atc)
    MBB = bld.sb("MBB", [128, 2], F32, stack=atc)
    MBS = bld.sb("MBS", [1, 2], F32, stack=atc)
    BND = bld.sb("BND", [2, 16], F32, stack=atc)
    SEL2 = bld.sb("SEL2", [128, 2], BF16, stack=atc)

    war = {
        "negc": [(PT, 0)], "negt": [(LG, 0)], "negm": [(PT, 0)],
        "ptr": [[(PT, 0)] for _ in range(3)],
        "ptr2": [[(PT, 0)], [(PT, 0)]],
        "rrow": [[(PT, 0)], [(PT, 0)]],
        "rtb2": [[(PT, 0)], [(PT, 0)]],
        "lnsb": [(PT, 0)], "lnsq": [(PT, 0)],
        "sgr": [[(PT, 0)], [(PT, 0)]],
        "cs": [[(VT, 0)]],
        "rtb": [[(PT, 0)], [(PT, 0)]],
        "yraw": [(VT, 0)], "sums": [(PT, 0)],
    }

    def warwait(eng, key, slot=None):
        cell = war[key] if slot is None else war[key][slot]
        sem, val = cell[0]
        bld.wait(eng, sem, val)

    def warset(key, sem, val, slot=None):
        cell = war[key] if slot is None else war[key][slot]
        cell[0] = (sem, val)

    stg_free = [0, 0]
    stg_next = [0]

    def emit_bound(qring, kring, ps, ps_free):
        for mi, ring in ((0, qring), (1, kring)):
            for tg in range(NTG):
                warwait("s", "lnsq")
                sq = bld.op("s", "activation", LNSQ[:, :],
                            ring[:, tg * 512:(tg + 1) * 512], AF.Square, inc=ST)
                sem, val = ps_free[0]
                bld.wait("t", sem, val)
                bld.wait("t", ST, sq)
                pv = bld.mm(ps[0:2, 0:512], SEL2[:, :], LNSQ[:, :],
                            start=True, stop=True, inc=PT)
                warset("lnsq", PT, pv)
                bld.wait("v", PT, pv)
                rv = bld.op("v", "reduce_max", BND[0:2, mi * 4 + tg: mi * 4 + tg + 1],
                            ps[0:2, 0:512], axis=AX.X, inc=VT)
                ps_free[0] = (VT, rv)
        bld.wait("v", VT, bld.cnt[VT])
        r1 = bld.op("v", "reduce_max", BND[0:2, 8:9], BND[0:2, 0:4], axis=AX.X, inc=VT)
        bld.wait("v", VT, r1)
        r2 = bld.op("v", "reduce_max", BND[0:2, 9:10], BND[0:2, 4:8], axis=AX.X, inc=VT)
        bld.wait("v", VT, r2)
        r3 = bld.op("v", "tensor_add", BND[0:2, 10:11], BND[0:2, 8:9],
                    BND[0:2, 9:10], inc=VT)
        bld.wait("v", VT, r3)
        r4 = bld.op("v", "tensor_scalar_mul", BND[0:2, 10:11], BND[0:2, 10:11],
                    -0.0625, inc=VT)
        bld.wait("v", VT, r4)
        r5 = bld.op("v", "tensor_scalar_add", BND[0:2, 10:11], BND[0:2, 10:11],
                    36.0, inc=VT)
        bld.wait("v", VT, r5)
        r6 = bld.op("v", "tensor_scalar_min", BND[0:2, 10:11], BND[0:2, 10:11],
                    0.0, inc=VT)
        sem, val = ps_free[0]
        bld.wait("t", sem, val)
        bld.wait("t", VT, r6)
        tv = bld.op("t", "transpose", ps[0:1, 0:2], BND[0:2, 10:11],
                    IDF[0:2, 0:2], inc=PT)
        bld.wait("s", PT, tv)
        me = bld.op("s", "copy", MBS[0:1, 0:2], ps[0:1, 0:2], inc=ST)
        for hh in range(2):
            sem, val = ps_free[0]
            bld.wait("t", sem, val)
            bld.wait("t", ST, me)
            bv = bld.mm(ps[0:128, 0:1], ONER_F[0:1, 0:128],
                        MBS[0:1, hh:hh + 1], start=True, stop=True, inc=PT)
            bld.wait("s", PT, bv)
            mbe = bld.op("s", "copy", MBB[:, hh:hh + 1], ps[0:128, 0:1], inc=ST)
            ps_free[0] = (ST, mbe)

    def stage_load(dram_ap, cols, cast_dst, pre_gwait=None):
        s = stg_next[0] % 2
        stg_next[0] += 1
        bld.wait("y", GT, stg_free[s])
        lv = bld.op("y", "dma_start", inc=LD, incby=16,
                    out=STG[:, s * 1024: s * 1024 + cols], in_=dram_ap)
        if pre_gwait is not None:
            pws = pre_gwait if isinstance(pre_gwait, list) else [pre_gwait]
            for sem, val in pws:
                bld.wait("g", sem, val)
        bld.wait("g", LD, lv)
        gv = bld.op("g", "tensor_copy", cast_dst,
                    STG[:, s * 1024: s * 1024 + cols], inc=GT)
        stg_free[s] = gv
        return gv

    def tap_dump(name, src_ap):
        if name in tap_out:
            for sem in (PT, VT, ST, GT):
                bld.wait("s", sem, bld.cnt[sem])
            bld.op("s", "dma_start", inc=LO, incby=16,
                   out=tap_out[name][:, :], in_=src_ap)

    actx = ExitStack()
    MSTR = bld.sb("MSTR", [32, 16 * 128], BF16, stack=actx)
    VSEL = bld.sb("VSEL", [128, NTT * 32], BF16, stack=actx)
    QRG = bld.sb("QRG", [128, T], BF16, stack=actx)
    KRG = bld.sb("KRG", [128, T], BF16, stack=actx)
    VRG = bld.sb("VRG", [128, NTT * 130], BF16, stack=actx)
    VSB = bld.sb("VSB", [32, 130], BF16, stack=actx)
    WQ = bld.sb("WQ", [128, NC * 128], BF16, stack=actx)
    WK = bld.sb("WK", [128, NC * 128], BF16, stack=actx)
    WV = bld.sb("WV", [128, NC * 128], BF16, stack=actx)
    PA = [bld.ps("PA0", [128, 512], F32, stack=actx),
          bld.ps("PA1", [128, 512], F32, stack=actx)]
    PB = [bld.ps("PB0", [128, 512], F32, stack=actx),
          bld.ps("PB1", [128, 512], F32, stack=actx),
          bld.ps("PB2x", [128, 512], F32, stack=actx)]
    PC = [bld.ps("PC0", [128, 512], F32, stack=actx),
          bld.ps("PC1", [128, 512], F32, stack=actx)]
    PM = bld.ps("PM", [128, 512], F32, stack=actx)

    for dst, src in [
        (MBAND2, dr["mask_band"][:, :]), (MBI, dr["mask_inner"][:, :]), (MSTR, dr["mask_str"][:, :]),
        (IDB, dr["iden_bf"][:, :]),
        (IDF, dr["iden_f"][:, :]), (ONER_BF, dr["ones_row_bf"][:, :]),
        (ONEC_BF, dr["ones_col_bf"][:, :]), (ONER_F, dr["ones_row_f"][:, :]),
        (ONEC_F, dr["ones_col_f"][:, :]), (G1, dr["g1"][:, :]), (G2, dr["g2"][:, :]),
        (SEL2, dr["sel2"][:, :]),
        (VSEL, dr["vsel"][:, :]), (RBIAS, dr["rbias"][:, :]),
    ]:
        bld.op("y", "dma_start", out=dst[:, :], in_=src, inc=LD, incby=16)
    bld.op("y", "dma_start", out=RWT[:, :].rearrange("p (k f) -> p k f", k=NC),
           in_=dr["rwT"][:, :].rearrange("(k p) f -> p k f", p=128), inc=LD, incby=16)
    bld.op("g", "memset", STA[:, :], 0.0)
    bld.op("g", "memset", STB[:, :], 0.0)
    zmem = bld.op("g", "memset", STC[:, :], 0.0, inc=GT)

    xld = bld.op("y", "dma_start", inc=XLD, incby=16,
                 out=RES[:, :].rearrange("p (k f) -> p k f", k=NC),
                 in_=dr["x"][:, :].rearrange("(k p) f -> p k f", p=128))

    pa_free = [[(PT, 0)], [(PT, 0)]]
    pb_free = [[(PT, 0)], [(PT, 0)], [(PT, 0)]]
    pc_free = [[(PT, 0)], [(PT, 0)]]
    pm_free = [(PT, 0)]
    vs_free = [(PT, 0)]

    bld.wait("v", XLD, xld)
    xb_cast = []
    for k in range(NC):
        v = bld.op("v", "tensor_copy", XBBF[:, k * T:(k + 1) * T],
                   RES[:, k * T:(k + 1) * T], inc=VT)
        xb_cast.append(v)

    wbuf_ready = {}
    wbuf_freed = {"q": [0], "k": [0], "v": [0]}
    ring_free = {"q": [(PT, 0)], "k": [(PT, 0)], "v": [(PT, 0)]}
    ring_ready = {}

    def load_wchunk(mat, dram, c):
        dst = {"q": WQ, "k": WK, "v": WV}[mat]
        gv = stage_load(
            dram[:, c * 128:(c + 1) * 128].rearrange("(k p) f -> p k f", p=128),
            1024,
            dst[:, :].rearrange("p (k f) -> p k f", k=NC),
            pre_gwait=(PT, wbuf_freed[mat][0]))
        wbuf_ready[(mat, c)] = gv

    def qkv_chunk(c):
        for mat, wbuf, ring in (("q", WQ, QRG), ("k", WK, KRG)):
            bld.wait("t", GT, wbuf_ready[(mat, c)])
            sem, val = ring_free[mat][0]
            bld.wait("t", sem, val)
            ev = None
            for tg in range(NTG):
                pi = tg % 2
                sem, val = pa_free[pi][0]
                bld.wait("t", sem, val)
                for k in range(NC):
                    bld.wait("t", VT, xb_cast[k])
                    pv = bld.mm(PA[pi][:, :],
                                wbuf[:, k * 128:(k + 1) * 128],
                                XBBF[:, k * T + tg * 512: k * T + tg * 512 + 512],
                                start=(k == 0), stop=(k == NC - 1),
                                inc=PT if k == NC - 1 else None)
                bld.wait("s", PT, pv)
                ev = bld.op("s", "copy", ring[:, tg * 512: tg * 512 + 512],
                            PA[pi][:, :], inc=ST)
                pa_free[pi][0] = (ST, ev)
            ring_ready[mat] = ev
        bld.wait("t", GT, wbuf_ready[("v", c)])
        sem, val = ring_free["v"][0]
        bld.wait("g", sem, val)
        mz = bld.op("g", "memset",
                    VRG[0:128, :].rearrange("p (t c) -> p t c", c=130)[:, :, 64:130:65],
                    1.0, inc=GT)
        ev = None
        for tb in range(NTT):
            pi = tb % 2
            sem, val = pa_free[pi][0]
            bld.wait("t", sem, val)
            for k in range(NC):
                pv = bld.mm(PA[pi][:, 0:128],
                            XBBF[:, k * T + tb * 128: k * T + (tb + 1) * 128],
                            WV[:, k * 128:(k + 1) * 128],
                            start=(k == 0), stop=(k == NC - 1),
                            inc=PT if k == NC - 1 else None)
            bld.wait("s", PT, pv)
            bld.wait("s", GT, mz)
            base = tb * 130
            bld.op("s", "copy", VRG[:, base: base + 64], PA[pi][:, 0:64])
            ev = bld.op("s", "copy", VRG[:, base + 65: base + 129], PA[pi][:, 64:128],
                        inc=ST)
            pa_free[pi][0] = (ST, ev)
        ring_ready["v"] = ev
        bld.wait("t", ST, ev)
        sem, val = vs_free[0]
        bld.wait("t", sem, val)
        sem, val = pm_free[0]
        bld.wait("t", sem, val)
        for tile in range(NTT):
            pv = bld.mm(PM[0:32, 0:130],
                        VSEL[:, tile * 32:(tile + 1) * 32],
                        VRG[:, tile * 130:(tile + 1) * 130],
                        start=(tile == 0), stop=(tile == NTT - 1),
                        inc=PT if tile == NTT - 1 else None)
        bld.wait("s", PT, pv)
        vs_ev = bld.op("s", "copy", VSB[0:32, 0:130], PM[0:32, 0:130], inc=ST)
        pm_free[0] = (ST, vs_ev)
        return vs_ev

    def attn_outer(c, hh, vs_ev):
        r0 = hh * 64
        bld.wait("t", ST, ring_ready["q"])
        bld.wait("t", ST, ring_ready["k"])
        bld.wait("t", ST, vs_ev)

        def emit_s2(pp):
            pi = pp % 3
            sem, val = pb_free[pi][0]
            bld.wait("t", sem, val)
            first = True
            for j in range(2):
                qb = 2 * pp + j
                qs = (qb * 128, (qb + 1) * 128)
                co = j * 256
                if qb > 0:
                    bld.mm(PB[pi][:, co:co + 128],
                           KRG[r0:r0 + 64, (qb - 1) * 128: qb * 128],
                           QRG[r0:r0 + 64, qs[0]:qs[1]], start=first, stop=False)
                    first = False
                bld.mm(PB[pi][:, co + 128:co + 256],
                       KRG[r0:r0 + 64, qs[0]:qs[1]],
                       QRG[r0:r0 + 64, qs[0]:qs[1]], start=first, stop=False)
                first = False
            return bld.mm(PB[pi][:, 0:512], IDB[:, :], MBAND2[:, :],
                          start=False, stop=True, inc=PT)

        def emit_sstr(b):
            sem, val = pm_free[0]
            bld.wait("t", sem, val)
            bld.mm(PM[0:32, 0:512],
                   KRG[r0:r0 + 64, 0:T:64],
                   QRG[r0:r0 + 64, b * 512:(b + 1) * 512],
                   start=True, stop=False)
            return bld.mm(PM[0:32, 0:512], IDB[0:32, 0:32],
                          MSTR[0:32, b * 512:(b + 1) * 512],
                          start=False, stop=True, inc=PT)

        s2pv = {0: emit_s2(0), 1: emit_s2(1)}
        sstrpv = emit_sstr(0)
        exs = {}
        yv = None
        pend = []

        def flush_norm(pq):
            nonlocal yv
            while pq:
                db, dai, dpts2, drv, devv = pq.pop(0)
                bld.wait("t", ST, drv)
                bld.wait("t", VT, devv)
                bv = bld.op("t", "matmul", PC[dai][0:64, 0:512],
                            ONER_BF[32:33, 0:64],
                            PTSB[32:33, dpts2 * 512:(dpts2 + 1) * 512],
                            start=True, stop=True, inc=PT)
                warset("rrow", PT, bv, dpts2)
                bld.wait("v", PT, bv)
                yv = bld.op("v", "tensor_mul",
                            AOBF[r0:r0 + 64,
                                 c * T + db * 512: c * T + (db + 1) * 512],
                            YS[0:64, db * 512:(db + 1) * 512],
                            PC[dai][0:64, 0:512], inc=VT)
                pc_free[dai][0] = (VT, yv)

        for b in range(NTG):
            for j2 in range(2):
                pp = 2 * b + j2
                if pp + 2 < 2 * NTG:
                    s2pv[pp + 2] = emit_s2(pp + 2)
                pi = pp % 3
                pts = pp % 3
                warwait("s", "ptr", pts)
                bld.wait("s", PT, s2pv[pp])
                ex = bld.op("s", "activation", PTRB[:, pts * 512:(pts + 1) * 512],
                            PB[pi][:, 0:512], AF.Exp, bias=MBB[:, hh:hh + 1],
                            scale=0.125, inc=ST)
                pb_free[pi][0] = (ST, ex)
                exs[pp] = ex
            if pend:
                flush_norm(pend)
            pts2 = b % 2
            warwait("s", "ptr2", pts2)
            bld.wait("s", PT, sstrpv)
            exstr = bld.op("s", "activation", PTSB[0:32, pts2 * 512:(pts2 + 1) * 512],
                           PM[0:32, 0:512], AF.Exp, bias=MBB[0:32, hh:hh + 1],
                           scale=0.125, inc=ST)
            pm_free[0] = (ST, exstr)
            ai = b % 2
            sem, val = pc_free[ai][0]
            bld.wait("t", sem, val)
            first = True
            for j in range(4):
                qb = 4 * b + j
                pp = qb // 2
                pts = pp % 3
                co = (qb % 2) * 256
                if j % 2 == 0:
                    bld.wait("t", ST, exs[pp])
                if qb > 0:
                    bld.mm(PC[ai][0:65, j * 128:(j + 1) * 128],
                           VRG[:, (qb - 1) * 130 + hh * 65:
                               (qb - 1) * 130 + hh * 65 + 65],
                           PTRB[:, pts * 512 + co: pts * 512 + co + 128],
                           start=first, stop=False)
                    first = False
                lastmm = bld.mm(PC[ai][0:65, j * 128:(j + 1) * 128],
                                VRG[:, qb * 130 + hh * 65: qb * 130 + hh * 65 + 65],
                                PTRB[:, pts * 512 + co + 128: pts * 512 + co + 256],
                                start=first, stop=False,
                                inc=PT if j % 2 == 1 else None)
                first = False
                if j % 2 == 1:
                    warset("ptr", PT, lastmm, pts)
            bld.wait("t", ST, exstr)
            av = bld.mm(PC[ai][0:65, 0:512], VSB[0:32, hh * 65: hh * 65 + 65],
                        PTSB[0:32, pts2 * 512:(pts2 + 1) * 512],
                        start=False, stop=True, inc=PT)
            warset("ptr2", PT, av, pts2)
            if b + 1 < NTG:
                sstrpv = emit_sstr(b + 1)
            bld.wait("v", PT, av)
            evv = bld.op("v", "tensor_copy", YS[:, b * 512:(b + 1) * 512],
                         PC[ai][0:65, 0:512], inc=VT)
            pc_free[ai][0] = (VT, evv)
            bld.wait("s", VT, evv)
            l1 = bld.op("s", "activation", YS[64:65, b * 512:(b + 1) * 512],
                        YS[64:65, b * 512:(b + 1) * 512], AF.Ln, inc=ST)
            bld.wait("s", ST, l1)
            warwait("s", "rrow", pts2)
            rv = bld.op("s", "activation", PTSB[32:33, pts2 * 512:(pts2 + 1) * 512],
                        YS[64:65, b * 512:(b + 1) * 512], AF.Exp,
                        scale=-1.0, inc=ST)
            pend.append((b, ai, pts2, rv, evv))
            if b == NTG - 1:
                flush_norm(pend)
        return yv

    a_last = None
    xb_dead_pt = [0]
    load_wchunk("q", dr["wqT"], 0)
    load_wchunk("k", dr["wkT"], 0)
    load_wchunk("v", dr["wvT"], 0)
    for c in range(NC):
        vs_ev = qkv_chunk(c)
        wbuf_freed["q"][0] = bld.cnt[PT]
        wbuf_freed["k"][0] = bld.cnt[PT]
        wbuf_freed["v"][0] = bld.cnt[PT]
        if c + 1 < NC:
            load_wchunk("q", dr["wqT"], c + 1)
            load_wchunk("k", dr["wkT"], c + 1)
            load_wchunk("v", dr["wvT"], c + 1)
        if c == NC - 1:
            xb_dead_pt[0] = bld.cnt[PT]
        bld.wait("s", ST, ring_ready["q"])
        bld.wait("s", ST, ring_ready["k"])
        emit_bound(QRG, KRG, PM, pm_free)
        for hh in range(2):
            a_last = attn_outer(c, hh, vs_ev)
        ring_free["q"][0] = (PT, bld.cnt[PT])
        ring_free["k"][0] = (PT, bld.cnt[PT])
        ring_free["v"][0] = (PT, bld.cnt[PT])
        vs_free[0] = (PT, bld.cnt[PT])
    tap_dump("ao", AOBF[:, :])
    tap_dump("yraw", YS[0:64, :]); tap_dump("sums", YS[64:65, :])
    tap_dump("sumsb", YS[64:65, :]); tap_dump("mbb", MBB[:, :])

    wo_ready = []
    for k in range(NC):
        gv = stage_load(dr["woT"][k * 128:(k + 1) * 128, :], 1024,
                        XBBF[:, 8192 + k * 1024: 8192 + (k + 1) * 1024],
                        pre_gwait=(PT, xb_dead_pt[0]))
        wo_ready.append(gv)

    def proj_to_res(wbase, src_bf, ps_pair, ps_free, wready):
        for oc in range(NC):
            for tg in range(NTG):
                pi = (oc * NTG + tg) % 2
                sem, val = ps_free[pi][0]
                bld.wait("t", sem, val)
                for k in range(NC):
                    bld.wait("t", GT, wready[k])
                    pv = bld.mm(ps_pair[pi][:, :],
                                XBBF[:, wbase + k * 1024 + oc * 128:
                                     wbase + k * 1024 + (oc + 1) * 128],
                                src_bf[:, k * T + tg * 512: k * T + tg * 512 + 512],
                                start=(k == 0), stop=(k == NC - 1),
                                inc=PT if k == NC - 1 else None)
                bld.wait("v", PT, pv)
                rv = bld.op("v", "tensor_add",
                            RES[:, oc * T + tg * 512: oc * T + tg * 512 + 512],
                            RES[:, oc * T + tg * 512: oc * T + tg * 512 + 512],
                            ps_pair[pi][:, :], inc=VT)
                ps_free[pi][0] = (VT, rv)

    bld.wait("t", VT, a_last)
    proj_to_res(8192, AOBF, PA, pa_free, wo_ready)

    def feature_ln(with_mean, gamma, out_bf, inplace, f32_sink=False,
                   ps_pair=None, ps_free=None, router_ctx=None, h_fused=None):
        bld.wait("s", VT, bld.cnt[VT])
        bld.wait("t", VT, bld.cnt[VT])
        for tg in range(NTG):
            for pi in range(2):
                sem, val = ps_free[pi][0]
                bld.wait("t", sem, val)
            pv_sum = None
            pv_sq = None
            for k in range(NC):
                if with_mean:
                    warwait("v", "lnsb")
                    cv = bld.op("v", "tensor_copy", LNSB[:, :],
                                RES[:, k * T + tg * 512: k * T + tg * 512 + 512], inc=VT)
                    bld.wait("t", VT, cv)
                    pv_sum = bld.mm(ps_pair[0][0:1, 0:512], ONEC_BF[:, :], LNSB[:, :],
                                    start=(k == 0), stop=(k == NC - 1), inc=PT)
                    warset("lnsb", PT, pv_sum)
                warwait("s", "lnsq")
                sv = bld.op("s", "activation", LNSQ[:, :],
                            RES[:, k * T + tg * 512: k * T + tg * 512 + 512],
                            AF.Square, inc=ST)
                bld.wait("t", ST, sv)
                pv_sq = bld.mm(ps_pair[1][0:1, 0:512], ONEC_BF[:, :], LNSQ[:, :],
                               start=(k == 0), stop=(k == NC - 1), inc=PT)
                warset("lnsq", PT, pv_sq)
            row = tg * 32
            if with_mean:
                bld.wait("s", PT, pv_sum)
                e1 = bld.op("s", "copy", STA[row:row + 1, :],
                            ps_pair[0][0:1, 0:512], inc=ST)
                ps_free[0][0] = (ST, e1)
            bld.wait("s", PT, pv_sq)
            e2 = bld.op("s", "copy", STB[row:row + 1, :],
                        ps_pair[1][0:1, 0:512], inc=ST)
            ps_free[1][0] = (ST, e2)
        bld.wait("v", ST, bld.cnt[ST])
        bld.wait("v", GT, zmem)
        if with_mean:
            v1 = bld.op("v", "tensor_scalar_mul", STA[:, :], STA[:, :],
                        1.0 / D, inc=VT)
            bld.wait("v", VT, v1)
            v2 = bld.op("v", "tensor_mul", STC[:, :], STA[:, :], STA[:, :], inc=VT)
            bld.wait("v", VT, v2)
            v3 = bld.op("v", "scalar_tensor_tensor", STB[:, :], STB[:, :],
                        1.0 / D, STC[:, :], op0=ALU.mult, op1=ALU.subtract, inc=VT)
        else:
            v3 = bld.op("v", "tensor_scalar_mul", STB[:, :], STB[:, :],
                        1.0 / D, inc=VT)
        bld.wait("v", VT, v3)
        v4 = bld.op("v", "tensor_scalar_add", STB[:, :], STB[:, :], EPS, inc=VT)
        bld.wait("s", VT, v4)
        s1 = bld.op("s", "activation", STB[:, :], STB[:, :], AF.Ln, inc=ST)
        bld.wait("s", ST, s1)
        s2 = bld.op("s", "activation", STB[:, :], STB[:, :], AF.Exp,
                    scale=-0.5, inc=ST)
        v5 = None
        if with_mean:
            bld.wait("v", ST, s2)
            v5 = bld.op("v", "tensor_mul", STC[:, :], STA[:, :], STB[:, :], inc=VT)
        r1s = None
        if h_fused is not None:
            bld.wait("v", VT, v5)
            v6 = bld.op("v", "tensor_mul", STA[:, :], STB[:, :], STB[:, :], inc=VT)
            bld.wait("v", VT, v6)
            v7 = bld.op("v", "tensor_scalar", STA[:, :], STA[:, :],
                        -EPS, 1.0 + EPS, op0=ALU.mult, op1=ALU.add, inc=VT)
            bld.wait("s", VT, v7)
            s3 = bld.op("s", "activation", STA[:, :], STA[:, :], AF.Ln, inc=ST)
            bld.wait("s", ST, s3)
            r1s = bld.op("s", "activation", STA[:, :], STA[:, :], AF.Exp,
                         scale=-0.5, inc=ST)
        bc_war = getattr(feature_ln, "_bc_war", None)
        if bc_war is None:
            bc_war = [(PT, 0)]
            feature_ln._bc_war = bc_war
        for tg in range(NTG):
            row = tg * 32
            sem, val = ps_free[0][0]
            bld.wait("t", sem, val)
            sem, val = bc_war[0]
            bld.wait("s", sem, val)
            bld.wait("s", ST, s2)
            if with_mean:
                bld.wait("s", VT, v5)
            cb1 = bld.op("s", "copy", BCR[:, :], STB[row:row + 1, :], inc=ST)
            bld.wait("t", ST, cb1)
            rbv = bld.mm(ps_pair[0][:, :], ONER_F[0:1, 0:128],
                         BCR[:, :], start=True, stop=True, inc=PT)
            b2 = None
            if with_mean:
                sem, val = ps_free[1][0]
                bld.wait("t", sem, val)
                bld.wait("s", PT, rbv)
                cb2 = bld.op("s", "copy", BCR[:, :], STC[row:row + 1, :], inc=ST)
                bld.wait("t", ST, cb2)
                b2 = bld.mm(ps_pair[1][:, :], ONER_F[0:1, 0:128],
                            BCR[:, :], start=True, stop=True, inc=PT)
            bc_war[0] = (PT, bld.cnt[PT])
            bld.wait("v", PT, bld.cnt[PT])
            last_v = None
            for k in range(NC):
                xa = RES[:, k * T + tg * 512: k * T + tg * 512 + 512]
                if inplace:
                    bld.op("v", "tensor_mul", xa, xa, ps_pair[0][:, :])
                    last_v = bld.op("v", "tensor_sub", xa, xa, ps_pair[1][:, :], inc=VT)
                    continue
                if out_bf is not None:
                    dstt, base = out_bf
                    dst = dstt[:, base + k * T + tg * 512: base + k * T + tg * 512 + 512]
                    if with_mean:
                        sgs = 0
                        warwait("v", "sgr", sgs)
                        sf = SGR[:, 0:512]
                        bld.op("v", "tensor_mul", sf, xa, ps_pair[0][:, :])
                        fv0 = bld.op("v", "tensor_sub", sf, sf, ps_pair[1][:, :], inc=VT)
                        bld.wait("v", VT, fv0)
                        if f32_sink:
                            fv = bld.op("v", "tensor_scalar_mul", sf, sf,
                                        gamma[:, k:k + 1], inc=VT)
                            bld.wait("v", VT, fv)
                            last_v = bld.op("v", "tensor_copy", dst, sf, inc=VT)
                            bld.wait("t", VT, fv)
                            rpv = bld.mm(router_ctx["psum"][tg][0:7, 0:512],
                                         RWT[:, k * 7:(k + 1) * 7], sf,
                                         start=(k == 0), stop=(k == NC - 1), inc=PT)
                            router_ctx["mark"][tg] = rpv
                            warset("sgr", PT, rpv, sgs)
                        else:
                            if gamma is not None:
                                last_v = bld.op("v", "tensor_scalar_mul", dst, sf,
                                                gamma[:, k:k + 1], inc=VT)
                            else:
                                last_v = bld.op("v", "tensor_copy", dst, sf, inc=VT)
                            warset("sgr", VT, last_v, sgs)
                    elif gamma is not None:
                        last_v = bld.op("v", "scalar_tensor_tensor", dst, xa,
                                        gamma[:, k:k + 1], ps_pair[0][:, :],
                                        op0=ALU.mult, op1=ALU.mult, inc=VT)
                    else:
                        last_v = bld.op("v", "tensor_mul", dst, xa,
                                        ps_pair[0][:, :], inc=VT)
            if h_fused is not None:
                dstt, base, gam = h_fused
                sem, val = bc_war[0]
                bld.wait("s", sem, val)
                bld.wait("s", ST, r1s)
                cb3 = bld.op("s", "copy", BCR[:, :], STA[row:row + 1, :], inc=ST)
                bld.wait("t", ST, cb3)
                bld.wait("t", VT, last_v)
                b3 = bld.mm(ps_pair[0][:, :], ONER_F[0:1, 0:128], BCR[:, :],
                            start=True, stop=True, inc=PT)
                bc_war[0] = (PT, bld.cnt[PT])
                bld.wait("v", PT, b3)
                for k in range(NC):
                    xa = RES[:, k * T + tg * 512: k * T + tg * 512 + 512]
                    dst = dstt[:, base + k * T + tg * 512:
                               base + k * T + tg * 512 + 512]
                    last_v = bld.op("v", "scalar_tensor_tensor", dst, xa,
                                    gam[:, k:k + 1], ps_pair[0][:, :],
                                    op0=ALU.mult, op1=ALU.mult, inc=VT)
            ps_free[0][0] = (VT, last_v)
            if b2 is not None:
                ps_free[1][0] = (VT, last_v)
        return bld.cnt[VT]

    h_done = feature_ln(True, None, None, True, ps_pair=PB, ps_free=pb_free,
                        h_fused=(XBBF, 0, G1))
    tap_dump("x1", RES[:, :])
    tap_dump("h", XBBF[:, :])

    if upto == "A":
        actx.close(); atc.close(); aoctx.close(); cglobal.close(); abctx.close()
        _finish(bld)
        return nc, sorted(tap_out.keys())

    actx.close()
    a_end_pt = bld.cnt[PT]
    a_end_vt = bld.cnt[VT]
    bctx = ExitStack()
    KVL = bld.sb("KVL", [128, 2 * T], BF16, stack=bctx)
    QL = bld.sb("QL", [128, 2 * T], BF16, stack=bctx)
    QR2 = bld.sb("QR2", [128, T], BF16, stack=bctx)
    KR2 = bld.sb("KR2", [128, T], BF16, stack=bctx)
    VR2 = bld.sb("VR2", [128, NTT * 130], BF16, stack=bctx)
    WKU = bld.sb("WKU", [128, 256], BF16, stack=bctx)
    WQU = bld.sb("WQU", [128, 256], BF16, stack=bctx)
    WVU = bld.sb("WVU", [128, 256], BF16, stack=bctx)
    WRQ = bld.sb("WRQ", [128, 128], BF16, stack=bctx)
    WRK = bld.sb("WRK", [128, 512], BF16, stack=bctx)
    CS = bld.sb("CS", [128, 512], F32, stack=bctx)
    RSC = SGR
    RTB = bld.sb("RTB", [128, 2 * 256], BF16, stack=bctx)
    PB2 = [bld.ps("PB2%d" % i, [128, 512], F32, stack=bctx) for i in range(5)]
    PCAV = [bld.ps("PCAVa", [128, 512], F32, stack=bctx),
            bld.ps("PCAVb", [128, 512], F32, stack=bctx)]
    PRBF = bld.ps("PRBF", [128, 1024], BF16, stack=bctx)

    pb2_free = [[(PT, 0)] for _ in range(5)]
    pcav_free = [[(PT, 0)], [(PT, 0)]]
    prbf_free = [(PT, 0)]

    WKVD = QR2
    WQD = KR2
    kvd_ready = None
    a_guard = [(PT, a_end_pt), (VT, a_end_vt)]
    for kk in range(2):
        stage_load(dr["kvdT"][kk * 512:(kk + 1) * 512, :]
                   .rearrange("(k p) f -> p k f", p=128), 1024,
                   WKVD[:, kk * 1024:(kk + 1) * 1024]
                   .rearrange("p (k f) -> p k f", k=4), pre_gwait=a_guard)
        kvd_ready = stage_load(dr["qdT"][kk * 512:(kk + 1) * 512, :]
                               .rearrange("(k p) f -> p k f", p=128), 1024,
                               WQD[:, kk * 1024:(kk + 1) * 1024]
                               .rearrange("p (k f) -> p k f", k=4), pre_gwait=a_guard)

    bld.wait("t", VT, h_done)
    bld.wait("t", GT, kvd_ready)
    lat_done = None
    for (buf, wbuf) in ((KVL, WKVD), (QL, WQD)):
        for lc in range(2):
            for tg in range(NTG):
                pi = tg % 2
                sem, val = pb2_free[pi][0]
                bld.wait("t", sem, val)
                for k in range(NC):
                    pv = bld.mm(PB2[pi][:, :],
                                wbuf[:, k * 256 + lc * 128: k * 256 + (lc + 1) * 128],
                                XBBF[:, k * T + tg * 512: k * T + tg * 512 + 512],
                                start=(k == 0), stop=(k == NC - 1),
                                inc=PT if k == NC - 1 else None)
                bld.wait("s", PT, pv)
                lat_done = bld.op("s", "copy",
                                  buf[:, lc * T + tg * 512: lc * T + tg * 512 + 512],
                                  PB2[pi][:, :], inc=ST)
                pb2_free[pi][0] = (ST, lat_done)
    tap_dump("lat", KVL[:, :])

    ring2_free = {"q": [(PT, bld.cnt[PT])], "k": [(PT, bld.cnt[PT])],
                  "v": [(PT, a_end_pt)]}
    war["cs"][0][0] = (VT, a_end_vt)
    war["rtb"][0][0] = (PT, a_end_pt)
    war["rtb"][1][0] = (PT, a_end_pt)
    up_freed = [0, 0]

    lu_cast_done = [0]

    def load_up(c):
        pg = [(PT, max(up_freed[0], a_end_pt)), (VT, a_end_vt)]
        srcs = [
            (dr["kuT"][:, c * 128:(c + 1) * 128]
             .rearrange("(k p) f -> p k f", p=128), 0, 256,
             WKU[:, :].rearrange("p (k f) -> p k f", k=2)),
            (dr["quT"][:, c * 128:(c + 1) * 128]
             .rearrange("(k p) f -> p k f", p=128), 256, 256,
             WQU[:, :].rearrange("p (k f) -> p k f", k=2)),
            (dr["vuT"][:, c * 128:(c + 1) * 128]
             .rearrange("(k p) f -> p k f", p=128), 512, 256,
             WVU[:, :].rearrange("p (k f) -> p k f", k=2)),
            (dr["rqT"][:, c * 64:(c + 1) * 64]
             .rearrange("(k p) f -> p k f", p=128), 768, 128,
             WRQ[:, :].rearrange("p (k f) -> p k f", k=2)),
            (dr["rkT"][:, c * 64:(c + 1) * 64]
             .rearrange("(k p) f -> p k f", p=128), 896, 512,
             WRK[:, :].rearrange("p (k f) -> p k f", k=NC)),
        ]
        bld.wait("y", GT, max(lu_cast_done[0], stg_free[0], stg_free[1]))
        for src_ap, off, cols, _dst in srcs:
            bld.op("y", "dma_start", inc=LU, incby=16,
                   out=STG[:, off:off + cols], in_=src_ap)
        full = bld.cnt[LU]
        out = []
        for _src, off, cols, dst in srcs:
            for sem, val in pg:
                bld.wait("g", sem, val)
            bld.wait("g", LU, full)
            gv = bld.op("g", "tensor_copy", dst, STG[:, off:off + cols], inc=GT)
            out.append(gv)
        lu_cast_done[0] = out[-1]
        return tuple(out)

    def mla_qkv(c, wready):
        slot = c % 2
        for (mat, wbuf, ring, wr) in (("q", WQU, QR2, wready[1]),
                                      ("k", WKU, KR2, wready[0])):
            bld.wait("t", GT, wr)
            sem, val = ring2_free[mat][0]
            bld.wait("t", sem, val)
            bld.wait("t", ST, lat_done)
            src = QL if mat == "q" else KVL
            for tg in range(NTG):
                pi = tg % 2
                sem, val = pb2_free[pi][0]
                bld.wait("t", sem, val)
                for lc in range(2):
                    pv = bld.mm(PB2[pi][:, :],
                                wbuf[:, lc * 128:(lc + 1) * 128],
                                src[:, lc * T + tg * 512: lc * T + tg * 512 + 512],
                                start=(lc == 0), stop=(lc == 1),
                                inc=PT if lc == 1 else None)
                bld.wait("s", PT, pv)
                ev = bld.op("s", "copy", ring[:, tg * 512: tg * 512 + 512],
                            PB2[pi][:, :], inc=ST)
                pb2_free[pi][0] = (ST, ev)
        bld.wait("t", GT, wready[2])
        sem, val = ring2_free["v"][0]
        bld.wait("g", sem, val)
        mz = bld.op("g", "memset",
                    VR2[0:128, :].rearrange("p (t c) -> p t c", c=130)[:, :, 64:130:65],
                    1.0, inc=GT)
        for tb in range(NTT):
            pi = tb % 2
            sem, val = pb2_free[pi][0]
            bld.wait("t", sem, val)
            for lc in range(2):
                pv = bld.mm(PB2[pi][:, 0:128],
                            KVL[:, lc * T + tb * 128: lc * T + (tb + 1) * 128],
                            WVU[:, lc * 128:(lc + 1) * 128],
                            start=(lc == 0), stop=(lc == 1),
                            inc=PT if lc == 1 else None)
            bld.wait("s", PT, pv)
            bld.wait("s", GT, mz)
            base = tb * 130
            bld.op("s", "copy", VR2[:, base: base + 64], PB2[pi][:, 0:64])
            ev = bld.op("s", "copy", VR2[:, base + 65: base + 129],
                        PB2[pi][:, 64:128], inc=ST)
            pb2_free[pi][0] = (ST, ev)
        for (mat, ring, wr) in (("q", QR2, wready[3]), ("k", KR2, wready[4])):
            for g in range(NTT // 4):
                tb0 = g * 4
                warwait("g", "cs", 0)
                bld.op("g", "dma_start", inc=CSLD, incby=16,
                       out=CS[:, 0:256].rearrange("p (j f) -> p j f", j=4),
                       in_=dr["cosE"][tb0 * 128:(tb0 + 4) * 128, :]
                       .rearrange("(j p) f -> p j f", p=128))
                sv_ = bld.op("g", "dma_start", inc=CSLD, incby=16,
                             out=CS[:, 256:512].rearrange("p (j f) -> p j f", j=4),
                             in_=dr["sinE"][tb0 * 128:(tb0 + 4) * 128, :]
                             .rearrange("(j p) f -> p j f", p=128))
                pi = g % 2
                sem, val = pb2_free[pi][0]
                bld.wait("t", sem, val)
                bld.wait("t", GT, wr)
                for j in range(4):
                    tb = tb0 + j
                    if mat == "q":
                        for lc in range(2):
                            pv = bld.mm(PB2[pi][:, j * 64:(j + 1) * 64],
                                        QL[:, lc * T + tb * 128: lc * T + (tb + 1) * 128],
                                        WRQ[:, lc * 64:(lc + 1) * 64],
                                        start=(lc == 0), stop=(lc == 1),
                                        inc=PT if (lc == 1 and j == 3) else None)
                    else:
                        for k in range(NC):
                            pv = bld.mm(PB2[pi][:, j * 64:(j + 1) * 64],
                                        XBBF[:, k * T + tb * 128: k * T + (tb + 1) * 128],
                                        WRK[:, k * 64:(k + 1) * 64],
                                        start=(k == 0), stop=(k == NC - 1),
                                        inc=PT if (k == NC - 1 and j == 3) else None)
                bld.wait("v", PT, pv)
                bld.wait("v", CSLD, sv_)
                bld.op("v", "tensor_scalar_mul",
                       RSC[0:128, 0:256].rearrange("p (h d) -> p h d", d=32)[:, :, 0:16],
                       PB2[pi][0:128, 0:256].rearrange("p (h d) -> p h d", d=32)[:, :, 16:32],
                       -1.0)
                x2v = bld.op("v", "tensor_scalar_mul",
                             RSC[0:128, 0:256].rearrange("p (h d) -> p h d", d=32)[:, :, 16:32],
                             PB2[pi][0:128, 0:256].rearrange("p (h d) -> p h d", d=32)[:, :, 0:16],
                             1.0, inc=VT)
                bld.op("v", "tensor_mul", RSC[:, 256:512], PB2[pi][:, 0:256],
                       CS[:, 0:256])
                bld.wait("v", VT, x2v)
                x3v = bld.op("v", "tensor_mul", RSC[:, 0:256], RSC[:, 0:256],
                             CS[:, 256:512], inc=VT)
                warset("cs", VT, x3v, 0)
                rs = g % 2
                warwait("v", "rtb2", rs)
                bld.wait("v", VT, x3v)
                x4v = bld.op("v", "tensor_add", RTB[:, rs * 256:rs * 256 + 256],
                             RSC[:, 256:512], RSC[:, 0:256], inc=VT)
                pb2_free[pi][0] = (VT, x4v)
                for jp in range(2):
                    tbA = tb0 + 2 * jp
                    sem, val = prbf_free[0]
                    bld.wait("t", sem, val)
                    bld.wait("t", VT, x4v)
                    tv = bld.op("t", "transpose", PRBF[0:128, 0:128],
                                RTB[:, rs * 256 + jp * 128: rs * 256 + (jp + 1) * 128],
                                IDB[:, :], inc=PT)
                    if jp == 1:
                        warset("rtb2", PT, tv, rs)
                    bld.wait("s", PT, tv)
                    bld.op("s", "copy", ring[0:32, tbA * 128:(tbA + 1) * 128],
                           PRBF[0:32, 0:128])
                    bld.op("s", "copy", ring[64:96, tbA * 128:(tbA + 1) * 128],
                           PRBF[32:64, 0:128])
                    bld.op("s", "copy", ring[0:32, (tbA + 1) * 128:(tbA + 2) * 128],
                           PRBF[64:96, 0:128])
                    e2 = bld.op("s", "copy",
                                ring[64:96, (tbA + 1) * 128:(tbA + 2) * 128],
                                PRBF[96:128, 0:128], inc=ST)
                    prbf_free[0] = (ST, e2)
        if c == 3 and "rtb3" in tap_out:
            bld.wait("s", VT, bld.cnt[VT])
            bld.op("s", "dma_start", out=tap_out["rtb3"][:, :], in_=RTB[:, :],
                   inc=LO, incby=16)
            bld.op("s", "dma_start", out=tap_out["wrk3"][:, :], in_=WRK[:, :],
                   inc=LO, incby=16)
        emit_bound(QR2, KR2, PB2[0], pb2_free[0])
        return bld.cnt[ST]

    def attn_inner_dual(c, ck):
        bld.wait("t", ST, ck)
        yv = None
        seq = []
        for qg in range(NTG):
            for kt in range(4 * qg + 4):
                for hh in (0, 1):
                    seq.append((qg, kt, hh))
        L = len(seq)

        def emit_s2(i):
            qg, kt, hh = seq[i]
            pi = i % 5
            sem, val = pb2_free[pi][0]
            bld.wait("t", sem, val)
            r0 = hh * 64
            r = kt - 4 * qg
            if r < 0:
                return bld.mm(PB2[pi][:, :],
                              KR2[r0:r0 + 64, kt * 128:(kt + 1) * 128],
                              QR2[r0:r0 + 64, qg * 512:(qg + 1) * 512],
                              start=True, stop=True, inc=PT)
            bld.mm(PB2[pi][:, :],
                   KR2[r0:r0 + 64, kt * 128:(kt + 1) * 128],
                   QR2[r0:r0 + 64, qg * 512:(qg + 1) * 512],
                   start=True, stop=False)
            return bld.mm(PB2[pi][:, :], IDB[:, :],
                          MBI[:, 384 - r * 128: 896 - r * 128],
                          start=False, stop=True, inc=PT)

        DEPTH = 4
        pend = []
        s2pv = {}
        for i in range(min(DEPTH, L)):
            s2pv[i] = emit_s2(i)
        for i, (qg, kt, hh) in enumerate(seq):
            pi = i % 5
            if i + DEPTH < L:
                s2pv[i + DEPTH] = emit_s2(i + DEPTH)
            nkt = 4 * qg + 4
            pts = i % 3
            warwait("s", "ptr", pts)
            bld.wait("s", PT, s2pv[i])
            ex = bld.op("s", "activation", PTRB[:, pts * 512:(pts + 1) * 512],
                        PB2[pi][:, :], AF.Exp, bias=MBB[:, hh:hh + 1],
                        scale=0.125, inc=ST)
            pb2_free[pi][0] = (ST, ex)
            if kt == 0:
                sem, val = pcav_free[hh][0]
                bld.wait("t", sem, val)
            bld.wait("t", ST, ex)
            av = bld.mm(PCAV[hh][0:65, 0:512],
                        VR2[:, kt * 130 + hh * 65: kt * 130 + hh * 65 + 65],
                        PTRB[:, pts * 512:(pts + 1) * 512],
                        start=(kt == 0), stop=(kt == nkt - 1), inc=PT)
            warset("ptr", PT, av, pts)
            if kt == nkt - 1:
                r0 = hh * 64
                ysc = ((qg % 2) * 2 + hh) * 512
                bld.wait("v", PT, av)
                evv = bld.op("v", "tensor_copy", YS[:, ysc:ysc + 512],
                             PCAV[hh][0:65, 0:512], inc=VT)
                pcav_free[hh][0] = (VT, evv)
                bld.wait("s", VT, evv)
                l1 = bld.op("s", "activation", YS[64:65, ysc:ysc + 512],
                            YS[64:65, ysc:ysc + 512], AF.Ln, inc=ST)
                bld.wait("s", ST, l1)
                warwait("s", "rrow", hh)
                rv = bld.op("s", "activation",
                            PTSB[32:33, hh * 512:(hh + 1) * 512],
                            YS[64:65, ysc:ysc + 512], AF.Exp,
                            scale=-1.0, inc=ST)
                pi2 = i % 5
                sem, val = pb2_free[pi2][0]
                bld.wait("t", sem, val)
                bld.wait("t", ST, rv)
                bld.wait("t", VT, evv)
                bv = bld.op("t", "matmul", PB2[pi2][0:64, 0:512],
                            ONER_BF[32:33, 0:64],
                            PTSB[32:33, hh * 512:(hh + 1) * 512],
                            start=True, stop=True, inc=PT)
                warset("rrow", PT, bv, hh)
                bld.wait("v", PT, bv)
                yv = bld.op("v", "tensor_mul",
                            AOBF[r0:r0 + 64,
                                 c * T + qg * 512: c * T + (qg + 1) * 512],
                            YS[0:64, ysc:ysc + 512], PB2[pi2][0:64, 0:512],
                            inc=VT)
                pb2_free[pi2][0] = (VT, yv)
        return yv

    b_last = None
    h_dead_pt = [0]
    for c in range(NC):
        wready = load_up(c)
        ck = mla_qkv(c, wready)
        if c == NC - 1:
            h_dead_pt[0] = bld.cnt[PT]
        tap_dump("kc%d" % c, KR2[:, :])
        tap_dump("mb%d" % c, MBB[:, :])
        b_last = attn_inner_dual(c, ck)
        for mat in ("q", "k", "v"):
            ring2_free[mat][0] = (PT, bld.cnt[PT])
        up_freed[0] = bld.cnt[PT]
    tap_dump("yraw2", YS[0:64, :]); tap_dump("sums2", YS[64:65, :])
    tap_dump("sumsb2", YS[64:65, :]); tap_dump("mbb2", MBB[:, :])
    tap_dump("yb", AOBF[:, :])

    stg_free[0] = max(stg_free[0], lu_cast_done[0])
    stg_free[1] = max(stg_free[1], lu_cast_done[0])
    ow_ready = []
    for k in range(NC):
        gv = stage_load(dr["owT2"][k * 128:(k + 1) * 128, :], 1024,
                        XBBF[:, k * 1024:(k + 1) * 1024],
                        pre_gwait=(PT, h_dead_pt[0]))
        ow_ready.append(gv)
    bld.wait("t", VT, b_last)
    proj_to_res(0, AOBF, PB2, pb2_free, ow_ready)
    tap_dump("x2", RES[:, :])

    if upto == "B":
        bctx.close(); atc.close(); aoctx.close(); cglobal.close(); abctx.close()
        _finish(bld)
        return nc, sorted(tap_out.keys())

    bctx.close()
    atc.close()

    cps = ExitStack()
    PRT = [bld.ps("PRT%d" % i, [128, 512], F32, stack=cps) for i in range(NTG)]
    PB3 = [bld.ps("PB3a", [128, 512], F32, stack=cps),
           bld.ps("PB3b", [128, 512], F32, stack=cps)]
    pb3_free = [[(PT, 0)], [(PT, 0)]]

    rctx = {"psum": PRT, "mark": [0] * NTG}
    h2_done = feature_ln(True, G2, (XBBF, 0), False, f32_sink=True,
                         ps_pair=PB3, ps_free=pb3_free, router_ctx=rctx)
    tap_dump("h2", XBBF[:, :])
    lgt_war = [(PT, 0)]
    for tg in range(NTG):
        bld.wait("t", PT, rctx["mark"][tg])
        bv = bld.mm(PRT[tg][0:7, 0:512], RBIAS[:, :], ONER_F[0:1, 0:512],
                    start=False, stop=True, inc=PT)
        sem, val = lgt_war[0]
        bld.wait("s", sem, val)
        bld.wait("s", PT, bv)
        lv = bld.op("s", "copy", LGT[0:7, :], PRT[tg][0:7, 0:512], inc=ST)
        for j in range(4):
            tb = tg * 4 + j
            pi = tb % 2
            sem, val = pb3_free[pi][0]
            bld.wait("t", sem, val)
            bld.wait("t", ST, lv)
            tv = bld.op("t", "transpose", PB3[pi][0:128, 0:7],
                        LGT[0:7, j * 128:(j + 1) * 128], IDF[0:7, 0:7], inc=PT)
            bld.wait("s", PT, tv)
            sg = bld.op("s", "activation", PRBp[:, tb * 8: tb * 8 + 7],
                        PB3[pi][0:128, 0:7], AF.Sigmoid, inc=ST)
            pb3_free[pi][0] = (ST, sg)
            lgt_war[0] = (PT, tv)
    prb_done = bld.cnt[ST]
    bld.wait("v", ST, prb_done)
    for tb in range(NTT):
        P7 = PRBp[:, tb * 8: tb * 8 + 7]
        W7 = WGT[:, tb * 8: tb * 8 + 7]
        v1 = bld.op("v", "reduce_max", TOPS[:, 0:1], P7, axis=AX.X, inc=VT)
        bld.wait("v", VT, v1)
        v2 = bld.op("v", "tensor_single_scalar", W7, P7, TOPS[:, 0:1],
                    op=ALU.is_ge, inc=VT)
        bld.wait("v", VT, v2)
        v3 = bld.op("v", "scalar_tensor_tensor", W7, W7, -1e9, P7,
                    op0=ALU.mult, op1=ALU.add, inc=VT)
        bld.wait("v", VT, v3)
        v4 = bld.op("v", "reduce_max", TOPS[:, 1:2], W7, axis=AX.X, inc=VT)
        bld.wait("v", VT, v4)
        v5 = bld.op("v", "tensor_add", TOPS[:, 2:3], TOPS[:, 0:1], TOPS[:, 1:2], inc=VT)
        bld.wait("v", VT, v5)
        v6 = bld.op("v", "reciprocal", TOPS[:, 3:4], TOPS[:, 2:3], inc=VT)
        bld.wait("v", VT, v6)
        v7 = bld.op("v", "tensor_single_scalar", W7, P7, TOPS[:, 1:2],
                    op=ALU.is_ge, inc=VT)
        bld.wait("v", VT, v7)
        v8 = bld.op("v", "tensor_mul", W7, W7, P7, inc=VT)
        bld.wait("v", VT, v8)
        bld.op("v", "tensor_scalar_mul", W7, W7, TOPS[:, 3:4], inc=VT)
    wgt_done = bld.cnt[VT]
    tap_dump("wgt", WGT[:, :])
    tap_dump("prb", PRBp[:, :])

    def sigma(p):
        tb, k = p // 8, p % 8
        return ((tb % 2) * 8 + k) * 8 + tb // 2

    bld.wait("t", VT, bld.cnt[VT])
    seen = set()
    tp_ping = [0]
    for p0 in range(128):
        if p0 in seen:
            continue
        cyc = [p0]
        seen.add(p0)
        qq_ = sigma(p0)
        while qq_ != p0:
            cyc.append(qq_)
            seen.add(qq_)
            qq_ = sigma(qq_)
        held = None
        for i, p in enumerate(cyc):
            pi = tp_ping[0] % 2
            tp_ping[0] += 1
            sem, val = pb3_free[pi][0]
            bld.wait("t", sem, val)
            tb, k = p // 8, p % 8
            tv = bld.op("t", "transpose", PB3[pi][:, 0:128],
                        RES[:, k * T + tb * 128: k * T + (tb + 1) * 128],
                        IDF[:, :], inc=PT)
            if held is not None:
                hpi, hv = held
                bld.wait("s", PT, tv)
                ev = bld.op("s", "activation",
                            RES[:, k * T + tb * 128: k * T + (tb + 1) * 128],
                            PB3[hpi][:, 0:128], AF.Copy, bias=0.0, scale=2.0, inc=ST)
                pb3_free[hpi][0] = (ST, ev)
            held = (pi, tv)
        tb, k = cyc[0] // 8, cyc[0] % 8
        hpi, hv = held
        bld.wait("s", PT, hv)
        ev = bld.op("s", "activation",
                    RES[:, k * T + tb * 128: k * T + (tb + 1) * 128],
                    PB3[hpi][:, 0:128], AF.Copy, bias=0.0, scale=2.0, inc=ST)
        pb3_free[hpi][0] = (ST, ev)
    x2t_done = bld.cnt[ST]
    X2TOK = RES

    p2ctx = ExitStack()
    OW2B = bld.sb("OW2B", [128, NC * 1024], BF16, stack=p2ctx)
    ow2b_ready = []
    for k in range(NC):
        gv = stage_load(dr["owT2"][k * 128:(k + 1) * 128, :], 1024,
                        OW2B[:, k * 1024:(k + 1) * 1024],
                        pre_gwait=(ST, x2t_done) if k == 0 else None)
        ow2b_ready.append(gv)
    bld.wait("t", ST, x2t_done)
    for tb in range(NTT):
        for no in range(2):
            pi = (tb * 2 + no) % 2
            sem, val = pb3_free[pi][0]
            bld.wait("t", sem, val)
            for k in range(NC):
                bld.wait("t", GT, ow2b_ready[k])
                pv = bld.mm(PB3[pi][:, :],
                            AOBF[:, k * T + tb * 128: k * T + (tb + 1) * 128],
                            OW2B[:, k * 1024 + no * 512: k * 1024 + (no + 1) * 512],
                            start=(k == 0), stop=(k == NC - 1),
                            inc=PT if k == NC - 1 else None)
            bld.wait("v", PT, pv)
            rv = bld.op("v", "tensor_sub",
                        X2TOK[:, tb * D + no * 512: tb * D + (no + 1) * 512],
                        X2TOK[:, tb * D + no * 512: tb * D + (no + 1) * 512],
                        PB3[pi][:, :], inc=VT)
            pb3_free[pi][0] = (VT, rv)
    x2t_done = bld.cnt[VT]
    p2ctx.close()
    cps.close()
    aoctx.close()

    cmain = ExitStack()
    WS = [bld.sb("WS%d" % i, [128, 4 * 1024], BF16, stack=cmain) for i in range(5)]
    ZB = bld.sb("ZB", [128, NC * T], BF16, stack=cmain)
    PG = [bld.ps("PGa", [128, 512], F32, stack=cmain),
          bld.ps("PGb", [128, 512], F32, stack=cmain)]
    PU = [bld.ps("PUa", [128, 512], F32, stack=cmain),
          bld.ps("PUb", [128, 512], F32, stack=cmain)]
    PD = [bld.ps("PDa", [128, 512], F32, stack=cmain),
          bld.ps("PDb", [128, 512], F32, stack=cmain)]
    pg_free = [[(PT, 0)], [(PT, 0)]]
    pu_free = [[(PT, 0)], [(PT, 0)]]
    pd_free = [[(PT, 0)], [(PT, 0)]]
    ws_free = [[(PT, 0)] for _ in range(5)]
    ws_first = [True] * 5
    slot_rr = [0]
    sgr2_war = [[(PT, 0)]]

    def load_piece(e, half, kind, sub2):
        s = slot_rr[0] % 5
        slot_rr[0] += 1
        sem, val = ws_free[s][0]
        gv = None
        for sub in range(4):
            if kind in ("g", "u"):
                dram3 = dr["gatesT"] if kind == "g" else dr["upsT"]
                src = dram3[e, sub * 256:(sub + 1) * 256,
                            half * 1024 + sub2 * 512: half * 1024 + (sub2 + 1) * 512] \
                    .rearrange("(k p) f -> p k f", p=128)
                dst = WS[s][:, sub * 1024:(sub + 1) * 1024] \
                    .rearrange("p (k f) -> p k f", k=2)
            else:
                src = dr["downsT"][e, half * 1024 + sub2 * 512 + sub * 128:
                                   half * 1024 + sub2 * 512 + (sub + 1) * 128, :]
                dst = WS[s][:, sub * 1024:(sub + 1) * 1024]
            pre = (sem, val) if sub == 0 else None
            if ws_first[s] and sub == 0:
                bld.wait("g", VT, x2t_done)
            gv = stage_load(src, 1024, dst, pre_gwait=pre)
        ws_first[s] = False
        return s, gv

    def expert(e):
        for half in range(2):
            sg1, gv_g1 = load_piece(e, half, "g", 0)
            sg2, gv_g2 = load_piece(e, half, "g", 1)
            su1, gv_u1 = load_piece(e, half, "u", 0)
            su2, gv_u2 = load_piece(e, half, "u", 1)
            sd1 = sd2 = gv_d1 = gv_d2 = None
            bld.wait("t", VT, h2_done)
            if e == 0 and half == 0:
                bld.wait("v", VT, x2t_done)
            z_last = None
            for ic in range(NC):
                gsl = sg1 if ic < 4 else sg2
                usl = su1 if ic < 4 else su2
                icl = ic % 4
                if ic == 0:
                    bld.wait("t", GT, gv_g1)
                    bld.wait("t", GT, gv_u1)
                    sd1, gv_d1 = load_piece(e, half, "d", 0)
                if ic == 4:
                    bld.wait("t", GT, gv_g2)
                    bld.wait("t", GT, gv_u2)
                for tg in range(NTG):
                    gi = (ic * NTG + tg) % 2
                    sem, val = pg_free[gi][0]
                    bld.wait("t", sem, val)
                    sem, val = pu_free[gi][0]
                    bld.wait("t", sem, val)
                    for hc in range(NC):
                        bld.mm(PG[gi][:, :],
                               WS[gsl][:, hc * 512 + icl * 128: hc * 512 + (icl + 1) * 128],
                               XBBF[:, hc * T + tg * 512: hc * T + tg * 512 + 512],
                               start=(hc == 0), stop=(hc == NC - 1))
                    for hc in range(NC):
                        pv = bld.mm(PU[gi][:, :],
                                    WS[usl][:, hc * 512 + icl * 128: hc * 512 + (icl + 1) * 128],
                                    XBBF[:, hc * T + tg * 512: hc * T + tg * 512 + 512],
                                    start=(hc == 0), stop=(hc == NC - 1),
                                    inc=PT if hc == NC - 1 else None)
                    sem, val = sgr2_war[0][0]
                    bld.wait("s", sem, val)
                    bld.wait("s", PT, pv)
                    sl = bld.op("s", "activation", SGR[:, 0:512],
                                PG[gi][:, :], AF.Silu, inc=ST)
                    pg_free[gi][0] = (ST, sl)
                    bld.wait("v", ST, sl)
                    zv = bld.op("v", "tensor_mul",
                                ZB[:, ic * T + tg * 512: ic * T + tg * 512 + 512],
                                SGR[:, 0:512], PU[gi][:, :], inc=VT)
                    sgr2_war[0][0] = (VT, zv)
                    pu_free[gi][0] = (VT, zv)
                    z_last = zv
                if ic == 3:
                    ws_free[sg1][0] = (PT, bld.cnt[PT])
                    ws_free[su1][0] = (PT, bld.cnt[PT])
                    sd2, gv_d2 = load_piece(e, half, "d", 1)
            ws_free[sg2][0] = (PT, bld.cnt[PT])
            ws_free[su2][0] = (PT, bld.cnt[PT])
            bld.wait("t", GT, gv_d1)
            bld.wait("t", GT, gv_d2)
            bld.wait("t", VT, z_last)
            for tb in range(NTT):
                for hhh in range(2):
                    di = (tb * 2 + hhh) % 2
                    sem, val = pd_free[di][0]
                    bld.wait("t", sem, val)
                    for ic in range(NC):
                        dsl = sd1 if ic < 4 else sd2
                        icl = ic % 4
                        pv = bld.mm(PD[di][:, :],
                                    ZB[:, ic * T + tb * 128: ic * T + (tb + 1) * 128],
                                    WS[dsl][:, icl * 1024 + hhh * 512:
                                            icl * 1024 + (hhh + 1) * 512],
                                    start=(ic == 0), stop=(ic == NC - 1),
                                    inc=PT if ic == NC - 1 else None)
                    bld.wait("v", PT, pv)
                    dst = X2TOK[:, tb * D + hhh * 512: tb * D + (hhh + 1) * 512]
                    scal = ONEC_F[:, 0:1] if e == 0 else WGT[:, tb * 8 + e - 1: tb * 8 + e]
                    av = bld.op("v", "scalar_tensor_tensor", dst, PD[di][:, :], scal, dst,
                                op0=ALU.mult, op1=ALU.add, inc=VT)
                    pd_free[di][0] = (VT, av)
            ws_free[sd1][0] = (PT, bld.cnt[PT])
            ws_free[sd2][0] = (PT, bld.cnt[PT])

    bld.wait("v", VT, wgt_done)
    for e in range(NEXP):
        expert(e)
    tap_dump("x3", X2TOK[:, :])
    cmain.close()

    fctx = ExitStack()
    OBUF = bld.sb("OBUF", [128, 2 * D], F32, stack=fctx)
    FSQ = bld.sb("FSQ", [128, D], F32, stack=fctx)
    FST = bld.sb("FST", [128, 8], F32, stack=fctx)
    moe_done = bld.cnt[VT]
    bld.wait("s", VT, moe_done)
    obuf_war = [[(LO, 0)], [(LO, 0)]]
    for tb in range(NTT):
        xt = X2TOK[:, tb * D:(tb + 1) * D]
        v1 = bld.op("v", "reduce_sum", FST[:, 0:1], xt, axis=AX.X, inc=VT)
        s1 = bld.op("s", "activation", FSQ[:, :], xt, AF.Square,
                    accum_out=FST[:, 1:2], inc=ST)
        bld.wait("v", VT, v1)
        bld.wait("v", ST, s1)
        v2 = bld.op("v", "tensor_scalar_mul", FST[:, 2:3], FST[:, 0:1], 1.0 / D, inc=VT)
        bld.wait("v", VT, v2)
        v3 = bld.op("v", "tensor_mul", FST[:, 3:4], FST[:, 2:3], FST[:, 2:3], inc=VT)
        bld.wait("v", VT, v3)
        v4 = bld.op("v", "scalar_tensor_tensor", FST[:, 4:5], FST[:, 1:2], 1.0 / D,
                    FST[:, 3:4], op0=ALU.mult, op1=ALU.subtract, inc=VT)
        bld.wait("v", VT, v4)
        v5 = bld.op("v", "tensor_scalar_add", FST[:, 4:5], FST[:, 4:5], EPS, inc=VT)
        bld.wait("s", VT, v5)
        s2 = bld.op("s", "activation", FST[:, 5:6], FST[:, 4:5], AF.Ln, inc=ST)
        bld.wait("s", ST, s2)
        s3 = bld.op("s", "activation", FST[:, 5:6], FST[:, 5:6], AF.Exp,
                    scale=-0.5, inc=ST)
        bld.wait("v", ST, s3)
        v6 = bld.op("v", "scalar_tensor_tensor", FST[:, 6:7], FST[:, 2:3], -1.0,
                    FST[:, 5:6], op0=ALU.mult, op1=ALU.mult, inc=VT)
        bld.wait("v", VT, v6)
        ob = tb % 2
        sem, val = obuf_war[ob][0]
        bld.wait("v", sem, val)
        v7 = bld.op("v", "tensor_scalar", OBUF[:, ob * D:(ob + 1) * D], xt,
                    FST[:, 5:6], FST[:, 6:7], op0=ALU.mult, op1=ALU.add, inc=VT)
        bld.wait("s", VT, v7)
        lo = bld.op("s", "dma_start", inc=LO, incby=16,
                    out=out[tb * 128:(tb + 1) * 128, :],
                    in_=OBUF[:, ob * D:(ob + 1) * D])
        obuf_war[ob][0] = (LO, lo)

    fctx.close(); cglobal.close(); abctx.close()
    _finish(bld)
    return nc, sorted(tap_out.keys())




def _finish(bld):
    bld.wait("y", bld.LO, bld.cnt[bld.LO])
    bld.wait("y", bld.LG, bld.cnt[bld.LG])
    _emit_block(bld)
    bld.ctx.close()


def kernel(**inputs):
    key = (BUILD_UPTO, tuple(sorted(TAPS)))
    if key not in _CACHE:
        _CACHE[key] = build_program(upto=BUILD_UPTO, taps=TAPS)
    nc, tap_names = _CACHE[key]
    w = _host_prep(inputs)
    c = _host_constants()
    x = np.asarray(inputs["L0_attn_x"], dtype=np.float32)
    base = {}
    base.update(w)
    base.update(c)
    cores = CORES if CORES is not None else list(range(8))
    in_maps = []
    for core in cores:
        m = dict(base)
        m["x"] = np.ascontiguousarray(x[core].T)
        in_maps.append(m)
    res = run_bass_kernel_spmd(nc, in_maps, core_ids=cores)
    outs = np.stack([np.asarray(res.results[i]["out"]) for i in range(len(cores))],
                    axis=0)
    if tap_names:
        kernel.last_taps = [
            {t: np.asarray(res.results[i]["tap_" + t]) for t in tap_names}
            for i in range(len(cores))]
    if len(cores) < 8:
        full = np.zeros((8, T, D), np.float32)
        full[np.array(cores)] = outs
        return full
    return outs


# version 21
# speedup vs baseline: 1.0310x; 1.0310x over previous
import numpy as np
import ml_dtypes
from contextlib import ExitStack

import concourse.bass as bass
import concourse.mybir as mybir
from concourse.bass_utils import run_bass_kernel_spmd

F32 = mybir.dt.float32
BF16 = mybir.dt.bfloat16
AF = mybir.ActivationFunctionType
ALU = mybir.AluOpType
AX = mybir.AxisListType

T = 2048
D = 1024
NC = 8
LAT = 256
RD = 32
NEXP = 8
NTT = 16
NTG = 4
BIGNEG = -1e30
EPS = 1e-5

BUILD_UPTO = "C"
TAPS = ()
CORES = None

_CACHE = {}


class B:
    def __init__(self, nc):
        self.nc = nc
        self.q = {k: [] for k in "tvsgy"}
        self.cnt = {}
        self.ctx = ExitStack()

    def sem(self, name):
        h = self.ctx.enter_context(self.nc.semaphore(name))
        self.cnt[h] = 0
        return h

    def sb(self, name, shape, dt, stack=None):
        return (stack or self.ctx).enter_context(self.nc.sbuf_tensor(name, shape, dt))

    def ps(self, name, shape, dt=F32, stack=None):
        return (stack or self.ctx).enter_context(self.nc.psum_tensor(name, shape, dt))

    def bump(self, sem, by=1):
        self.cnt[sem] += by
        return self.cnt[sem]

    def emit(self, eng, fn):
        self.q[eng].append(fn)

    def wait(self, eng, sem, val):
        if val and val > 0:
            self.emit(eng, lambda e, s=sem, v=val: e.wait_ge(s, v))

    def op(self, eng, meth, *args, inc=None, incby=1, **kw):
        v = self.bump(inc, incby) if inc is not None else None
        def f(e, m=meth, a=args, k=kw, i=inc, ib=incby):
            r = getattr(e, m)(*a, **k)
            if i is not None:
                r.then_inc(i, ib)
        self.emit(eng, f)
        return v

    def mm(self, out, lhsT, rhs, start, stop, inc=None):
        return self.op("t", "matmul", out, lhsT, rhs, start=start, stop=stop, inc=inc)


def _emit_block(bld):
    with bld.nc.Block() as block:
        @block.tensor
        def _(e):
            for f in bld.q["t"]:
                f(e)
        @block.vector
        def _(e):
            for f in bld.q["v"]:
                f(e)
        @block.scalar
        def _(e):
            for f in bld.q["s"]:
                f(e)
        @block.gpsimd
        def _(e):
            for f in bld.q["g"]:
                f(e)
        @block.sync
        def _(e):
            for f in bld.q["y"]:
                f(e)


def _host_constants():
    c = {}
    p = np.arange(128)[:, None]
    j = np.arange(128)[None, :]
    mA = np.where(p >= j, 0.0, BIGNEG).astype(np.float32)
    mB = np.where(p <= j, 0.0, BIGNEG).astype(np.float32)
    c["mask_band"] = np.concatenate([mA, mB, mA, mB], axis=1).astype(ml_dtypes.bfloat16)
    c["mask_inner"] = np.concatenate(
        [np.full((128, 384), BIGNEG, np.float32), mB,
         np.zeros((128, 512), np.float32)], axis=1).astype(ml_dtypes.bfloat16)
    ms = np.full((32, 16, 128), BIGNEG, np.float32)
    qv = np.arange(128)
    for qb in range(16):
        for jj in range(32):
            ms[jj, qb, :] = np.where((64 * jj) < (128 * qb + qv - 128), 0.0, BIGNEG)
    c["mask_str"] = ms.reshape(32, 16 * 128).astype(ml_dtypes.bfloat16)
    c["mask_allneg"] = np.full((128, 256), BIGNEG, np.float32).astype(ml_dtypes.bfloat16)
    c["iden_bf"] = np.eye(128, dtype=ml_dtypes.bfloat16)
    c["iden_f"] = np.eye(128, dtype=np.float32)
    c["ones_row_bf"] = np.ones((33, 128), dtype=ml_dtypes.bfloat16)
    c["ones_blk_bf"] = np.ones((65, 128), dtype=ml_dtypes.bfloat16)
    s2m = np.zeros((128, 2), np.float32)
    s2m[0:64, 0] = 1.0
    s2m[64:128, 1] = 1.0
    c["sel2"] = s2m.astype(ml_dtypes.bfloat16)
    c["ones_col_bf"] = np.ones((128, 1), dtype=ml_dtypes.bfloat16)
    c["ones_row_f"] = np.ones((65, 512), dtype=np.float32)
    c["ones_col_f"] = np.ones((128, 1), dtype=np.float32)
    sel = np.zeros((128, NTT, 32), np.float32)
    for tile in range(NTT):
        for i in range(2):
            sel[i * 64, tile, tile * 2 + i] = 1.0
    c["vsel"] = sel.reshape(128, NTT * 32).astype(ml_dtypes.bfloat16)
    inv = 1.0 / (10000.0 ** (np.arange(0, RD, 2, dtype=np.float64) / RD))
    tpos = np.arange(T, dtype=np.float64)
    fr = tpos[:, None] * inv[None, :]
    emb = np.concatenate([fr, fr], axis=-1)
    cos = np.cos(emb).astype(np.float32)
    sin = np.sin(emb).astype(np.float32)
    c["cosE"] = np.ascontiguousarray(np.concatenate([cos, cos], axis=1))
    c["sinE"] = np.ascontiguousarray(np.concatenate([sin, sin], axis=1))
    return c


def _host_prep(inputs):
    t = lambda a: np.ascontiguousarray(np.asarray(a, dtype=np.float32).T)
    w = {}
    wqkv = np.asarray(inputs["L0_attn_w_qkv"], dtype=np.float32)
    w["wqT"] = t(wqkv[0:D])
    w["wkT"] = t(wqkv[D:2 * D])
    w["wvT"] = t(wqkv[2 * D:3 * D])
    w["woT"] = t(inputs["L0_attn_w_out"])
    w["g1"] = np.ascontiguousarray(
        np.asarray(inputs["L0_ffn_ln1_w"], dtype=np.float32).reshape(NC, 128).T)
    w["g2"] = np.ascontiguousarray(
        np.asarray(inputs["L0_ffn_ln2_w"], dtype=np.float32).reshape(NC, 128).T)
    w["kvdT"] = t(inputs["L0_ffn_kv_d"])
    w["qdT"] = t(inputs["L0_ffn_q_d"])
    w["kuT"] = t(inputs["L0_ffn_k_u"])
    w["quT"] = t(inputs["L0_ffn_q_u"])
    w["vuT"] = t(inputs["L0_ffn_v_u"])
    w["rqT"] = t(inputs["L0_ffn_rope_q_w"])
    w["rkT"] = t(inputs["L0_ffn_rope_k_w"])
    w["owT2"] = t(inputs["L0_ffn_o_w"])
    w["rwT"] = t(inputs["L0_ffn_router_w"])
    w["rbias"] = np.asarray(inputs["L0_ffn_routing_bias"], dtype=np.float32).reshape(1, 7)
    sg = np.asarray(inputs["L0_ffn_shared_gate"], dtype=np.float32)
    su = np.asarray(inputs["L0_ffn_shared_up"], dtype=np.float32)
    sd = np.asarray(inputs["L0_ffn_shared_down"], dtype=np.float32)
    rg = np.asarray(inputs["L0_ffn_routed_gate"], dtype=np.float32)
    ru = np.asarray(inputs["L0_ffn_routed_up"], dtype=np.float32)
    rd = np.asarray(inputs["L0_ffn_routed_down"], dtype=np.float32)
    w["gatesT"] = np.ascontiguousarray(np.concatenate([sg, rg], 0).transpose(0, 2, 1))
    w["upsT"] = np.ascontiguousarray(np.concatenate([su, ru], 0).transpose(0, 2, 1))
    w["downsT"] = np.ascontiguousarray(np.concatenate([sd, rd], 0).transpose(0, 2, 1))
    return w


def build_program(upto="C", taps=()):
    nc = bass.Bass("TRN2", target_bir_lowering=False)
    bld = B(nc)
    taps = set(taps)

    dr = {}
    def din(name, shape, dt=F32):
        dr[name] = nc.dram_tensor(name, list(shape), dt, kind="ExternalInput").ap()
    din("x", [D, T])
    din("wqT", [D, D]); din("wkT", [D, D]); din("wvT", [D, D]); din("woT", [D, D])
    din("g1", [128, NC]); din("g2", [128, NC])
    din("kvdT", [D, LAT]); din("qdT", [D, LAT])
    din("kuT", [LAT, D]); din("quT", [LAT, D]); din("vuT", [LAT, D])
    din("rqT", [LAT, 512]); din("rkT", [D, 512])
    din("owT2", [D, D])
    din("rwT", [D, 7]); din("rbias", [1, 7])
    din("gatesT", [NEXP, D, 2048]); din("upsT", [NEXP, D, 2048])
    din("downsT", [NEXP, 2048, D])
    din("cosE", [T, 64]); din("sinE", [T, 64])
    din("mask_band", [128, 512], BF16)
    din("mask_inner", [128, 1024], BF16)
    din("mask_str", [32, 16 * 128], BF16)
    din("mask_allneg", [128, 256], BF16)
    din("iden_bf", [128, 128], BF16); din("iden_f", [128, 128], F32)
    din("ones_row_bf", [33, 128], BF16); din("ones_col_bf", [128, 1], BF16)
    din("ones_blk_bf", [65, 128], BF16)
    din("sel2", [128, 2], BF16)
    din("ones_row_f", [65, 512], F32); din("ones_col_f", [128, 1], F32)
    din("vsel", [128, NTT * 32], BF16)

    out = nc.dram_tensor("out", [T, D], F32, kind="ExternalOutput").ap()
    tap_out = {}
    def mktap(name, shape, dt=F32):
        if name in taps:
            tap_out[name] = nc.dram_tensor("tap_" + name, list(shape), dt,
                                           kind="ExternalOutput").ap()
    mktap("x1", [128, NC * T]); mktap("h", [128, NC * T], BF16)
    mktap("x2", [128, NC * T]); mktap("h2", [128, NC * T], BF16)
    mktap("wgt", [128, NTT * 8]); mktap("x3", [128, NTT * D])
    mktap("prb", [128, NTT * 8])
    mktap("ao", [128, NC * T], BF16); mktap("yb", [128, NC * T], BF16)
    mktap("yraw", [64, T]); mktap("sums", [1, T])
    mktap("sumsb", [1, T]); mktap("mbb", [128, 2])
    mktap("yraw2", [64, T]); mktap("sums2", [1, T])
    for _c in range(NC):
        mktap("kc%d" % _c, [128, T], BF16)
        mktap("mb%d" % _c, [128, 2])
    mktap("rtb3", [128, 512], BF16)
    mktap("wrk3", [128, 512], BF16)
    mktap("sumsb2", [1, T]); mktap("mbb2", [128, 2])
    mktap("q0", [128, T], BF16); mktap("k0", [128, T], BF16)
    mktap("v0", [128, NTT * 130], BF16)
    mktap("lat", [128, 2 * T], BF16)

    LD = bld.LD = bld.sem("LD")
    LG = bld.LG = bld.sem("LG")
    LO = bld.LO = bld.sem("LO")
    PT = bld.PT = bld.sem("PT")
    VT = bld.VT = bld.sem("VT")
    ST = bld.ST = bld.sem("ST")
    GT = bld.GT = bld.sem("GT")
    LU = bld.LU = bld.sem("LU")
    CSLD = bld.CSLD = bld.sem("CSLD")
    XLD = bld.XLD = bld.sem("XLD")

    XBBF = bld.sb("XBBF", [128, NC * T], BF16)
    STG = bld.sb("STG", [128, 2 * 1024], F32)
    IDF = bld.sb("IDF", [128, 128], F32)
    ONER_BF = bld.sb("ONER_BF", [33, 128], BF16)
    ONEC_BF = bld.sb("ONEC_BF", [128, 1], BF16)
    ONER_F = bld.sb("ONER_F", [65, 512], F32)
    ONEC_F = bld.sb("ONEC_F", [128, 1], F32)
    G1 = bld.sb("G1", [128, NC], F32)
    G2 = bld.sb("G2", [128, NC], F32)
    STA = bld.sb("STA", [128, 512], F32)
    STB = bld.sb("STB", [128, 512], F32)
    STC = bld.sb("STC", [128, 512], F32)
    BCR = bld.sb("BCR", [1, 512], F32)
    LNSB = bld.sb("LNSB", [128, 512], BF16)
    LNSQ = bld.sb("LNSQ", [128, 512], BF16)
    RWT = bld.sb("RWT", [128, NC * 7], F32)
    RBIAS = bld.sb("RBIAS", [1, 7], F32)
    SGR = bld.sb("SGR", [128, 512], F32)

    abctx = ExitStack()
    RES = bld.sb("RES", [128, NC * T], F32, stack=abctx)
    cglobal = ExitStack()
    LGT = bld.sb("LGT", [7, 512], F32, stack=cglobal)
    PRBp = bld.sb("PRBp", [128, NTT * 8], F32, stack=cglobal)
    WGT = bld.sb("WGT", [128, NTT * 8], F32, stack=cglobal)
    TOPS = bld.sb("TOPS", [128, 8], F32, stack=cglobal)
    aoctx = ExitStack()
    AOBF = bld.sb("AOBF", [128, NC * T], BF16, stack=aoctx)
    atc = ExitStack()
    IDB = bld.sb("IDB", [128, 128], BF16, stack=atc)
    PTRB = bld.sb("PTRB", [128, 3 * 512], BF16, stack=atc)
    MBAND2 = bld.sb("MBAND2", [128, 512], BF16, stack=atc)
    MBI = bld.sb("MBI", [128, 1024], BF16, stack=atc)
    YS = bld.sb("YS", [65, T], F32, stack=atc)
    PTSB = bld.sb("PTSB", [33, 2 * 512], BF16, stack=atc)
    MBB = bld.sb("MBB", [128, 2], F32, stack=atc)
    MBS = bld.sb("MBS", [1, 2], F32, stack=atc)
    BND = bld.sb("BND", [2, 16], F32, stack=atc)
    SEL2 = bld.sb("SEL2", [128, 2], BF16, stack=atc)

    war = {
        "negc": [(PT, 0)], "negt": [(LG, 0)], "negm": [(PT, 0)],
        "ptr": [[(PT, 0)] for _ in range(3)],
        "ptr2": [[(PT, 0)], [(PT, 0)]],
        "rrow": [[(PT, 0)], [(PT, 0)]],
        "rtb2": [[(PT, 0)], [(PT, 0)]],
        "lnsb": [(PT, 0)], "lnsq": [(PT, 0)],
        "sgr": [[(PT, 0)], [(PT, 0)]],
        "cs": [[(VT, 0)]],
        "rtb": [[(PT, 0)], [(PT, 0)]],
        "yraw": [(VT, 0)], "sums": [(PT, 0)],
    }

    def warwait(eng, key, slot=None):
        cell = war[key] if slot is None else war[key][slot]
        sem, val = cell[0]
        bld.wait(eng, sem, val)

    def warset(key, sem, val, slot=None):
        cell = war[key] if slot is None else war[key][slot]
        cell[0] = (sem, val)

    stg_free = [0, 0]
    stg_next = [0]

    def emit_bound(qring, kring, ps, ps_free):
        for mi, ring in ((0, qring), (1, kring)):
            for tg in range(NTG):
                warwait("s", "lnsq")
                sq = bld.op("s", "activation", LNSQ[:, :],
                            ring[:, tg * 512:(tg + 1) * 512], AF.Square, inc=ST)
                sem, val = ps_free[0]
                bld.wait("t", sem, val)
                bld.wait("t", ST, sq)
                pv = bld.mm(ps[0:2, 0:512], SEL2[:, :], LNSQ[:, :],
                            start=True, stop=True, inc=PT)
                warset("lnsq", PT, pv)
                bld.wait("v", PT, pv)
                rv = bld.op("v", "reduce_max", BND[0:2, mi * 4 + tg: mi * 4 + tg + 1],
                            ps[0:2, 0:512], axis=AX.X, inc=VT)
                ps_free[0] = (VT, rv)
        bld.wait("v", VT, bld.cnt[VT])
        r1 = bld.op("v", "reduce_max", BND[0:2, 8:9], BND[0:2, 0:4], axis=AX.X, inc=VT)
        bld.wait("v", VT, r1)
        r2 = bld.op("v", "reduce_max", BND[0:2, 9:10], BND[0:2, 4:8], axis=AX.X, inc=VT)
        bld.wait("v", VT, r2)
        r3 = bld.op("v", "tensor_add", BND[0:2, 10:11], BND[0:2, 8:9],
                    BND[0:2, 9:10], inc=VT)
        bld.wait("v", VT, r3)
        r4 = bld.op("v", "tensor_scalar_mul", BND[0:2, 10:11], BND[0:2, 10:11],
                    -0.0625, inc=VT)
        bld.wait("v", VT, r4)
        r5 = bld.op("v", "tensor_scalar_add", BND[0:2, 10:11], BND[0:2, 10:11],
                    36.0, inc=VT)
        bld.wait("v", VT, r5)
        r6 = bld.op("v", "tensor_scalar_min", BND[0:2, 10:11], BND[0:2, 10:11],
                    0.0, inc=VT)
        sem, val = ps_free[0]
        bld.wait("t", sem, val)
        bld.wait("t", VT, r6)
        tv = bld.op("t", "transpose", ps[0:1, 0:2], BND[0:2, 10:11],
                    IDF[0:2, 0:2], inc=PT)
        bld.wait("s", PT, tv)
        me = bld.op("s", "copy", MBS[0:1, 0:2], ps[0:1, 0:2], inc=ST)
        for hh in range(2):
            sem, val = ps_free[0]
            bld.wait("t", sem, val)
            bld.wait("t", ST, me)
            bv = bld.mm(ps[0:128, 0:1], ONER_F[0:1, 0:128],
                        MBS[0:1, hh:hh + 1], start=True, stop=True, inc=PT)
            bld.wait("s", PT, bv)
            mbe = bld.op("s", "copy", MBB[:, hh:hh + 1], ps[0:128, 0:1], inc=ST)
            ps_free[0] = (ST, mbe)

    def stage_load(dram_ap, cols, cast_dst, pre_gwait=None):
        s = stg_next[0] % 2
        stg_next[0] += 1
        bld.wait("y", GT, stg_free[s])
        lv = bld.op("y", "dma_start", inc=LD, incby=16,
                    out=STG[:, s * 1024: s * 1024 + cols], in_=dram_ap)
        if pre_gwait is not None:
            pws = pre_gwait if isinstance(pre_gwait, list) else [pre_gwait]
            for sem, val in pws:
                bld.wait("g", sem, val)
        bld.wait("g", LD, lv)
        gv = bld.op("g", "tensor_copy", cast_dst,
                    STG[:, s * 1024: s * 1024 + cols], inc=GT)
        stg_free[s] = gv
        return gv

    def tap_dump(name, src_ap):
        if name in tap_out:
            for sem in (PT, VT, ST, GT):
                bld.wait("s", sem, bld.cnt[sem])
            bld.op("s", "dma_start", inc=LO, incby=16,
                   out=tap_out[name][:, :], in_=src_ap)

    actx = ExitStack()
    MSTR = bld.sb("MSTR", [32, 16 * 128], BF16, stack=actx)
    VSEL = bld.sb("VSEL", [128, NTT * 32], BF16, stack=actx)
    QRG = bld.sb("QRG", [128, T], BF16, stack=actx)
    KRG = bld.sb("KRG", [128, T], BF16, stack=actx)
    VRG = bld.sb("VRG", [128, NTT * 130], BF16, stack=actx)
    VSB = bld.sb("VSB", [32, 130], BF16, stack=actx)
    WQ = bld.sb("WQ", [128, NC * 128], BF16, stack=actx)
    WK = bld.sb("WK", [128, NC * 128], BF16, stack=actx)
    WV = bld.sb("WV", [128, NC * 128], BF16, stack=actx)
    PA = [bld.ps("PA0", [128, 512], F32, stack=actx),
          bld.ps("PA1", [128, 512], F32, stack=actx)]
    PB = [bld.ps("PB0", [128, 512], F32, stack=actx),
          bld.ps("PB1", [128, 512], F32, stack=actx),
          bld.ps("PB2x", [128, 512], F32, stack=actx)]
    PC = [bld.ps("PC0", [128, 512], F32, stack=actx),
          bld.ps("PC1", [128, 512], F32, stack=actx)]
    PM = bld.ps("PM", [128, 512], F32, stack=actx)

    for dst, src in [
        (MBAND2, dr["mask_band"][:, :]), (MBI, dr["mask_inner"][:, :]), (MSTR, dr["mask_str"][:, :]),
        (IDB, dr["iden_bf"][:, :]),
        (IDF, dr["iden_f"][:, :]), (ONER_BF, dr["ones_row_bf"][:, :]),
        (ONEC_BF, dr["ones_col_bf"][:, :]), (ONER_F, dr["ones_row_f"][:, :]),
        (ONEC_F, dr["ones_col_f"][:, :]), (G1, dr["g1"][:, :]), (G2, dr["g2"][:, :]),
        (SEL2, dr["sel2"][:, :]),
        (VSEL, dr["vsel"][:, :]), (RBIAS, dr["rbias"][:, :]),
    ]:
        bld.op("y", "dma_start", out=dst[:, :], in_=src, inc=LD, incby=16)
    bld.op("y", "dma_start", out=RWT[:, :].rearrange("p (k f) -> p k f", k=NC),
           in_=dr["rwT"][:, :].rearrange("(k p) f -> p k f", p=128), inc=LD, incby=16)
    bld.op("g", "memset", STA[:, :], 0.0)
    bld.op("g", "memset", STB[:, :], 0.0)
    zmem = bld.op("g", "memset", STC[:, :], 0.0, inc=GT)

    xld = bld.op("y", "dma_start", inc=XLD, incby=16,
                 out=RES[:, :].rearrange("p (k f) -> p k f", k=NC),
                 in_=dr["x"][:, :].rearrange("(k p) f -> p k f", p=128))

    pa_free = [[(PT, 0)], [(PT, 0)]]
    pb_free = [[(PT, 0)], [(PT, 0)], [(PT, 0)]]
    pc_free = [[(PT, 0)], [(PT, 0)]]
    pm_free = [(PT, 0)]
    vs_free = [(PT, 0)]

    bld.wait("v", XLD, xld)
    xb_cast = []
    for k in range(NC):
        v = bld.op("v", "tensor_copy", XBBF[:, k * T:(k + 1) * T],
                   RES[:, k * T:(k + 1) * T], inc=VT)
        xb_cast.append(v)

    wbuf_ready = {}
    wbuf_freed = {"q": [0], "k": [0], "v": [0]}
    ring_free = {"q": [(PT, 0)], "k": [(PT, 0)], "v": [(PT, 0)]}
    ring_ready = {}

    def load_wchunk(mat, dram, c):
        dst = {"q": WQ, "k": WK, "v": WV}[mat]
        gv = stage_load(
            dram[:, c * 128:(c + 1) * 128].rearrange("(k p) f -> p k f", p=128),
            1024,
            dst[:, :].rearrange("p (k f) -> p k f", k=NC),
            pre_gwait=(PT, wbuf_freed[mat][0]))
        wbuf_ready[(mat, c)] = gv

    def qkv_chunk(c):
        for mat, wbuf, ring in (("q", WQ, QRG), ("k", WK, KRG)):
            bld.wait("t", GT, wbuf_ready[(mat, c)])
            sem, val = ring_free[mat][0]
            bld.wait("t", sem, val)
            ev = None
            for tg in range(NTG):
                pi = tg % 2
                sem, val = pa_free[pi][0]
                bld.wait("t", sem, val)
                for k in range(NC):
                    bld.wait("t", VT, xb_cast[k])
                    pv = bld.mm(PA[pi][:, :],
                                wbuf[:, k * 128:(k + 1) * 128],
                                XBBF[:, k * T + tg * 512: k * T + tg * 512 + 512],
                                start=(k == 0), stop=(k == NC - 1),
                                inc=PT if k == NC - 1 else None)
                bld.wait("s", PT, pv)
                ev = bld.op("s", "copy", ring[:, tg * 512: tg * 512 + 512],
                            PA[pi][:, :], inc=ST)
                pa_free[pi][0] = (ST, ev)
            ring_ready[mat] = ev
        bld.wait("t", GT, wbuf_ready[("v", c)])
        sem, val = ring_free["v"][0]
        bld.wait("g", sem, val)
        mz = bld.op("g", "memset",
                    VRG[0:128, :].rearrange("p (t c) -> p t c", c=130)[:, :, 64:130:65],
                    1.0, inc=GT)
        ev = None
        for tb in range(NTT):
            pi = tb % 2
            sem, val = pa_free[pi][0]
            bld.wait("t", sem, val)
            for k in range(NC):
                pv = bld.mm(PA[pi][:, 0:128],
                            XBBF[:, k * T + tb * 128: k * T + (tb + 1) * 128],
                            WV[:, k * 128:(k + 1) * 128],
                            start=(k == 0), stop=(k == NC - 1),
                            inc=PT if k == NC - 1 else None)
            bld.wait("s", PT, pv)
            bld.wait("s", GT, mz)
            base = tb * 130
            bld.op("s", "copy", VRG[:, base: base + 64], PA[pi][:, 0:64])
            ev = bld.op("s", "copy", VRG[:, base + 65: base + 129], PA[pi][:, 64:128],
                        inc=ST)
            pa_free[pi][0] = (ST, ev)
        ring_ready["v"] = ev
        bld.wait("t", ST, ev)
        sem, val = vs_free[0]
        bld.wait("t", sem, val)
        sem, val = pm_free[0]
        bld.wait("t", sem, val)
        for tile in range(NTT):
            pv = bld.mm(PM[0:32, 0:130],
                        VSEL[:, tile * 32:(tile + 1) * 32],
                        VRG[:, tile * 130:(tile + 1) * 130],
                        start=(tile == 0), stop=(tile == NTT - 1),
                        inc=PT if tile == NTT - 1 else None)
        bld.wait("s", PT, pv)
        vs_ev = bld.op("s", "copy", VSB[0:32, 0:130], PM[0:32, 0:130], inc=ST)
        pm_free[0] = (ST, vs_ev)
        return vs_ev

    def attn_outer_dual(c, vs_ev):
        bld.wait("t", ST, ring_ready["q"])
        bld.wait("t", ST, ring_ready["k"])
        bld.wait("t", ST, vs_ev)
        units = [(bb, hh) for bb in range(NTG) for hh in (0, 1)]

        def emit_pair(gp):
            u, j2 = gp // 2, gp % 2
            bb, hh = units[u]
            r0 = hh * 64
            pp = 2 * bb + j2
            pi = gp % 3
            sem, val = pb_free[pi][0]
            bld.wait("t", sem, val)
            first = True
            for j in range(2):
                qb = 2 * pp + j
                qs = (qb * 128, (qb + 1) * 128)
                co = j * 256
                if qb > 0:
                    bld.mm(PB[pi][:, co:co + 128],
                           KRG[r0:r0 + 64, (qb - 1) * 128: qb * 128],
                           QRG[r0:r0 + 64, qs[0]:qs[1]], start=first, stop=False)
                    first = False
                bld.mm(PB[pi][:, co + 128:co + 256],
                       KRG[r0:r0 + 64, qs[0]:qs[1]],
                       QRG[r0:r0 + 64, qs[0]:qs[1]], start=first, stop=False)
                first = False
            return bld.mm(PB[pi][:, 0:512], IDB[:, :], MBAND2[:, :],
                          start=False, stop=True, inc=PT)

        def emit_sstr(u):
            bb, hh = units[u]
            r0 = hh * 64
            sem, val = pm_free[0]
            bld.wait("t", sem, val)
            bld.mm(PM[0:32, 0:512],
                   KRG[r0:r0 + 64, 0:T:64],
                   QRG[r0:r0 + 64, bb * 512:(bb + 1) * 512],
                   start=True, stop=False)
            return bld.mm(PM[0:32, 0:512], IDB[0:32, 0:32],
                          MSTR[0:32, bb * 512:(bb + 1) * 512],
                          start=False, stop=True, inc=PT)

        s2pv = {0: emit_pair(0), 1: emit_pair(1)}
        sstrpv = {0: emit_sstr(0)}
        exs = {}
        pend = []
        yv = None

        def flush_norm():
            nonlocal yv
            while pend:
                dbb, dhh, dysc, du, drv, devv = pend.pop(0)
                pa_i = du % 2
                sem, val = pa_free[pa_i][0]
                bld.wait("t", sem, val)
                bld.wait("t", ST, drv)
                bld.wait("t", VT, devv)
                bv = bld.op("t", "matmul", PA[pa_i][0:64, 0:512],
                            ONER_BF[32:33, 0:64],
                            PTSB[32:33, dhh * 512:(dhh + 1) * 512],
                            start=True, stop=True, inc=PT)
                warset("rrow", PT, bv, dhh)
                bld.wait("v", PT, bv)
                yv = bld.op("v", "tensor_mul",
                            AOBF[dhh * 64:dhh * 64 + 64,
                                 c * T + dbb * 512: c * T + (dbb + 1) * 512],
                            YS[0:64, dysc:dysc + 512], PA[pa_i][0:64, 0:512],
                            inc=VT)
                pa_free[pa_i][0] = (VT, yv)

        for u, (bb, hh) in enumerate(units):
            r0 = hh * 64
            for j2 in range(2):
                gp = 2 * u + j2
                if gp + 2 < 16:
                    s2pv[gp + 2] = emit_pair(gp + 2)
                pi = gp % 3
                pts = gp % 3
                warwait("s", "ptr", pts)
                bld.wait("s", PT, s2pv[gp])
                ex = bld.op("s", "activation", PTRB[:, pts * 512:(pts + 1) * 512],
                            PB[pi][:, 0:512], AF.Exp, bias=MBB[:, hh:hh + 1],
                            scale=0.125, inc=ST)
                pb_free[pi][0] = (ST, ex)
                exs[gp] = ex
            flush_norm()
            pts2 = u % 2
            warwait("s", "ptr2", pts2)
            bld.wait("s", PT, sstrpv[u])
            exstr = bld.op("s", "activation",
                           PTSB[0:32, pts2 * 512:(pts2 + 1) * 512],
                           PM[0:32, 0:512], AF.Exp, bias=MBB[0:32, hh:hh + 1],
                           scale=0.125, inc=ST)
            pm_free[0] = (ST, exstr)
            sem, val = pc_free[hh][0]
            bld.wait("t", sem, val)
            first = True
            for j in range(4):
                qb = 4 * bb + j
                gp = 2 * u + j // 2
                pts = gp % 3
                co = (qb % 2) * 256
                if j % 2 == 0:
                    bld.wait("t", ST, exs[gp])
                if qb > 0:
                    bld.mm(PC[hh][0:65, j * 128:(j + 1) * 128],
                           VRG[:, (qb - 1) * 130 + hh * 65:
                               (qb - 1) * 130 + hh * 65 + 65],
                           PTRB[:, pts * 512 + co: pts * 512 + co + 128],
                           start=first, stop=False)
                    first = False
                lastmm = bld.mm(PC[hh][0:65, j * 128:(j + 1) * 128],
                                VRG[:, qb * 130 + hh * 65: qb * 130 + hh * 65 + 65],
                                PTRB[:, pts * 512 + co + 128: pts * 512 + co + 256],
                                start=first, stop=False,
                                inc=PT if j % 2 == 1 else None)
                first = False
                if j % 2 == 1:
                    warset("ptr", PT, lastmm, pts)
            bld.wait("t", ST, exstr)
            av = bld.mm(PC[hh][0:65, 0:512], VSB[0:32, hh * 65: hh * 65 + 65],
                        PTSB[0:32, pts2 * 512:(pts2 + 1) * 512],
                        start=False, stop=True, inc=PT)
            warset("ptr2", PT, av, pts2)
            if u + 1 < len(units):
                sstrpv[u + 1] = emit_sstr(u + 1)
            ysc = ((bb % 2) * 2 + hh) * 512
            bld.wait("v", PT, av)
            evv = bld.op("v", "tensor_copy", YS[:, ysc:ysc + 512],
                         PC[hh][0:65, 0:512], inc=VT)
            pc_free[hh][0] = (VT, evv)
            bld.wait("s", VT, evv)
            l1 = bld.op("s", "activation", YS[64:65, ysc:ysc + 512],
                        YS[64:65, ysc:ysc + 512], AF.Ln, inc=ST)
            bld.wait("s", ST, l1)
            warwait("s", "rrow", hh)
            rv = bld.op("s", "activation", PTSB[32:33, hh * 512:(hh + 1) * 512],
                        YS[64:65, ysc:ysc + 512], AF.Exp,
                        scale=-1.0, inc=ST)
            pend.append((bb, hh, ysc, u, rv, evv))
        flush_norm()
        return yv

    a_last = None
    xb_dead_pt = [0]
    load_wchunk("q", dr["wqT"], 0)
    load_wchunk("k", dr["wkT"], 0)
    load_wchunk("v", dr["wvT"], 0)
    for c in range(NC):
        vs_ev = qkv_chunk(c)
        wbuf_freed["q"][0] = bld.cnt[PT]
        wbuf_freed["k"][0] = bld.cnt[PT]
        wbuf_freed["v"][0] = bld.cnt[PT]
        if c + 1 < NC:
            load_wchunk("q", dr["wqT"], c + 1)
            load_wchunk("k", dr["wkT"], c + 1)
            load_wchunk("v", dr["wvT"], c + 1)
        if c == NC - 1:
            xb_dead_pt[0] = bld.cnt[PT]
        bld.wait("s", ST, ring_ready["q"])
        bld.wait("s", ST, ring_ready["k"])
        emit_bound(QRG, KRG, PM, pm_free)
        a_last = attn_outer_dual(c, vs_ev)
        ring_free["q"][0] = (PT, bld.cnt[PT])
        ring_free["k"][0] = (PT, bld.cnt[PT])
        ring_free["v"][0] = (PT, bld.cnt[PT])
        vs_free[0] = (PT, bld.cnt[PT])
    tap_dump("ao", AOBF[:, :])
    tap_dump("yraw", YS[0:64, :]); tap_dump("sums", YS[64:65, :])
    tap_dump("sumsb", YS[64:65, :]); tap_dump("mbb", MBB[:, :])

    wo_ready = []
    for k in range(NC):
        gv = stage_load(dr["woT"][k * 128:(k + 1) * 128, :], 1024,
                        XBBF[:, 8192 + k * 1024: 8192 + (k + 1) * 1024],
                        pre_gwait=(PT, xb_dead_pt[0]))
        wo_ready.append(gv)

    def proj_to_res(wbase, src_bf, ps_pair, ps_free, wready):
        for oc in range(NC):
            for tg in range(NTG):
                pi = (oc * NTG + tg) % 2
                sem, val = ps_free[pi][0]
                bld.wait("t", sem, val)
                for k in range(NC):
                    bld.wait("t", GT, wready[k])
                    pv = bld.mm(ps_pair[pi][:, :],
                                XBBF[:, wbase + k * 1024 + oc * 128:
                                     wbase + k * 1024 + (oc + 1) * 128],
                                src_bf[:, k * T + tg * 512: k * T + tg * 512 + 512],
                                start=(k == 0), stop=(k == NC - 1),
                                inc=PT if k == NC - 1 else None)
                bld.wait("v", PT, pv)
                rv = bld.op("v", "tensor_add",
                            RES[:, oc * T + tg * 512: oc * T + tg * 512 + 512],
                            RES[:, oc * T + tg * 512: oc * T + tg * 512 + 512],
                            ps_pair[pi][:, :], inc=VT)
                ps_free[pi][0] = (VT, rv)

    bld.wait("t", VT, a_last)
    proj_to_res(8192, AOBF, PA, pa_free, wo_ready)

    def feature_ln(with_mean, gamma, out_bf, inplace, f32_sink=False,
                   ps_pair=None, ps_free=None, router_ctx=None, h_fused=None):
        bld.wait("s", VT, bld.cnt[VT])
        bld.wait("t", VT, bld.cnt[VT])
        for tg in range(NTG):
            for pi in range(2):
                sem, val = ps_free[pi][0]
                bld.wait("t", sem, val)
            pv_sum = None
            pv_sq = None
            for k in range(NC):
                if with_mean:
                    warwait("v", "lnsb")
                    cv = bld.op("v", "tensor_copy", LNSB[:, :],
                                RES[:, k * T + tg * 512: k * T + tg * 512 + 512], inc=VT)
                    bld.wait("t", VT, cv)
                    pv_sum = bld.mm(ps_pair[0][0:1, 0:512], ONEC_BF[:, :], LNSB[:, :],
                                    start=(k == 0), stop=(k == NC - 1), inc=PT)
                    warset("lnsb", PT, pv_sum)
                warwait("s", "lnsq")
                sv = bld.op("s", "activation", LNSQ[:, :],
                            RES[:, k * T + tg * 512: k * T + tg * 512 + 512],
                            AF.Square, inc=ST)
                bld.wait("t", ST, sv)
                pv_sq = bld.mm(ps_pair[1][0:1, 0:512], ONEC_BF[:, :], LNSQ[:, :],
                               start=(k == 0), stop=(k == NC - 1), inc=PT)
                warset("lnsq", PT, pv_sq)
            row = tg * 32
            if with_mean:
                bld.wait("s", PT, pv_sum)
                e1 = bld.op("s", "copy", STA[row:row + 1, :],
                            ps_pair[0][0:1, 0:512], inc=ST)
                ps_free[0][0] = (ST, e1)
            bld.wait("s", PT, pv_sq)
            e2 = bld.op("s", "copy", STB[row:row + 1, :],
                        ps_pair[1][0:1, 0:512], inc=ST)
            ps_free[1][0] = (ST, e2)
        bld.wait("v", ST, bld.cnt[ST])
        bld.wait("v", GT, zmem)
        if with_mean:
            v1 = bld.op("v", "tensor_scalar_mul", STA[:, :], STA[:, :],
                        1.0 / D, inc=VT)
            bld.wait("v", VT, v1)
            v2 = bld.op("v", "tensor_mul", STC[:, :], STA[:, :], STA[:, :], inc=VT)
            bld.wait("v", VT, v2)
            v3 = bld.op("v", "scalar_tensor_tensor", STB[:, :], STB[:, :],
                        1.0 / D, STC[:, :], op0=ALU.mult, op1=ALU.subtract, inc=VT)
        else:
            v3 = bld.op("v", "tensor_scalar_mul", STB[:, :], STB[:, :],
                        1.0 / D, inc=VT)
        bld.wait("v", VT, v3)
        v4 = bld.op("v", "tensor_scalar_add", STB[:, :], STB[:, :], EPS, inc=VT)
        bld.wait("s", VT, v4)
        s1 = bld.op("s", "activation", STB[:, :], STB[:, :], AF.Ln, inc=ST)
        bld.wait("s", ST, s1)
        s2 = bld.op("s", "activation", STB[:, :], STB[:, :], AF.Exp,
                    scale=-0.5, inc=ST)
        v5 = None
        if with_mean:
            bld.wait("v", ST, s2)
            v5 = bld.op("v", "tensor_mul", STC[:, :], STA[:, :], STB[:, :], inc=VT)
        r1s = None
        if h_fused is not None:
            bld.wait("v", VT, v5)
            v6 = bld.op("v", "tensor_mul", STA[:, :], STB[:, :], STB[:, :], inc=VT)
            bld.wait("v", VT, v6)
            v7 = bld.op("v", "tensor_scalar", STA[:, :], STA[:, :],
                        -EPS, 1.0 + EPS, op0=ALU.mult, op1=ALU.add, inc=VT)
            bld.wait("s", VT, v7)
            s3 = bld.op("s", "activation", STA[:, :], STA[:, :], AF.Ln, inc=ST)
            bld.wait("s", ST, s3)
            r1s = bld.op("s", "activation", STA[:, :], STA[:, :], AF.Exp,
                         scale=-0.5, inc=ST)
        bc_war = getattr(feature_ln, "_bc_war", None)
        if bc_war is None:
            bc_war = [(PT, 0)]
            feature_ln._bc_war = bc_war
        for tg in range(NTG):
            row = tg * 32
            sem, val = ps_free[0][0]
            bld.wait("t", sem, val)
            sem, val = bc_war[0]
            bld.wait("s", sem, val)
            bld.wait("s", ST, s2)
            if with_mean:
                bld.wait("s", VT, v5)
            cb1 = bld.op("s", "copy", BCR[:, :], STB[row:row + 1, :], inc=ST)
            bld.wait("t", ST, cb1)
            rbv = bld.mm(ps_pair[0][:, :], ONER_F[0:1, 0:128],
                         BCR[:, :], start=True, stop=True, inc=PT)
            b2 = None
            if with_mean:
                sem, val = ps_free[1][0]
                bld.wait("t", sem, val)
                bld.wait("s", PT, rbv)
                cb2 = bld.op("s", "copy", BCR[:, :], STC[row:row + 1, :], inc=ST)
                bld.wait("t", ST, cb2)
                b2 = bld.mm(ps_pair[1][:, :], ONER_F[0:1, 0:128],
                            BCR[:, :], start=True, stop=True, inc=PT)
            bc_war[0] = (PT, bld.cnt[PT])
            bld.wait("v", PT, bld.cnt[PT])
            last_v = None
            for k in range(NC):
                xa = RES[:, k * T + tg * 512: k * T + tg * 512 + 512]
                if inplace:
                    bld.op("v", "tensor_mul", xa, xa, ps_pair[0][:, :])
                    last_v = bld.op("v", "tensor_sub", xa, xa, ps_pair[1][:, :], inc=VT)
                    continue
                if out_bf is not None:
                    dstt, base = out_bf
                    dst = dstt[:, base + k * T + tg * 512: base + k * T + tg * 512 + 512]
                    if with_mean:
                        sgs = 0
                        warwait("v", "sgr", sgs)
                        sf = SGR[:, 0:512]
                        bld.op("v", "tensor_mul", sf, xa, ps_pair[0][:, :])
                        fv0 = bld.op("v", "tensor_sub", sf, sf, ps_pair[1][:, :], inc=VT)
                        bld.wait("v", VT, fv0)
                        if f32_sink:
                            fv = bld.op("v", "tensor_scalar_mul", sf, sf,
                                        gamma[:, k:k + 1], inc=VT)
                            bld.wait("v", VT, fv)
                            last_v = bld.op("v", "tensor_copy", dst, sf, inc=VT)
                            bld.wait("t", VT, fv)
                            rpv = bld.mm(router_ctx["psum"][tg][0:7, 0:512],
                                         RWT[:, k * 7:(k + 1) * 7], sf,
                                         start=(k == 0), stop=(k == NC - 1), inc=PT)
                            router_ctx["mark"][tg] = rpv
                            warset("sgr", PT, rpv, sgs)
                        else:
                            if gamma is not None:
                                last_v = bld.op("v", "tensor_scalar_mul", dst, sf,
                                                gamma[:, k:k + 1], inc=VT)
                            else:
                                last_v = bld.op("v", "tensor_copy", dst, sf, inc=VT)
                            warset("sgr", VT, last_v, sgs)
                    elif gamma is not None:
                        last_v = bld.op("v", "scalar_tensor_tensor", dst, xa,
                                        gamma[:, k:k + 1], ps_pair[0][:, :],
                                        op0=ALU.mult, op1=ALU.mult, inc=VT)
                    else:
                        last_v = bld.op("v", "tensor_mul", dst, xa,
                                        ps_pair[0][:, :], inc=VT)
            if h_fused is not None:
                dstt, base, gam = h_fused
                sem, val = bc_war[0]
                bld.wait("s", sem, val)
                bld.wait("s", ST, r1s)
                cb3 = bld.op("s", "copy", BCR[:, :], STA[row:row + 1, :], inc=ST)
                bld.wait("t", ST, cb3)
                bld.wait("t", VT, last_v)
                b3 = bld.mm(ps_pair[0][:, :], ONER_F[0:1, 0:128], BCR[:, :],
                            start=True, stop=True, inc=PT)
                bc_war[0] = (PT, bld.cnt[PT])
                bld.wait("v", PT, b3)
                for k in range(NC):
                    xa = RES[:, k * T + tg * 512: k * T + tg * 512 + 512]
                    dst = dstt[:, base + k * T + tg * 512:
                               base + k * T + tg * 512 + 512]
                    last_v = bld.op("v", "scalar_tensor_tensor", dst, xa,
                                    gam[:, k:k + 1], ps_pair[0][:, :],
                                    op0=ALU.mult, op1=ALU.mult, inc=VT)
            ps_free[0][0] = (VT, last_v)
            if b2 is not None:
                ps_free[1][0] = (VT, last_v)
        return bld.cnt[VT]

    h_done = feature_ln(True, None, None, True, ps_pair=PB, ps_free=pb_free,
                        h_fused=(XBBF, 0, G1))
    tap_dump("x1", RES[:, :])
    tap_dump("h", XBBF[:, :])

    if upto == "A":
        actx.close(); atc.close(); aoctx.close(); cglobal.close(); abctx.close()
        _finish(bld)
        return nc, sorted(tap_out.keys())

    actx.close()
    a_end_pt = bld.cnt[PT]
    a_end_vt = bld.cnt[VT]
    bctx = ExitStack()
    KVL = bld.sb("KVL", [128, 2 * T], BF16, stack=bctx)
    QL = bld.sb("QL", [128, 2 * T], BF16, stack=bctx)
    QR2 = bld.sb("QR2", [128, T], BF16, stack=bctx)
    KR2 = bld.sb("KR2", [128, T], BF16, stack=bctx)
    VR2 = bld.sb("VR2", [128, NTT * 130], BF16, stack=bctx)
    WKU = bld.sb("WKU", [128, 256], BF16, stack=bctx)
    WQU = bld.sb("WQU", [128, 256], BF16, stack=bctx)
    WVU = bld.sb("WVU", [128, 256], BF16, stack=bctx)
    WRQ = bld.sb("WRQ", [128, 128], BF16, stack=bctx)
    WRK = bld.sb("WRK", [128, 512], BF16, stack=bctx)
    CS = bld.sb("CS", [128, 512], F32, stack=bctx)
    RSC = SGR
    RTB = bld.sb("RTB", [128, 2 * 256], BF16, stack=bctx)
    PB2 = [bld.ps("PB2%d" % i, [128, 512], F32, stack=bctx) for i in range(5)]
    PCAV = [bld.ps("PCAVa", [128, 512], F32, stack=bctx),
            bld.ps("PCAVb", [128, 512], F32, stack=bctx)]
    PRBF = bld.ps("PRBF", [128, 1024], BF16, stack=bctx)

    pb2_free = [[(PT, 0)] for _ in range(5)]
    pcav_free = [[(PT, 0)], [(PT, 0)]]
    prbf_free = [(PT, 0)]

    WKVD = QR2
    WQD = KR2
    kvd_ready = None
    a_guard = [(PT, a_end_pt), (VT, a_end_vt)]
    for kk in range(2):
        stage_load(dr["kvdT"][kk * 512:(kk + 1) * 512, :]
                   .rearrange("(k p) f -> p k f", p=128), 1024,
                   WKVD[:, kk * 1024:(kk + 1) * 1024]
                   .rearrange("p (k f) -> p k f", k=4), pre_gwait=a_guard)
        kvd_ready = stage_load(dr["qdT"][kk * 512:(kk + 1) * 512, :]
                               .rearrange("(k p) f -> p k f", p=128), 1024,
                               WQD[:, kk * 1024:(kk + 1) * 1024]
                               .rearrange("p (k f) -> p k f", k=4), pre_gwait=a_guard)

    bld.wait("t", VT, h_done)
    bld.wait("t", GT, kvd_ready)
    lat_done = None
    for (buf, wbuf) in ((KVL, WKVD), (QL, WQD)):
        for lc in range(2):
            for tg in range(NTG):
                pi = tg % 2
                sem, val = pb2_free[pi][0]
                bld.wait("t", sem, val)
                for k in range(NC):
                    pv = bld.mm(PB2[pi][:, :],
                                wbuf[:, k * 256 + lc * 128: k * 256 + (lc + 1) * 128],
                                XBBF[:, k * T + tg * 512: k * T + tg * 512 + 512],
                                start=(k == 0), stop=(k == NC - 1),
                                inc=PT if k == NC - 1 else None)
                bld.wait("s", PT, pv)
                lat_done = bld.op("s", "copy",
                                  buf[:, lc * T + tg * 512: lc * T + tg * 512 + 512],
                                  PB2[pi][:, :], inc=ST)
                pb2_free[pi][0] = (ST, lat_done)
    tap_dump("lat", KVL[:, :])

    ring2_free = {"q": [(PT, bld.cnt[PT])], "k": [(PT, bld.cnt[PT])],
                  "v": [(PT, a_end_pt)]}
    war["cs"][0][0] = (VT, a_end_vt)
    war["rtb"][0][0] = (PT, a_end_pt)
    war["rtb"][1][0] = (PT, a_end_pt)
    up_freed = [0, 0]

    lu_cast_done = [0]

    def load_up(c):
        pg = [(PT, max(up_freed[0], a_end_pt)), (VT, a_end_vt)]
        srcs = [
            (dr["kuT"][:, c * 128:(c + 1) * 128]
             .rearrange("(k p) f -> p k f", p=128), 0, 256,
             WKU[:, :].rearrange("p (k f) -> p k f", k=2)),
            (dr["quT"][:, c * 128:(c + 1) * 128]
             .rearrange("(k p) f -> p k f", p=128), 256, 256,
             WQU[:, :].rearrange("p (k f) -> p k f", k=2)),
            (dr["vuT"][:, c * 128:(c + 1) * 128]
             .rearrange("(k p) f -> p k f", p=128), 512, 256,
             WVU[:, :].rearrange("p (k f) -> p k f", k=2)),
            (dr["rqT"][:, c * 64:(c + 1) * 64]
             .rearrange("(k p) f -> p k f", p=128), 768, 128,
             WRQ[:, :].rearrange("p (k f) -> p k f", k=2)),
            (dr["rkT"][:, c * 64:(c + 1) * 64]
             .rearrange("(k p) f -> p k f", p=128), 896, 512,
             WRK[:, :].rearrange("p (k f) -> p k f", k=NC)),
        ]
        bld.wait("y", GT, max(lu_cast_done[0], stg_free[0], stg_free[1]))
        for src_ap, off, cols, _dst in srcs:
            bld.op("y", "dma_start", inc=LU, incby=16,
                   out=STG[:, off:off + cols], in_=src_ap)
        full = bld.cnt[LU]
        out = []
        for _src, off, cols, dst in srcs:
            for sem, val in pg:
                bld.wait("g", sem, val)
            bld.wait("g", LU, full)
            gv = bld.op("g", "tensor_copy", dst, STG[:, off:off + cols], inc=GT)
            out.append(gv)
        lu_cast_done[0] = out[-1]
        return tuple(out)

    def mla_qkv(c, wready):
        def up_group(mat, tg):
            wbuf = WQU if mat == "q" else WKU
            ring = QR2 if mat == "q" else KR2
            src = QL if mat == "q" else KVL
            pi = 2 + tg % 2
            sem, val = pb2_free[pi][0]
            bld.wait("t", sem, val)
            for lc in range(2):
                pv = bld.mm(PB2[pi][:, :],
                            wbuf[:, lc * 128:(lc + 1) * 128],
                            src[:, lc * T + tg * 512: lc * T + tg * 512 + 512],
                            start=(lc == 0), stop=(lc == 1),
                            inc=PT if lc == 1 else None)
            bld.wait("s", PT, pv)
            ev = bld.op("s", "copy", ring[:, tg * 512: tg * 512 + 512],
                        PB2[pi][:, :], inc=ST)
            pb2_free[pi][0] = (ST, ev)

        def vup_tb(tb, mz):
            pi = 2 + tb % 2
            sem, val = pb2_free[pi][0]
            bld.wait("t", sem, val)
            for lc in range(2):
                pv = bld.mm(PB2[pi][:, 0:128],
                            KVL[:, lc * T + tb * 128: lc * T + (tb + 1) * 128],
                            WVU[:, lc * 128:(lc + 1) * 128],
                            start=(lc == 0), stop=(lc == 1),
                            inc=PT if lc == 1 else None)
            bld.wait("s", PT, pv)
            bld.wait("s", GT, mz)
            base = tb * 130
            bld.op("s", "copy", VR2[:, base: base + 64], PB2[pi][:, 0:64])
            ev = bld.op("s", "copy", VR2[:, base + 65: base + 129],
                        PB2[pi][:, 64:128], inc=ST)
            pb2_free[pi][0] = (ST, ev)

        def rope_group(mat, ring, wr, g):
            tb0 = g * 4
            warwait("g", "cs", 0)
            bld.op("g", "dma_start", inc=CSLD, incby=16,
                   out=CS[:, 0:256].rearrange("p (j f) -> p j f", j=4),
                   in_=dr["cosE"][tb0 * 128:(tb0 + 4) * 128, :]
                   .rearrange("(j p) f -> p j f", p=128))
            sv_ = bld.op("g", "dma_start", inc=CSLD, incby=16,
                         out=CS[:, 256:512].rearrange("p (j f) -> p j f", j=4),
                         in_=dr["sinE"][tb0 * 128:(tb0 + 4) * 128, :]
                         .rearrange("(j p) f -> p j f", p=128))
            pi = g % 2
            sem, val = pb2_free[pi][0]
            bld.wait("t", sem, val)
            bld.wait("t", GT, wr)
            for j in range(4):
                tb = tb0 + j
                if mat == "q":
                    for lc in range(2):
                        pv = bld.mm(PB2[pi][:, j * 64:(j + 1) * 64],
                                    QL[:, lc * T + tb * 128: lc * T + (tb + 1) * 128],
                                    WRQ[:, lc * 64:(lc + 1) * 64],
                                    start=(lc == 0), stop=(lc == 1),
                                    inc=PT if (lc == 1 and j == 3) else None)
                else:
                    for k in range(NC):
                        pv = bld.mm(PB2[pi][:, j * 64:(j + 1) * 64],
                                    XBBF[:, k * T + tb * 128: k * T + (tb + 1) * 128],
                                    WRK[:, k * 64:(k + 1) * 64],
                                    start=(k == 0), stop=(k == NC - 1),
                                    inc=PT if (k == NC - 1 and j == 3) else None)
            bld.wait("v", PT, pv)
            bld.wait("v", CSLD, sv_)
            bld.op("v", "tensor_scalar_mul",
                   RSC[0:128, 0:256].rearrange("p (h d) -> p h d", d=32)[:, :, 0:16],
                   PB2[pi][0:128, 0:256].rearrange("p (h d) -> p h d", d=32)[:, :, 16:32],
                   -1.0)
            x2v = bld.op("v", "tensor_scalar_mul",
                         RSC[0:128, 0:256].rearrange("p (h d) -> p h d", d=32)[:, :, 16:32],
                         PB2[pi][0:128, 0:256].rearrange("p (h d) -> p h d", d=32)[:, :, 0:16],
                         1.0, inc=VT)
            bld.op("v", "tensor_mul", RSC[:, 256:512], PB2[pi][:, 0:256],
                   CS[:, 0:256])
            bld.wait("v", VT, x2v)
            x3v = bld.op("v", "tensor_mul", RSC[:, 0:256], RSC[:, 0:256],
                         CS[:, 256:512], inc=VT)
            warset("cs", VT, x3v, 0)
            rs = g % 2
            warwait("v", "rtb2", rs)
            bld.wait("v", VT, x3v)
            x4v = bld.op("v", "tensor_add", RTB[:, rs * 256:rs * 256 + 256],
                         RSC[:, 256:512], RSC[:, 0:256], inc=VT)
            pb2_free[pi][0] = (VT, x4v)
            for jp in range(2):
                tbA = tb0 + 2 * jp
                sem, val = prbf_free[0]
                bld.wait("t", sem, val)
                bld.wait("t", VT, x4v)
                tv = bld.op("t", "transpose", PRBF[0:128, 0:128],
                            RTB[:, rs * 256 + jp * 128: rs * 256 + (jp + 1) * 128],
                            IDB[:, :], inc=PT)
                if jp == 1:
                    warset("rtb2", PT, tv, rs)
                bld.wait("s", PT, tv)
                bld.op("s", "copy", ring[0:32, tbA * 128:(tbA + 1) * 128],
                       PRBF[0:32, 0:128])
                bld.op("s", "copy", ring[64:96, tbA * 128:(tbA + 1) * 128],
                       PRBF[32:64, 0:128])
                bld.op("s", "copy", ring[0:32, (tbA + 1) * 128:(tbA + 2) * 128],
                       PRBF[64:96, 0:128])
                e2 = bld.op("s", "copy",
                            ring[64:96, (tbA + 1) * 128:(tbA + 2) * 128],
                            PRBF[96:128, 0:128], inc=ST)
                prbf_free[0] = (ST, e2)

        bld.wait("t", GT, wready[1])
        sem, val = ring2_free["q"][0]
        bld.wait("t", sem, val)
        bld.wait("t", ST, lat_done)
        for tg in range(NTG):
            up_group("q", tg)
        bld.wait("t", GT, wready[0])
        sem, val = ring2_free["k"][0]
        bld.wait("t", sem, val)
        for g in range(NTG):
            up_group("k", g)
            rope_group("q", QR2, wready[3], g)
        bld.wait("t", GT, wready[2])
        sem, val = ring2_free["v"][0]
        bld.wait("g", sem, val)
        mz = bld.op("g", "memset",
                    VR2[0:128, :].rearrange("p (t c) -> p t c", c=130)[:, :, 64:130:65],
                    1.0, inc=GT)
        for g in range(NTG):
            for tb in range(4 * g, 4 * g + 4):
                vup_tb(tb, mz)
            rope_group("k", KR2, wready[4], g)
        if c == 3 and "rtb3" in tap_out:
            bld.wait("s", VT, bld.cnt[VT])
            bld.op("s", "dma_start", out=tap_out["rtb3"][:, :], in_=RTB[:, :],
                   inc=LO, incby=16)
            bld.op("s", "dma_start", out=tap_out["wrk3"][:, :], in_=WRK[:, :],
                   inc=LO, incby=16)
        emit_bound(QR2, KR2, PB2[0], pb2_free[0])
        return bld.cnt[ST]

    def attn_inner_dual(c, ck):
        bld.wait("t", ST, ck)
        yv = None
        seq = []
        for qg in range(NTG):
            for kt in range(4 * qg + 4):
                for hh in (0, 1):
                    seq.append((qg, kt, hh))
        L = len(seq)

        def emit_s2(i):
            qg, kt, hh = seq[i]
            pi = i % 5
            sem, val = pb2_free[pi][0]
            bld.wait("t", sem, val)
            r0 = hh * 64
            r = kt - 4 * qg
            if r < 0:
                return bld.mm(PB2[pi][:, :],
                              KR2[r0:r0 + 64, kt * 128:(kt + 1) * 128],
                              QR2[r0:r0 + 64, qg * 512:(qg + 1) * 512],
                              start=True, stop=True, inc=PT)
            bld.mm(PB2[pi][:, :],
                   KR2[r0:r0 + 64, kt * 128:(kt + 1) * 128],
                   QR2[r0:r0 + 64, qg * 512:(qg + 1) * 512],
                   start=True, stop=False)
            return bld.mm(PB2[pi][:, :], IDB[:, :],
                          MBI[:, 384 - r * 128: 896 - r * 128],
                          start=False, stop=True, inc=PT)

        DEPTH = 4
        pend = []
        s2pv = {}
        for i in range(min(DEPTH, L)):
            s2pv[i] = emit_s2(i)
        for i, (qg, kt, hh) in enumerate(seq):
            pi = i % 5
            if i + DEPTH < L:
                s2pv[i + DEPTH] = emit_s2(i + DEPTH)
            nkt = 4 * qg + 4
            pts = i % 3
            warwait("s", "ptr", pts)
            bld.wait("s", PT, s2pv[i])
            ex = bld.op("s", "activation", PTRB[:, pts * 512:(pts + 1) * 512],
                        PB2[pi][:, :], AF.Exp, bias=MBB[:, hh:hh + 1],
                        scale=0.125, inc=ST)
            pb2_free[pi][0] = (ST, ex)
            if kt == 0:
                sem, val = pcav_free[hh][0]
                bld.wait("t", sem, val)
            bld.wait("t", ST, ex)
            av = bld.mm(PCAV[hh][0:65, 0:512],
                        VR2[:, kt * 130 + hh * 65: kt * 130 + hh * 65 + 65],
                        PTRB[:, pts * 512:(pts + 1) * 512],
                        start=(kt == 0), stop=(kt == nkt - 1), inc=PT)
            warset("ptr", PT, av, pts)
            if kt == nkt - 1:
                r0 = hh * 64
                ysc = ((qg % 2) * 2 + hh) * 512
                bld.wait("v", PT, av)
                evv = bld.op("v", "tensor_copy", YS[:, ysc:ysc + 512],
                             PCAV[hh][0:65, 0:512], inc=VT)
                pcav_free[hh][0] = (VT, evv)
                bld.wait("s", VT, evv)
                l1 = bld.op("s", "activation", YS[64:65, ysc:ysc + 512],
                            YS[64:65, ysc:ysc + 512], AF.Ln, inc=ST)
                bld.wait("s", ST, l1)
                warwait("s", "rrow", hh)
                rv = bld.op("s", "activation",
                            PTSB[32:33, hh * 512:(hh + 1) * 512],
                            YS[64:65, ysc:ysc + 512], AF.Exp,
                            scale=-1.0, inc=ST)
                pi2 = i % 5
                sem, val = pb2_free[pi2][0]
                bld.wait("t", sem, val)
                bld.wait("t", ST, rv)
                bld.wait("t", VT, evv)
                bv = bld.op("t", "matmul", PB2[pi2][0:64, 0:512],
                            ONER_BF[32:33, 0:64],
                            PTSB[32:33, hh * 512:(hh + 1) * 512],
                            start=True, stop=True, inc=PT)
                warset("rrow", PT, bv, hh)
                bld.wait("v", PT, bv)
                yv = bld.op("v", "tensor_mul",
                            AOBF[r0:r0 + 64,
                                 c * T + qg * 512: c * T + (qg + 1) * 512],
                            YS[0:64, ysc:ysc + 512], PB2[pi2][0:64, 0:512],
                            inc=VT)
                pb2_free[pi2][0] = (VT, yv)
        return yv

    b_last = None
    h_dead_pt = [0]
    for c in range(NC):
        wready = load_up(c)
        ck = mla_qkv(c, wready)
        if c == NC - 1:
            h_dead_pt[0] = bld.cnt[PT]
        tap_dump("kc%d" % c, KR2[:, :])
        tap_dump("mb%d" % c, MBB[:, :])
        b_last = attn_inner_dual(c, ck)
        for mat in ("q", "k", "v"):
            ring2_free[mat][0] = (PT, bld.cnt[PT])
        up_freed[0] = bld.cnt[PT]
    tap_dump("yraw2", YS[0:64, :]); tap_dump("sums2", YS[64:65, :])
    tap_dump("sumsb2", YS[64:65, :]); tap_dump("mbb2", MBB[:, :])
    tap_dump("yb", AOBF[:, :])

    stg_free[0] = max(stg_free[0], lu_cast_done[0])
    stg_free[1] = max(stg_free[1], lu_cast_done[0])
    ow_ready = []
    for k in range(NC):
        gv = stage_load(dr["owT2"][k * 128:(k + 1) * 128, :], 1024,
                        XBBF[:, k * 1024:(k + 1) * 1024],
                        pre_gwait=(PT, h_dead_pt[0]))
        ow_ready.append(gv)
    bld.wait("t", VT, b_last)
    proj_to_res(0, AOBF, PB2, pb2_free, ow_ready)
    tap_dump("x2", RES[:, :])

    if upto == "B":
        bctx.close(); atc.close(); aoctx.close(); cglobal.close(); abctx.close()
        _finish(bld)
        return nc, sorted(tap_out.keys())

    bctx.close()
    atc.close()

    cps = ExitStack()
    PRT = [bld.ps("PRT%d" % i, [128, 512], F32, stack=cps) for i in range(NTG)]
    PB3 = [bld.ps("PB3a", [128, 512], F32, stack=cps),
           bld.ps("PB3b", [128, 512], F32, stack=cps)]
    pb3_free = [[(PT, 0)], [(PT, 0)]]

    rctx = {"psum": PRT, "mark": [0] * NTG}
    h2_done = feature_ln(True, G2, (XBBF, 0), False, f32_sink=True,
                         ps_pair=PB3, ps_free=pb3_free, router_ctx=rctx)
    tap_dump("h2", XBBF[:, :])
    lgt_war = [(PT, 0)]
    for tg in range(NTG):
        bld.wait("t", PT, rctx["mark"][tg])
        bv = bld.mm(PRT[tg][0:7, 0:512], RBIAS[:, :], ONER_F[0:1, 0:512],
                    start=False, stop=True, inc=PT)
        sem, val = lgt_war[0]
        bld.wait("s", sem, val)
        bld.wait("s", PT, bv)
        lv = bld.op("s", "copy", LGT[0:7, :], PRT[tg][0:7, 0:512], inc=ST)
        for j in range(4):
            tb = tg * 4 + j
            pi = tb % 2
            sem, val = pb3_free[pi][0]
            bld.wait("t", sem, val)
            bld.wait("t", ST, lv)
            tv = bld.op("t", "transpose", PB3[pi][0:128, 0:7],
                        LGT[0:7, j * 128:(j + 1) * 128], IDF[0:7, 0:7], inc=PT)
            bld.wait("s", PT, tv)
            sg = bld.op("s", "activation", PRBp[:, tb * 8: tb * 8 + 7],
                        PB3[pi][0:128, 0:7], AF.Sigmoid, inc=ST)
            pb3_free[pi][0] = (ST, sg)
            lgt_war[0] = (PT, tv)
    prb_done = bld.cnt[ST]
    bld.wait("v", ST, prb_done)
    for tb in range(NTT):
        P7 = PRBp[:, tb * 8: tb * 8 + 7]
        W7 = WGT[:, tb * 8: tb * 8 + 7]
        v1 = bld.op("v", "reduce_max", TOPS[:, 0:1], P7, axis=AX.X, inc=VT)
        bld.wait("v", VT, v1)
        v2 = bld.op("v", "tensor_single_scalar", W7, P7, TOPS[:, 0:1],
                    op=ALU.is_ge, inc=VT)
        bld.wait("v", VT, v2)
        v3 = bld.op("v", "scalar_tensor_tensor", W7, W7, -1e9, P7,
                    op0=ALU.mult, op1=ALU.add, inc=VT)
        bld.wait("v", VT, v3)
        v4 = bld.op("v", "reduce_max", TOPS[:, 1:2], W7, axis=AX.X, inc=VT)
        bld.wait("v", VT, v4)
        v5 = bld.op("v", "tensor_add", TOPS[:, 2:3], TOPS[:, 0:1], TOPS[:, 1:2], inc=VT)
        bld.wait("v", VT, v5)
        v6 = bld.op("v", "reciprocal", TOPS[:, 3:4], TOPS[:, 2:3], inc=VT)
        bld.wait("v", VT, v6)
        v7 = bld.op("v", "tensor_single_scalar", W7, P7, TOPS[:, 1:2],
                    op=ALU.is_ge, inc=VT)
        bld.wait("v", VT, v7)
        v8 = bld.op("v", "tensor_mul", W7, W7, P7, inc=VT)
        bld.wait("v", VT, v8)
        bld.op("v", "tensor_scalar_mul", W7, W7, TOPS[:, 3:4], inc=VT)
    wgt_done = bld.cnt[VT]
    tap_dump("wgt", WGT[:, :])
    tap_dump("prb", PRBp[:, :])

    def sigma(p):
        tb, k = p // 8, p % 8
        return ((tb % 2) * 8 + k) * 8 + tb // 2

    bld.wait("t", VT, bld.cnt[VT])
    seen = set()
    tp_ping = [0]
    for p0 in range(128):
        if p0 in seen:
            continue
        cyc = [p0]
        seen.add(p0)
        qq_ = sigma(p0)
        while qq_ != p0:
            cyc.append(qq_)
            seen.add(qq_)
            qq_ = sigma(qq_)
        held = None
        for i, p in enumerate(cyc):
            pi = tp_ping[0] % 2
            tp_ping[0] += 1
            sem, val = pb3_free[pi][0]
            bld.wait("t", sem, val)
            tb, k = p // 8, p % 8
            tv = bld.op("t", "transpose", PB3[pi][:, 0:128],
                        RES[:, k * T + tb * 128: k * T + (tb + 1) * 128],
                        IDF[:, :], inc=PT)
            if held is not None:
                hpi, hv = held
                bld.wait("s", PT, tv)
                ev = bld.op("s", "activation",
                            RES[:, k * T + tb * 128: k * T + (tb + 1) * 128],
                            PB3[hpi][:, 0:128], AF.Copy, bias=0.0, scale=2.0, inc=ST)
                pb3_free[hpi][0] = (ST, ev)
            held = (pi, tv)
        tb, k = cyc[0] // 8, cyc[0] % 8
        hpi, hv = held
        bld.wait("s", PT, hv)
        ev = bld.op("s", "activation",
                    RES[:, k * T + tb * 128: k * T + (tb + 1) * 128],
                    PB3[hpi][:, 0:128], AF.Copy, bias=0.0, scale=2.0, inc=ST)
        pb3_free[hpi][0] = (ST, ev)
    x2t_done = bld.cnt[ST]
    X2TOK = RES

    p2ctx = ExitStack()
    OW2B = bld.sb("OW2B", [128, NC * 1024], BF16, stack=p2ctx)
    ow2b_ready = []
    for k in range(NC):
        gv = stage_load(dr["owT2"][k * 128:(k + 1) * 128, :], 1024,
                        OW2B[:, k * 1024:(k + 1) * 1024],
                        pre_gwait=(ST, x2t_done) if k == 0 else None)
        ow2b_ready.append(gv)
    bld.wait("t", ST, x2t_done)
    for tb in range(NTT):
        for no in range(2):
            pi = (tb * 2 + no) % 2
            sem, val = pb3_free[pi][0]
            bld.wait("t", sem, val)
            for k in range(NC):
                bld.wait("t", GT, ow2b_ready[k])
                pv = bld.mm(PB3[pi][:, :],
                            AOBF[:, k * T + tb * 128: k * T + (tb + 1) * 128],
                            OW2B[:, k * 1024 + no * 512: k * 1024 + (no + 1) * 512],
                            start=(k == 0), stop=(k == NC - 1),
                            inc=PT if k == NC - 1 else None)
            bld.wait("v", PT, pv)
            rv = bld.op("v", "tensor_sub",
                        X2TOK[:, tb * D + no * 512: tb * D + (no + 1) * 512],
                        X2TOK[:, tb * D + no * 512: tb * D + (no + 1) * 512],
                        PB3[pi][:, :], inc=VT)
            pb3_free[pi][0] = (VT, rv)
    x2t_done = bld.cnt[VT]
    p2ctx.close()
    cps.close()
    aoctx.close()

    cmain = ExitStack()
    WS = [bld.sb("WS%d" % i, [128, 4 * 1024], BF16, stack=cmain) for i in range(5)]
    ZB = bld.sb("ZB", [128, NC * T], BF16, stack=cmain)
    PG = [bld.ps("PGa", [128, 512], F32, stack=cmain),
          bld.ps("PGb", [128, 512], F32, stack=cmain)]
    PU = [bld.ps("PUa", [128, 512], F32, stack=cmain),
          bld.ps("PUb", [128, 512], F32, stack=cmain)]
    PD = [bld.ps("PDa", [128, 512], F32, stack=cmain),
          bld.ps("PDb", [128, 512], F32, stack=cmain)]
    pg_free = [[(PT, 0)], [(PT, 0)]]
    pu_free = [[(PT, 0)], [(PT, 0)]]
    pd_free = [[(PT, 0)], [(PT, 0)]]
    ws_free = [[(PT, 0)] for _ in range(5)]
    ws_first = [True] * 5
    slot_rr = [0]
    sgr2_war = [[(PT, 0)]]

    def load_piece(e, half, kind, sub2):
        s = slot_rr[0] % 5
        slot_rr[0] += 1
        sem, val = ws_free[s][0]
        gv = None
        for sub in range(4):
            if kind in ("g", "u"):
                dram3 = dr["gatesT"] if kind == "g" else dr["upsT"]
                src = dram3[e, sub * 256:(sub + 1) * 256,
                            half * 1024 + sub2 * 512: half * 1024 + (sub2 + 1) * 512] \
                    .rearrange("(k p) f -> p k f", p=128)
                dst = WS[s][:, sub * 1024:(sub + 1) * 1024] \
                    .rearrange("p (k f) -> p k f", k=2)
            else:
                src = dr["downsT"][e, half * 1024 + sub2 * 512 + sub * 128:
                                   half * 1024 + sub2 * 512 + (sub + 1) * 128, :]
                dst = WS[s][:, sub * 1024:(sub + 1) * 1024]
            pre = (sem, val) if sub == 0 else None
            if ws_first[s] and sub == 0:
                bld.wait("g", VT, x2t_done)
            gv = stage_load(src, 1024, dst, pre_gwait=pre)
        ws_first[s] = False
        return s, gv

    def expert(e):
        for half in range(2):
            sg1, gv_g1 = load_piece(e, half, "g", 0)
            sg2, gv_g2 = load_piece(e, half, "g", 1)
            su1, gv_u1 = load_piece(e, half, "u", 0)
            su2, gv_u2 = load_piece(e, half, "u", 1)
            sd1 = sd2 = gv_d1 = gv_d2 = None
            bld.wait("t", VT, h2_done)
            if e == 0 and half == 0:
                bld.wait("v", VT, x2t_done)
            z_last = None
            for ic in range(NC):
                gsl = sg1 if ic < 4 else sg2
                usl = su1 if ic < 4 else su2
                icl = ic % 4
                if ic == 0:
                    bld.wait("t", GT, gv_g1)
                    bld.wait("t", GT, gv_u1)
                    sd1, gv_d1 = load_piece(e, half, "d", 0)
                if ic == 4:
                    bld.wait("t", GT, gv_g2)
                    bld.wait("t", GT, gv_u2)
                for tg in range(NTG):
                    gi = (ic * NTG + tg) % 2
                    sem, val = pg_free[gi][0]
                    bld.wait("t", sem, val)
                    sem, val = pu_free[gi][0]
                    bld.wait("t", sem, val)
                    for hc in range(NC):
                        bld.mm(PG[gi][:, :],
                               WS[gsl][:, hc * 512 + icl * 128: hc * 512 + (icl + 1) * 128],
                               XBBF[:, hc * T + tg * 512: hc * T + tg * 512 + 512],
                               start=(hc == 0), stop=(hc == NC - 1))
                    for hc in range(NC):
                        pv = bld.mm(PU[gi][:, :],
                                    WS[usl][:, hc * 512 + icl * 128: hc * 512 + (icl + 1) * 128],
                                    XBBF[:, hc * T + tg * 512: hc * T + tg * 512 + 512],
                                    start=(hc == 0), stop=(hc == NC - 1),
                                    inc=PT if hc == NC - 1 else None)
                    sem, val = sgr2_war[0][0]
                    bld.wait("s", sem, val)
                    bld.wait("s", PT, pv)
                    sl = bld.op("s", "activation", SGR[:, 0:512],
                                PG[gi][:, :], AF.Silu, inc=ST)
                    pg_free[gi][0] = (ST, sl)
                    bld.wait("v", ST, sl)
                    zv = bld.op("v", "tensor_mul",
                                ZB[:, ic * T + tg * 512: ic * T + tg * 512 + 512],
                                SGR[:, 0:512], PU[gi][:, :], inc=VT)
                    sgr2_war[0][0] = (VT, zv)
                    pu_free[gi][0] = (VT, zv)
                    z_last = zv
                if ic == 3:
                    ws_free[sg1][0] = (PT, bld.cnt[PT])
                    ws_free[su1][0] = (PT, bld.cnt[PT])
                    sd2, gv_d2 = load_piece(e, half, "d", 1)
            ws_free[sg2][0] = (PT, bld.cnt[PT])
            ws_free[su2][0] = (PT, bld.cnt[PT])
            bld.wait("t", GT, gv_d1)
            bld.wait("t", GT, gv_d2)
            bld.wait("t", VT, z_last)
            for tb in range(NTT):
                for hhh in range(2):
                    di = (tb * 2 + hhh) % 2
                    sem, val = pd_free[di][0]
                    bld.wait("t", sem, val)
                    for ic in range(NC):
                        dsl = sd1 if ic < 4 else sd2
                        icl = ic % 4
                        pv = bld.mm(PD[di][:, :],
                                    ZB[:, ic * T + tb * 128: ic * T + (tb + 1) * 128],
                                    WS[dsl][:, icl * 1024 + hhh * 512:
                                            icl * 1024 + (hhh + 1) * 512],
                                    start=(ic == 0), stop=(ic == NC - 1),
                                    inc=PT if ic == NC - 1 else None)
                    bld.wait("v", PT, pv)
                    dst = X2TOK[:, tb * D + hhh * 512: tb * D + (hhh + 1) * 512]
                    scal = ONEC_F[:, 0:1] if e == 0 else WGT[:, tb * 8 + e - 1: tb * 8 + e]
                    av = bld.op("v", "scalar_tensor_tensor", dst, PD[di][:, :], scal, dst,
                                op0=ALU.mult, op1=ALU.add, inc=VT)
                    pd_free[di][0] = (VT, av)
            ws_free[sd1][0] = (PT, bld.cnt[PT])
            ws_free[sd2][0] = (PT, bld.cnt[PT])

    bld.wait("v", VT, wgt_done)
    for e in range(NEXP):
        expert(e)
    tap_dump("x3", X2TOK[:, :])
    cmain.close()

    fctx = ExitStack()
    OBUF = bld.sb("OBUF", [128, 2 * D], F32, stack=fctx)
    FSQ = bld.sb("FSQ", [128, D], F32, stack=fctx)
    FST = bld.sb("FST", [128, 8], F32, stack=fctx)
    moe_done = bld.cnt[VT]
    bld.wait("s", VT, moe_done)
    obuf_war = [[(LO, 0)], [(LO, 0)]]
    for tb in range(NTT):
        xt = X2TOK[:, tb * D:(tb + 1) * D]
        v1 = bld.op("v", "reduce_sum", FST[:, 0:1], xt, axis=AX.X, inc=VT)
        s1 = bld.op("s", "activation", FSQ[:, :], xt, AF.Square,
                    accum_out=FST[:, 1:2], inc=ST)
        bld.wait("v", VT, v1)
        bld.wait("v", ST, s1)
        v2 = bld.op("v", "tensor_scalar_mul", FST[:, 2:3], FST[:, 0:1], 1.0 / D, inc=VT)
        bld.wait("v", VT, v2)
        v3 = bld.op("v", "tensor_mul", FST[:, 3:4], FST[:, 2:3], FST[:, 2:3], inc=VT)
        bld.wait("v", VT, v3)
        v4 = bld.op("v", "scalar_tensor_tensor", FST[:, 4:5], FST[:, 1:2], 1.0 / D,
                    FST[:, 3:4], op0=ALU.mult, op1=ALU.subtract, inc=VT)
        bld.wait("v", VT, v4)
        v5 = bld.op("v", "tensor_scalar_add", FST[:, 4:5], FST[:, 4:5], EPS, inc=VT)
        bld.wait("s", VT, v5)
        s2 = bld.op("s", "activation", FST[:, 5:6], FST[:, 4:5], AF.Ln, inc=ST)
        bld.wait("s", ST, s2)
        s3 = bld.op("s", "activation", FST[:, 5:6], FST[:, 5:6], AF.Exp,
                    scale=-0.5, inc=ST)
        bld.wait("v", ST, s3)
        v6 = bld.op("v", "scalar_tensor_tensor", FST[:, 6:7], FST[:, 2:3], -1.0,
                    FST[:, 5:6], op0=ALU.mult, op1=ALU.mult, inc=VT)
        bld.wait("v", VT, v6)
        ob = tb % 2
        sem, val = obuf_war[ob][0]
        bld.wait("v", sem, val)
        v7 = bld.op("v", "tensor_scalar", OBUF[:, ob * D:(ob + 1) * D], xt,
                    FST[:, 5:6], FST[:, 6:7], op0=ALU.mult, op1=ALU.add, inc=VT)
        bld.wait("s", VT, v7)
        lo = bld.op("s", "dma_start", inc=LO, incby=16,
                    out=out[tb * 128:(tb + 1) * 128, :],
                    in_=OBUF[:, ob * D:(ob + 1) * D])
        obuf_war[ob][0] = (LO, lo)

    fctx.close(); cglobal.close(); abctx.close()
    _finish(bld)
    return nc, sorted(tap_out.keys())




def _finish(bld):
    bld.wait("y", bld.LO, bld.cnt[bld.LO])
    bld.wait("y", bld.LG, bld.cnt[bld.LG])
    _emit_block(bld)
    bld.ctx.close()


def kernel(**inputs):
    key = (BUILD_UPTO, tuple(sorted(TAPS)))
    if key not in _CACHE:
        _CACHE[key] = build_program(upto=BUILD_UPTO, taps=TAPS)
    nc, tap_names = _CACHE[key]
    w = _host_prep(inputs)
    c = _host_constants()
    x = np.asarray(inputs["L0_attn_x"], dtype=np.float32)
    base = {}
    base.update(w)
    base.update(c)
    cores = CORES if CORES is not None else list(range(8))
    in_maps = []
    for core in cores:
        m = dict(base)
        m["x"] = np.ascontiguousarray(x[core].T)
        in_maps.append(m)
    res = run_bass_kernel_spmd(nc, in_maps, core_ids=cores)
    outs = np.stack([np.asarray(res.results[i]["out"]) for i in range(len(cores))],
                    axis=0)
    if tap_names:
        kernel.last_taps = [
            {t: np.asarray(res.results[i]["tap_" + t]) for t in tap_names}
            for i in range(len(cores))]
    if len(cores) < 8:
        full = np.zeros((8, T, D), np.float32)
        full[np.array(cores)] = outs
        return full
    return outs


# version 22
# speedup vs baseline: 1.0429x; 1.0115x over previous
import numpy as np
import ml_dtypes
from contextlib import ExitStack

import concourse.bass as bass
import concourse.mybir as mybir
from concourse.bass_utils import run_bass_kernel_spmd

F32 = mybir.dt.float32
BF16 = mybir.dt.bfloat16
AF = mybir.ActivationFunctionType
ALU = mybir.AluOpType
AX = mybir.AxisListType

T = 2048
D = 1024
NC = 8
LAT = 256
RD = 32
NEXP = 8
NTT = 16
NTG = 4
BIGNEG = -1e30
EPS = 1e-5

BUILD_UPTO = "C"
TAPS = ()
CORES = None

_CACHE = {}


class B:
    def __init__(self, nc):
        self.nc = nc
        self.q = {k: [] for k in "tvsgy"}
        self.cnt = {}
        self.ctx = ExitStack()

    def sem(self, name):
        h = self.ctx.enter_context(self.nc.semaphore(name))
        self.cnt[h] = 0
        return h

    def sb(self, name, shape, dt, stack=None):
        return (stack or self.ctx).enter_context(self.nc.sbuf_tensor(name, shape, dt))

    def ps(self, name, shape, dt=F32, stack=None):
        return (stack or self.ctx).enter_context(self.nc.psum_tensor(name, shape, dt))

    def bump(self, sem, by=1):
        self.cnt[sem] += by
        return self.cnt[sem]

    def emit(self, eng, fn):
        self.q[eng].append(fn)

    def wait(self, eng, sem, val):
        if val and val > 0:
            self.emit(eng, lambda e, s=sem, v=val: e.wait_ge(s, v))

    def op(self, eng, meth, *args, inc=None, incby=1, **kw):
        v = self.bump(inc, incby) if inc is not None else None
        def f(e, m=meth, a=args, k=kw, i=inc, ib=incby):
            r = getattr(e, m)(*a, **k)
            if i is not None:
                r.then_inc(i, ib)
        self.emit(eng, f)
        return v

    def mm(self, out, lhsT, rhs, start, stop, inc=None):
        return self.op("t", "matmul", out, lhsT, rhs, start=start, stop=stop, inc=inc)


def _emit_block(bld):
    with bld.nc.Block() as block:
        @block.tensor
        def _(e):
            for f in bld.q["t"]:
                f(e)
        @block.vector
        def _(e):
            for f in bld.q["v"]:
                f(e)
        @block.scalar
        def _(e):
            for f in bld.q["s"]:
                f(e)
        @block.gpsimd
        def _(e):
            for f in bld.q["g"]:
                f(e)
        @block.sync
        def _(e):
            for f in bld.q["y"]:
                f(e)


def _host_constants():
    c = {}
    p = np.arange(128)[:, None]
    j = np.arange(128)[None, :]
    mA = np.where(p >= j, 0.0, BIGNEG).astype(np.float32)
    mB = np.where(p <= j, 0.0, BIGNEG).astype(np.float32)
    c["mask_band"] = np.concatenate([mA, mB, mA, mB], axis=1).astype(ml_dtypes.bfloat16)
    c["mask_inner"] = np.concatenate(
        [np.full((128, 384), BIGNEG, np.float32), mB,
         np.zeros((128, 512), np.float32)], axis=1).astype(ml_dtypes.bfloat16)
    ms = np.full((32, 16, 128), BIGNEG, np.float32)
    qv = np.arange(128)
    for qb in range(16):
        for jj in range(32):
            ms[jj, qb, :] = np.where((64 * jj) < (128 * qb + qv - 128), 0.0, BIGNEG)
    c["mask_str"] = ms.reshape(32, 16 * 128).astype(ml_dtypes.bfloat16)
    c["mask_allneg"] = np.full((128, 256), BIGNEG, np.float32).astype(ml_dtypes.bfloat16)
    c["iden_bf"] = np.eye(128, dtype=ml_dtypes.bfloat16)
    c["iden_f"] = np.eye(128, dtype=np.float32)
    c["ones_row_bf"] = np.ones((33, 128), dtype=ml_dtypes.bfloat16)
    c["ones_blk_bf"] = np.ones((65, 128), dtype=ml_dtypes.bfloat16)
    s2m = np.zeros((128, 2), np.float32)
    s2m[0:64, 0] = 1.0
    s2m[64:128, 1] = 1.0
    c["sel2"] = s2m.astype(ml_dtypes.bfloat16)
    c["ones_col_bf"] = np.ones((128, 1), dtype=ml_dtypes.bfloat16)
    c["ones_row_f"] = np.ones((65, 512), dtype=np.float32)
    c["ones_col_f"] = np.ones((128, 1), dtype=np.float32)
    sel = np.zeros((128, NTT, 32), np.float32)
    for tile in range(NTT):
        for i in range(2):
            sel[i * 64, tile, tile * 2 + i] = 1.0
    c["vsel"] = sel.reshape(128, NTT * 32).astype(ml_dtypes.bfloat16)
    inv = 1.0 / (10000.0 ** (np.arange(0, RD, 2, dtype=np.float64) / RD))
    tpos = np.arange(T, dtype=np.float64)
    fr = tpos[:, None] * inv[None, :]
    emb = np.concatenate([fr, fr], axis=-1)
    cos = np.cos(emb).astype(np.float32)
    sin = np.sin(emb).astype(np.float32)
    c["cosE"] = np.ascontiguousarray(np.concatenate([cos, cos], axis=1))
    c["sinE"] = np.ascontiguousarray(np.concatenate([sin, sin], axis=1))
    return c


def _host_prep(inputs):
    t = lambda a: np.ascontiguousarray(np.asarray(a, dtype=np.float32).T)
    w = {}
    wqkv = np.asarray(inputs["L0_attn_w_qkv"], dtype=np.float32)
    w["wqT"] = t(wqkv[0:D])
    w["wkT"] = t(wqkv[D:2 * D])
    w["wvT"] = t(wqkv[2 * D:3 * D])
    w["woT"] = t(inputs["L0_attn_w_out"])
    w["g1"] = np.ascontiguousarray(
        np.asarray(inputs["L0_ffn_ln1_w"], dtype=np.float32).reshape(NC, 128).T)
    w["g2"] = np.ascontiguousarray(
        np.asarray(inputs["L0_ffn_ln2_w"], dtype=np.float32).reshape(NC, 128).T)
    w["kvdT"] = t(inputs["L0_ffn_kv_d"])
    w["qdT"] = t(inputs["L0_ffn_q_d"])
    w["kuT"] = t(inputs["L0_ffn_k_u"])
    w["quT"] = t(inputs["L0_ffn_q_u"])
    w["vuT"] = t(inputs["L0_ffn_v_u"])
    w["rqT"] = t(inputs["L0_ffn_rope_q_w"])
    w["rkT"] = t(inputs["L0_ffn_rope_k_w"])
    w["owT2"] = t(inputs["L0_ffn_o_w"])
    w["rwT"] = t(inputs["L0_ffn_router_w"])
    w["rbias"] = np.asarray(inputs["L0_ffn_routing_bias"], dtype=np.float32).reshape(1, 7)
    sg = np.asarray(inputs["L0_ffn_shared_gate"], dtype=np.float32)
    su = np.asarray(inputs["L0_ffn_shared_up"], dtype=np.float32)
    sd = np.asarray(inputs["L0_ffn_shared_down"], dtype=np.float32)
    rg = np.asarray(inputs["L0_ffn_routed_gate"], dtype=np.float32)
    ru = np.asarray(inputs["L0_ffn_routed_up"], dtype=np.float32)
    rd = np.asarray(inputs["L0_ffn_routed_down"], dtype=np.float32)
    w["gatesT"] = np.ascontiguousarray(np.concatenate([sg, rg], 0).transpose(0, 2, 1))
    w["upsT"] = np.ascontiguousarray(np.concatenate([su, ru], 0).transpose(0, 2, 1))
    w["downsT"] = np.ascontiguousarray(np.concatenate([sd, rd], 0).transpose(0, 2, 1))
    return w


def build_program(upto="C", taps=()):
    nc = bass.Bass("TRN2", target_bir_lowering=False)
    bld = B(nc)
    taps = set(taps)

    dr = {}
    def din(name, shape, dt=F32):
        dr[name] = nc.dram_tensor(name, list(shape), dt, kind="ExternalInput").ap()
    din("x", [D, T])
    din("wqT", [D, D]); din("wkT", [D, D]); din("wvT", [D, D]); din("woT", [D, D])
    din("g1", [128, NC]); din("g2", [128, NC])
    din("kvdT", [D, LAT]); din("qdT", [D, LAT])
    din("kuT", [LAT, D]); din("quT", [LAT, D]); din("vuT", [LAT, D])
    din("rqT", [LAT, 512]); din("rkT", [D, 512])
    din("owT2", [D, D])
    din("rwT", [D, 7]); din("rbias", [1, 7])
    din("gatesT", [NEXP, D, 2048]); din("upsT", [NEXP, D, 2048])
    din("downsT", [NEXP, 2048, D])
    din("cosE", [T, 64]); din("sinE", [T, 64])
    din("mask_band", [128, 512], BF16)
    din("mask_inner", [128, 1024], BF16)
    din("mask_str", [32, 16 * 128], BF16)
    din("mask_allneg", [128, 256], BF16)
    din("iden_bf", [128, 128], BF16); din("iden_f", [128, 128], F32)
    din("ones_row_bf", [33, 128], BF16); din("ones_col_bf", [128, 1], BF16)
    din("ones_blk_bf", [65, 128], BF16)
    din("sel2", [128, 2], BF16)
    din("ones_row_f", [65, 512], F32); din("ones_col_f", [128, 1], F32)
    din("vsel", [128, NTT * 32], BF16)

    out = nc.dram_tensor("out", [T, D], F32, kind="ExternalOutput").ap()
    tap_out = {}
    def mktap(name, shape, dt=F32):
        if name in taps:
            tap_out[name] = nc.dram_tensor("tap_" + name, list(shape), dt,
                                           kind="ExternalOutput").ap()
    mktap("x1", [128, NC * T]); mktap("h", [128, NC * T], BF16)
    mktap("x2", [128, NC * T]); mktap("h2", [128, NC * T], BF16)
    mktap("wgt", [128, NTT * 8]); mktap("x3", [128, NTT * D])
    mktap("prb", [128, NTT * 8])
    mktap("ao", [128, NC * T], BF16); mktap("yb", [128, NC * T], BF16)
    mktap("yraw", [64, T]); mktap("sums", [1, T])
    mktap("sumsb", [1, T]); mktap("mbb", [128, 2])
    mktap("yraw2", [64, T]); mktap("sums2", [1, T])
    for _c in range(NC):
        mktap("kc%d" % _c, [128, T], BF16)
        mktap("mb%d" % _c, [128, 2])
    mktap("rtb3", [128, 512], BF16)
    mktap("wrk3", [128, 512], BF16)
    mktap("sumsb2", [1, T]); mktap("mbb2", [128, 2])
    mktap("q0", [128, T], BF16); mktap("k0", [128, T], BF16)
    mktap("v0", [128, NTT * 130], BF16)
    mktap("lat", [128, 2 * T], BF16)

    LD = bld.LD = bld.sem("LD")
    LG = bld.LG = bld.sem("LG")
    LO = bld.LO = bld.sem("LO")
    PT = bld.PT = bld.sem("PT")
    VT = bld.VT = bld.sem("VT")
    ST = bld.ST = bld.sem("ST")
    GT = bld.GT = bld.sem("GT")
    LU = bld.LU = bld.sem("LU")
    CSLD = bld.CSLD = bld.sem("CSLD")
    XLD = bld.XLD = bld.sem("XLD")

    XBBF = bld.sb("XBBF", [128, NC * T], BF16)
    STG = bld.sb("STG", [128, 2 * 1024], F32)
    IDF = bld.sb("IDF", [128, 128], F32)
    ONER_BF = bld.sb("ONER_BF", [33, 128], BF16)
    ONEC_BF = bld.sb("ONEC_BF", [128, 1], BF16)
    ONER_F = bld.sb("ONER_F", [65, 512], F32)
    ONEC_F = bld.sb("ONEC_F", [128, 1], F32)
    G1 = bld.sb("G1", [128, NC], F32)
    G2 = bld.sb("G2", [128, NC], F32)
    STA = bld.sb("STA", [128, 512], F32)
    STB = bld.sb("STB", [128, 512], F32)
    STC = bld.sb("STC", [128, 512], F32)
    BCR = bld.sb("BCR", [1, 512], F32)
    LNSB = bld.sb("LNSB", [128, 512], BF16)
    LNSQ = bld.sb("LNSQ", [128, 512], BF16)
    RWT = bld.sb("RWT", [128, NC * 7], F32)
    RBIAS = bld.sb("RBIAS", [1, 7], F32)
    SGR = bld.sb("SGR", [128, 512], F32)

    abctx = ExitStack()
    RES = bld.sb("RES", [128, NC * T], F32, stack=abctx)
    cglobal = ExitStack()
    LGT = bld.sb("LGT", [7, 512], F32, stack=cglobal)
    PRBp = bld.sb("PRBp", [128, NTT * 8], F32, stack=cglobal)
    WGT = bld.sb("WGT", [128, NTT * 8], F32, stack=cglobal)
    TOPS = bld.sb("TOPS", [128, 8], F32, stack=cglobal)
    aoctx = ExitStack()
    AOBF = bld.sb("AOBF", [128, NC * T], BF16, stack=aoctx)
    atc = ExitStack()
    IDB = bld.sb("IDB", [128, 128], BF16, stack=atc)
    PTRB = bld.sb("PTRB", [128, 3 * 512], BF16, stack=atc)
    MBAND2 = bld.sb("MBAND2", [128, 512], BF16, stack=atc)
    MBI = bld.sb("MBI", [128, 1024], BF16, stack=atc)
    YS = bld.sb("YS", [65, T], F32, stack=atc)
    PTSB = bld.sb("PTSB", [33, 2 * 512], BF16, stack=atc)
    MBB = bld.sb("MBB", [128, 2], F32, stack=atc)
    MBS = bld.sb("MBS", [1, 2], F32, stack=atc)
    BND = bld.sb("BND", [2, 16], F32, stack=atc)
    SEL2 = bld.sb("SEL2", [128, 2], BF16, stack=atc)

    war = {
        "negc": [(PT, 0)], "negt": [(LG, 0)], "negm": [(PT, 0)],
        "ptr": [[(PT, 0)] for _ in range(3)],
        "ptr2": [[(PT, 0)], [(PT, 0)]],
        "rrow": [[(PT, 0)], [(PT, 0)]],
        "rtb2": [[(PT, 0)], [(PT, 0)]],
        "lnsb": [(PT, 0)], "lnsq": [(PT, 0)],
        "sgr": [[(PT, 0)], [(PT, 0)]],
        "cs": [[(VT, 0)]],
        "rtb": [[(PT, 0)], [(PT, 0)]],
        "yraw": [(VT, 0)], "sums": [(PT, 0)],
    }

    def warwait(eng, key, slot=None):
        cell = war[key] if slot is None else war[key][slot]
        sem, val = cell[0]
        bld.wait(eng, sem, val)

    def warset(key, sem, val, slot=None):
        cell = war[key] if slot is None else war[key][slot]
        cell[0] = (sem, val)

    stg_free = [0, 0]
    stg_next = [0]

    def emit_bound(qring, kring, ps, ps_free):
        for mi, ring in ((0, qring), (1, kring)):
            for tg in range(NTG):
                warwait("s", "lnsq")
                sq = bld.op("s", "activation", LNSQ[:, :],
                            ring[:, tg * 512:(tg + 1) * 512], AF.Square, inc=ST)
                sem, val = ps_free[0]
                bld.wait("t", sem, val)
                bld.wait("t", ST, sq)
                pv = bld.mm(ps[0:2, 0:512], SEL2[:, :], LNSQ[:, :],
                            start=True, stop=True, inc=PT)
                warset("lnsq", PT, pv)
                bld.wait("v", PT, pv)
                rv = bld.op("v", "reduce_max", BND[0:2, mi * 4 + tg: mi * 4 + tg + 1],
                            ps[0:2, 0:512], axis=AX.X, inc=VT)
                ps_free[0] = (VT, rv)
        bld.wait("v", VT, bld.cnt[VT])
        r1 = bld.op("v", "reduce_max", BND[0:2, 8:9], BND[0:2, 0:4], axis=AX.X, inc=VT)
        bld.wait("v", VT, r1)
        r2 = bld.op("v", "reduce_max", BND[0:2, 9:10], BND[0:2, 4:8], axis=AX.X, inc=VT)
        bld.wait("v", VT, r2)
        r3 = bld.op("v", "tensor_add", BND[0:2, 10:11], BND[0:2, 8:9],
                    BND[0:2, 9:10], inc=VT)
        bld.wait("v", VT, r3)
        r4 = bld.op("v", "tensor_scalar_mul", BND[0:2, 10:11], BND[0:2, 10:11],
                    -0.0625, inc=VT)
        bld.wait("v", VT, r4)
        r5 = bld.op("v", "tensor_scalar_add", BND[0:2, 10:11], BND[0:2, 10:11],
                    36.0, inc=VT)
        bld.wait("v", VT, r5)
        r6 = bld.op("v", "tensor_scalar_min", BND[0:2, 10:11], BND[0:2, 10:11],
                    0.0, inc=VT)
        sem, val = ps_free[0]
        bld.wait("t", sem, val)
        bld.wait("t", VT, r6)
        tv = bld.op("t", "transpose", ps[0:1, 0:2], BND[0:2, 10:11],
                    IDF[0:2, 0:2], inc=PT)
        bld.wait("s", PT, tv)
        me = bld.op("s", "copy", MBS[0:1, 0:2], ps[0:1, 0:2], inc=ST)
        for hh in range(2):
            sem, val = ps_free[0]
            bld.wait("t", sem, val)
            bld.wait("t", ST, me)
            bv = bld.mm(ps[0:128, 0:1], ONER_F[0:1, 0:128],
                        MBS[0:1, hh:hh + 1], start=True, stop=True, inc=PT)
            bld.wait("s", PT, bv)
            mbe = bld.op("s", "copy", MBB[:, hh:hh + 1], ps[0:128, 0:1], inc=ST)
            ps_free[0] = (ST, mbe)

    def stage_load(dram_ap, cols, cast_dst, pre_gwait=None):
        s = stg_next[0] % 2
        stg_next[0] += 1
        bld.wait("y", GT, stg_free[s])
        lv = bld.op("y", "dma_start", inc=LD, incby=16,
                    out=STG[:, s * 1024: s * 1024 + cols], in_=dram_ap)
        if pre_gwait is not None:
            pws = pre_gwait if isinstance(pre_gwait, list) else [pre_gwait]
            for sem, val in pws:
                bld.wait("g", sem, val)
        bld.wait("g", LD, lv)
        gv = bld.op("g", "tensor_copy", cast_dst,
                    STG[:, s * 1024: s * 1024 + cols], inc=GT)
        stg_free[s] = gv
        return gv

    def tap_dump(name, src_ap):
        if name in tap_out:
            for sem in (PT, VT, ST, GT):
                bld.wait("s", sem, bld.cnt[sem])
            bld.op("s", "dma_start", inc=LO, incby=16,
                   out=tap_out[name][:, :], in_=src_ap)

    actx = ExitStack()
    MSTR = bld.sb("MSTR", [32, 16 * 128], BF16, stack=actx)
    VSEL = bld.sb("VSEL", [128, NTT * 32], BF16, stack=actx)
    QRG = bld.sb("QRG", [128, T], BF16, stack=actx)
    KRG = bld.sb("KRG", [128, T], BF16, stack=actx)
    VRG = bld.sb("VRG", [128, NTT * 130], BF16, stack=actx)
    VSB = bld.sb("VSB", [32, 130], BF16, stack=actx)
    WQ = bld.sb("WQ", [128, NC * 128], BF16, stack=actx)
    WK = bld.sb("WK", [128, NC * 128], BF16, stack=actx)
    WV = bld.sb("WV", [128, NC * 128], BF16, stack=actx)
    PA = [bld.ps("PA0", [128, 512], F32, stack=actx),
          bld.ps("PA1", [128, 512], F32, stack=actx)]
    PB = [bld.ps("PB0", [128, 512], F32, stack=actx),
          bld.ps("PB1", [128, 512], F32, stack=actx),
          bld.ps("PB2x", [128, 512], F32, stack=actx)]
    PC = [bld.ps("PC0", [128, 512], F32, stack=actx),
          bld.ps("PC1", [128, 512], F32, stack=actx)]
    PM = bld.ps("PM", [128, 512], F32, stack=actx)

    for dst, src in [
        (MBAND2, dr["mask_band"][:, :]), (MBI, dr["mask_inner"][:, :]), (MSTR, dr["mask_str"][:, :]),
        (IDB, dr["iden_bf"][:, :]),
        (IDF, dr["iden_f"][:, :]), (ONER_BF, dr["ones_row_bf"][:, :]),
        (ONEC_BF, dr["ones_col_bf"][:, :]), (ONER_F, dr["ones_row_f"][:, :]),
        (ONEC_F, dr["ones_col_f"][:, :]), (G1, dr["g1"][:, :]), (G2, dr["g2"][:, :]),
        (SEL2, dr["sel2"][:, :]),
        (VSEL, dr["vsel"][:, :]), (RBIAS, dr["rbias"][:, :]),
    ]:
        bld.op("y", "dma_start", out=dst[:, :], in_=src, inc=LD, incby=16)
    bld.op("y", "dma_start", out=RWT[:, :].rearrange("p (k f) -> p k f", k=NC),
           in_=dr["rwT"][:, :].rearrange("(k p) f -> p k f", p=128), inc=LD, incby=16)
    bld.op("g", "memset", STA[:, :], 0.0)
    bld.op("g", "memset", STB[:, :], 0.0)
    zmem = bld.op("g", "memset", STC[:, :], 0.0, inc=GT)

    xld = bld.op("y", "dma_start", inc=XLD, incby=16,
                 out=RES[:, :].rearrange("p (k f) -> p k f", k=NC),
                 in_=dr["x"][:, :].rearrange("(k p) f -> p k f", p=128))

    pa_free = [[(PT, 0)], [(PT, 0)]]
    pb_free = [[(PT, 0)], [(PT, 0)], [(PT, 0)]]
    pc_free = [[(PT, 0)], [(PT, 0)]]
    pm_free = [(PT, 0)]
    vs_free = [(PT, 0)]

    bld.wait("v", XLD, xld)
    xb_cast = []
    for k in range(NC):
        v = bld.op("v", "tensor_copy", XBBF[:, k * T:(k + 1) * T],
                   RES[:, k * T:(k + 1) * T], inc=VT)
        xb_cast.append(v)

    wbuf_ready = {}
    wbuf_freed = {"q": [0], "k": [0], "v": [0]}
    ring_free = {"q": [(PT, 0)], "k": [(PT, 0)], "v": [(PT, 0)]}
    ring_ready = {}

    def load_wchunk(mat, dram, c):
        dst = {"q": WQ, "k": WK, "v": WV}[mat]
        gv = stage_load(
            dram[:, c * 128:(c + 1) * 128].rearrange("(k p) f -> p k f", p=128),
            1024,
            dst[:, :].rearrange("p (k f) -> p k f", k=NC),
            pre_gwait=(PT, wbuf_freed[mat][0]))
        wbuf_ready[(mat, c)] = gv

    def qkv_chunk(c):
        for mat, wbuf, ring in (("q", WQ, QRG), ("k", WK, KRG)):
            bld.wait("t", GT, wbuf_ready[(mat, c)])
            sem, val = ring_free[mat][0]
            bld.wait("t", sem, val)
            ev = None
            for tg in range(NTG):
                pi = tg % 2
                sem, val = pa_free[pi][0]
                bld.wait("t", sem, val)
                for k in range(NC):
                    bld.wait("t", VT, xb_cast[k])
                    pv = bld.mm(PA[pi][:, :],
                                wbuf[:, k * 128:(k + 1) * 128],
                                XBBF[:, k * T + tg * 512: k * T + tg * 512 + 512],
                                start=(k == 0), stop=(k == NC - 1),
                                inc=PT if k == NC - 1 else None)
                bld.wait("s", PT, pv)
                ev = bld.op("s", "copy", ring[:, tg * 512: tg * 512 + 512],
                            PA[pi][:, :], inc=ST)
                pa_free[pi][0] = (ST, ev)
            ring_ready[mat] = ev
        bld.wait("t", GT, wbuf_ready[("v", c)])
        sem, val = ring_free["v"][0]
        bld.wait("g", sem, val)
        mz = bld.op("g", "memset",
                    VRG[0:128, :].rearrange("p (t c) -> p t c", c=130)[:, :, 64:130:65],
                    1.0, inc=GT)
        ev = None
        for tb in range(NTT):
            pi = tb % 2
            sem, val = pa_free[pi][0]
            bld.wait("t", sem, val)
            for k in range(NC):
                pv = bld.mm(PA[pi][:, 0:128],
                            XBBF[:, k * T + tb * 128: k * T + (tb + 1) * 128],
                            WV[:, k * 128:(k + 1) * 128],
                            start=(k == 0), stop=(k == NC - 1),
                            inc=PT if k == NC - 1 else None)
            bld.wait("s", PT, pv)
            bld.wait("s", GT, mz)
            base = tb * 130
            bld.op("s", "copy", VRG[:, base: base + 64], PA[pi][:, 0:64])
            ev = bld.op("s", "copy", VRG[:, base + 65: base + 129], PA[pi][:, 64:128],
                        inc=ST)
            pa_free[pi][0] = (ST, ev)
        ring_ready["v"] = ev
        bld.wait("t", ST, ev)
        sem, val = vs_free[0]
        bld.wait("t", sem, val)
        sem, val = pm_free[0]
        bld.wait("t", sem, val)
        for tile in range(NTT):
            pv = bld.mm(PM[0:32, 0:130],
                        VSEL[:, tile * 32:(tile + 1) * 32],
                        VRG[:, tile * 130:(tile + 1) * 130],
                        start=(tile == 0), stop=(tile == NTT - 1),
                        inc=PT if tile == NTT - 1 else None)
        bld.wait("s", PT, pv)
        vs_ev = bld.op("s", "copy", VSB[0:32, 0:130], PM[0:32, 0:130], inc=ST)
        pm_free[0] = (ST, vs_ev)
        return vs_ev

    def attn_outer_dual(c, vs_ev):
        bld.wait("t", ST, ring_ready["q"])
        bld.wait("t", ST, ring_ready["k"])
        bld.wait("t", ST, vs_ev)
        units = [(bb, hh) for bb in range(NTG) for hh in (0, 1)]

        def emit_pair(gp):
            u, j2 = gp // 2, gp % 2
            bb, hh = units[u]
            r0 = hh * 64
            pp = 2 * bb + j2
            pi = gp % 3
            sem, val = pb_free[pi][0]
            bld.wait("t", sem, val)
            first = True
            for j in range(2):
                qb = 2 * pp + j
                qs = (qb * 128, (qb + 1) * 128)
                co = j * 256
                if qb > 0:
                    bld.mm(PB[pi][:, co:co + 128],
                           KRG[r0:r0 + 64, (qb - 1) * 128: qb * 128],
                           QRG[r0:r0 + 64, qs[0]:qs[1]], start=first, stop=False)
                    first = False
                bld.mm(PB[pi][:, co + 128:co + 256],
                       KRG[r0:r0 + 64, qs[0]:qs[1]],
                       QRG[r0:r0 + 64, qs[0]:qs[1]], start=first, stop=False)
                first = False
            return bld.mm(PB[pi][:, 0:512], IDB[:, :], MBAND2[:, :],
                          start=False, stop=True, inc=PT)

        def emit_sstr(u):
            bb, hh = units[u]
            r0 = hh * 64
            sem, val = pm_free[0]
            bld.wait("t", sem, val)
            bld.mm(PM[0:32, 0:512],
                   KRG[r0:r0 + 64, 0:T:64],
                   QRG[r0:r0 + 64, bb * 512:(bb + 1) * 512],
                   start=True, stop=False)
            return bld.mm(PM[0:32, 0:512], IDB[0:32, 0:32],
                          MSTR[0:32, bb * 512:(bb + 1) * 512],
                          start=False, stop=True, inc=PT)

        s2pv = {0: emit_pair(0), 1: emit_pair(1)}
        sstrpv = {0: emit_sstr(0)}
        exs = {}
        pend = []
        yv = None

        def flush_norm():
            nonlocal yv
            while pend:
                dbb, dhh, dysc, du, drv, devv = pend.pop(0)
                pa_i = du % 2
                sem, val = pa_free[pa_i][0]
                bld.wait("t", sem, val)
                bld.wait("t", ST, drv)
                bld.wait("t", VT, devv)
                bv = bld.op("t", "matmul", PA[pa_i][0:64, 0:512],
                            ONER_BF[32:33, 0:64],
                            PTSB[32:33, dhh * 512:(dhh + 1) * 512],
                            start=True, stop=True, inc=PT)
                warset("rrow", PT, bv, dhh)
                bld.wait("v", PT, bv)
                yv = bld.op("v", "tensor_mul",
                            AOBF[dhh * 64:dhh * 64 + 64,
                                 c * T + dbb * 512: c * T + (dbb + 1) * 512],
                            YS[0:64, dysc:dysc + 512], PA[pa_i][0:64, 0:512],
                            inc=VT)
                pa_free[pa_i][0] = (VT, yv)

        for u, (bb, hh) in enumerate(units):
            r0 = hh * 64
            for j2 in range(2):
                gp = 2 * u + j2
                if gp + 2 < 16:
                    s2pv[gp + 2] = emit_pair(gp + 2)
                pi = gp % 3
                pts = gp % 3
                warwait("s", "ptr", pts)
                bld.wait("s", PT, s2pv[gp])
                ex = bld.op("s", "activation", PTRB[:, pts * 512:(pts + 1) * 512],
                            PB[pi][:, 0:512], AF.Exp, bias=MBB[:, hh:hh + 1],
                            scale=0.125, inc=ST)
                pb_free[pi][0] = (ST, ex)
                exs[gp] = ex
            flush_norm()
            pts2 = u % 2
            warwait("s", "ptr2", pts2)
            bld.wait("s", PT, sstrpv[u])
            exstr = bld.op("s", "activation",
                           PTSB[0:32, pts2 * 512:(pts2 + 1) * 512],
                           PM[0:32, 0:512], AF.Exp, bias=MBB[0:32, hh:hh + 1],
                           scale=0.125, inc=ST)
            pm_free[0] = (ST, exstr)
            sem, val = pc_free[hh][0]
            bld.wait("t", sem, val)
            first = True
            for j in range(4):
                qb = 4 * bb + j
                gp = 2 * u + j // 2
                pts = gp % 3
                co = (qb % 2) * 256
                if j % 2 == 0:
                    bld.wait("t", ST, exs[gp])
                if qb > 0:
                    bld.mm(PC[hh][0:65, j * 128:(j + 1) * 128],
                           VRG[:, (qb - 1) * 130 + hh * 65:
                               (qb - 1) * 130 + hh * 65 + 65],
                           PTRB[:, pts * 512 + co: pts * 512 + co + 128],
                           start=first, stop=False)
                    first = False
                lastmm = bld.mm(PC[hh][0:65, j * 128:(j + 1) * 128],
                                VRG[:, qb * 130 + hh * 65: qb * 130 + hh * 65 + 65],
                                PTRB[:, pts * 512 + co + 128: pts * 512 + co + 256],
                                start=first, stop=False,
                                inc=PT if j % 2 == 1 else None)
                first = False
                if j % 2 == 1:
                    warset("ptr", PT, lastmm, pts)
            bld.wait("t", ST, exstr)
            av = bld.mm(PC[hh][0:65, 0:512], VSB[0:32, hh * 65: hh * 65 + 65],
                        PTSB[0:32, pts2 * 512:(pts2 + 1) * 512],
                        start=False, stop=True, inc=PT)
            warset("ptr2", PT, av, pts2)
            if u + 1 < len(units):
                sstrpv[u + 1] = emit_sstr(u + 1)
            ysc = ((bb % 2) * 2 + hh) * 512
            bld.wait("v", PT, av)
            evv = bld.op("v", "tensor_copy", YS[:, ysc:ysc + 512],
                         PC[hh][0:65, 0:512], inc=VT)
            pc_free[hh][0] = (VT, evv)
            bld.wait("s", VT, evv)
            l1 = bld.op("s", "activation", YS[64:65, ysc:ysc + 512],
                        YS[64:65, ysc:ysc + 512], AF.Ln, inc=ST)
            bld.wait("s", ST, l1)
            warwait("s", "rrow", hh)
            rv = bld.op("s", "activation", PTSB[32:33, hh * 512:(hh + 1) * 512],
                        YS[64:65, ysc:ysc + 512], AF.Exp,
                        scale=-1.0, inc=ST)
            pend.append((bb, hh, ysc, u, rv, evv))
        flush_norm()
        return yv

    a_last = None
    xb_dead_pt = [0]
    load_wchunk("q", dr["wqT"], 0)
    load_wchunk("k", dr["wkT"], 0)
    load_wchunk("v", dr["wvT"], 0)
    for c in range(NC):
        vs_ev = qkv_chunk(c)
        wbuf_freed["q"][0] = bld.cnt[PT]
        wbuf_freed["k"][0] = bld.cnt[PT]
        wbuf_freed["v"][0] = bld.cnt[PT]
        if c + 1 < NC:
            load_wchunk("q", dr["wqT"], c + 1)
            load_wchunk("k", dr["wkT"], c + 1)
            load_wchunk("v", dr["wvT"], c + 1)
        if c == NC - 1:
            xb_dead_pt[0] = bld.cnt[PT]
        bld.wait("s", ST, ring_ready["q"])
        bld.wait("s", ST, ring_ready["k"])
        emit_bound(QRG, KRG, PM, pm_free)
        a_last = attn_outer_dual(c, vs_ev)
        ring_free["q"][0] = (PT, bld.cnt[PT])
        ring_free["k"][0] = (PT, bld.cnt[PT])
        ring_free["v"][0] = (PT, bld.cnt[PT])
        vs_free[0] = (PT, bld.cnt[PT])
    tap_dump("ao", AOBF[:, :])
    tap_dump("yraw", YS[0:64, :]); tap_dump("sums", YS[64:65, :])
    tap_dump("sumsb", YS[64:65, :]); tap_dump("mbb", MBB[:, :])

    wo_ready = []
    for k in range(NC):
        gv = stage_load(dr["woT"][k * 128:(k + 1) * 128, :], 1024,
                        XBBF[:, 8192 + k * 1024: 8192 + (k + 1) * 1024],
                        pre_gwait=(PT, xb_dead_pt[0]))
        wo_ready.append(gv)

    def proj_to_res(wbase, src_bf, ps_pair, ps_free, wready):
        for oc in range(NC):
            for tg in range(NTG):
                pi = (oc * NTG + tg) % 2
                sem, val = ps_free[pi][0]
                bld.wait("t", sem, val)
                for k in range(NC):
                    bld.wait("t", GT, wready[k])
                    pv = bld.mm(ps_pair[pi][:, :],
                                XBBF[:, wbase + k * 1024 + oc * 128:
                                     wbase + k * 1024 + (oc + 1) * 128],
                                src_bf[:, k * T + tg * 512: k * T + tg * 512 + 512],
                                start=(k == 0), stop=(k == NC - 1),
                                inc=PT if k == NC - 1 else None)
                bld.wait("v", PT, pv)
                rv = bld.op("v", "tensor_add",
                            RES[:, oc * T + tg * 512: oc * T + tg * 512 + 512],
                            RES[:, oc * T + tg * 512: oc * T + tg * 512 + 512],
                            ps_pair[pi][:, :], inc=VT)
                ps_free[pi][0] = (VT, rv)

    bld.wait("t", VT, a_last)
    proj_to_res(8192, AOBF, PA, pa_free, wo_ready)

    def feature_ln(with_mean, gamma, out_bf, inplace, f32_sink=False,
                   ps_pair=None, ps_free=None, router_ctx=None, h_fused=None):
        bld.wait("s", VT, bld.cnt[VT])
        bld.wait("t", VT, bld.cnt[VT])
        for tg in range(NTG):
            for pi in range(2):
                sem, val = ps_free[pi][0]
                bld.wait("t", sem, val)
            pv_sum = None
            pv_sq = None
            for k in range(NC):
                if with_mean:
                    warwait("v", "lnsb")
                    cv = bld.op("v", "tensor_copy", LNSB[:, :],
                                RES[:, k * T + tg * 512: k * T + tg * 512 + 512], inc=VT)
                    bld.wait("t", VT, cv)
                    pv_sum = bld.mm(ps_pair[0][0:1, 0:512], ONEC_BF[:, :], LNSB[:, :],
                                    start=(k == 0), stop=(k == NC - 1), inc=PT)
                    warset("lnsb", PT, pv_sum)
                warwait("s", "lnsq")
                sv = bld.op("s", "activation", LNSQ[:, :],
                            RES[:, k * T + tg * 512: k * T + tg * 512 + 512],
                            AF.Square, inc=ST)
                bld.wait("t", ST, sv)
                pv_sq = bld.mm(ps_pair[1][0:1, 0:512], ONEC_BF[:, :], LNSQ[:, :],
                               start=(k == 0), stop=(k == NC - 1), inc=PT)
                warset("lnsq", PT, pv_sq)
            row = tg * 32
            if with_mean:
                bld.wait("s", PT, pv_sum)
                e1 = bld.op("s", "copy", STA[row:row + 1, :],
                            ps_pair[0][0:1, 0:512], inc=ST)
                ps_free[0][0] = (ST, e1)
            bld.wait("s", PT, pv_sq)
            e2 = bld.op("s", "copy", STB[row:row + 1, :],
                        ps_pair[1][0:1, 0:512], inc=ST)
            ps_free[1][0] = (ST, e2)
        bld.wait("v", ST, bld.cnt[ST])
        bld.wait("v", GT, zmem)
        if with_mean:
            v1 = bld.op("v", "tensor_scalar_mul", STA[:, :], STA[:, :],
                        1.0 / D, inc=VT)
            bld.wait("v", VT, v1)
            v2 = bld.op("v", "tensor_mul", STC[:, :], STA[:, :], STA[:, :], inc=VT)
            bld.wait("v", VT, v2)
            v3 = bld.op("v", "scalar_tensor_tensor", STB[:, :], STB[:, :],
                        1.0 / D, STC[:, :], op0=ALU.mult, op1=ALU.subtract, inc=VT)
        else:
            v3 = bld.op("v", "tensor_scalar_mul", STB[:, :], STB[:, :],
                        1.0 / D, inc=VT)
        bld.wait("v", VT, v3)
        v4 = bld.op("v", "tensor_scalar_add", STB[:, :], STB[:, :], EPS, inc=VT)
        bld.wait("s", VT, v4)
        s1 = bld.op("s", "activation", STB[:, :], STB[:, :], AF.Ln, inc=ST)
        bld.wait("s", ST, s1)
        s2 = bld.op("s", "activation", STB[:, :], STB[:, :], AF.Exp,
                    scale=-0.5, inc=ST)
        v5 = None
        if with_mean:
            bld.wait("v", ST, s2)
            v5 = bld.op("v", "tensor_mul", STC[:, :], STA[:, :], STB[:, :], inc=VT)
        r1s = None
        if h_fused is not None:
            bld.wait("v", VT, v5)
            v6 = bld.op("v", "tensor_mul", STA[:, :], STB[:, :], STB[:, :], inc=VT)
            bld.wait("v", VT, v6)
            v7 = bld.op("v", "tensor_scalar", STA[:, :], STA[:, :],
                        -EPS, 1.0 + EPS, op0=ALU.mult, op1=ALU.add, inc=VT)
            bld.wait("s", VT, v7)
            s3 = bld.op("s", "activation", STA[:, :], STA[:, :], AF.Ln, inc=ST)
            bld.wait("s", ST, s3)
            r1s = bld.op("s", "activation", STA[:, :], STA[:, :], AF.Exp,
                         scale=-0.5, inc=ST)
        bc_war = getattr(feature_ln, "_bc_war", None)
        if bc_war is None:
            bc_war = [(PT, 0)]
            feature_ln._bc_war = bc_war
        for tg in range(NTG):
            row = tg * 32
            sem, val = ps_free[0][0]
            bld.wait("t", sem, val)
            sem, val = bc_war[0]
            bld.wait("s", sem, val)
            bld.wait("s", ST, s2)
            if with_mean:
                bld.wait("s", VT, v5)
            cb1 = bld.op("s", "copy", BCR[:, :], STB[row:row + 1, :], inc=ST)
            bld.wait("t", ST, cb1)
            rbv = bld.mm(ps_pair[0][:, :], ONER_F[0:1, 0:128],
                         BCR[:, :], start=True, stop=True, inc=PT)
            b2 = None
            if with_mean:
                sem, val = ps_free[1][0]
                bld.wait("t", sem, val)
                bld.wait("s", PT, rbv)
                cb2 = bld.op("s", "copy", BCR[:, :], STC[row:row + 1, :], inc=ST)
                bld.wait("t", ST, cb2)
                b2 = bld.mm(ps_pair[1][:, :], ONER_F[0:1, 0:128],
                            BCR[:, :], start=True, stop=True, inc=PT)
            bc_war[0] = (PT, bld.cnt[PT])
            bld.wait("v", PT, bld.cnt[PT])
            last_v = None
            for k in range(NC):
                xa = RES[:, k * T + tg * 512: k * T + tg * 512 + 512]
                if inplace:
                    bld.op("v", "tensor_mul", xa, xa, ps_pair[0][:, :])
                    last_v = bld.op("v", "tensor_sub", xa, xa, ps_pair[1][:, :], inc=VT)
                    continue
                if out_bf is not None:
                    dstt, base = out_bf
                    dst = dstt[:, base + k * T + tg * 512: base + k * T + tg * 512 + 512]
                    if with_mean:
                        sgs = 0
                        warwait("v", "sgr", sgs)
                        sf = SGR[:, 0:512]
                        bld.op("v", "tensor_mul", sf, xa, ps_pair[0][:, :])
                        fv0 = bld.op("v", "tensor_sub", sf, sf, ps_pair[1][:, :], inc=VT)
                        bld.wait("v", VT, fv0)
                        if f32_sink:
                            fv = bld.op("v", "tensor_scalar_mul", sf, sf,
                                        gamma[:, k:k + 1], inc=VT)
                            bld.wait("v", VT, fv)
                            last_v = bld.op("v", "tensor_copy", dst, sf, inc=VT)
                            bld.wait("t", VT, fv)
                            rpv = bld.mm(router_ctx["psum"][tg][0:7, 0:512],
                                         RWT[:, k * 7:(k + 1) * 7], sf,
                                         start=(k == 0), stop=(k == NC - 1), inc=PT)
                            router_ctx["mark"][tg] = rpv
                            warset("sgr", PT, rpv, sgs)
                        else:
                            if gamma is not None:
                                last_v = bld.op("v", "tensor_scalar_mul", dst, sf,
                                                gamma[:, k:k + 1], inc=VT)
                            else:
                                last_v = bld.op("v", "tensor_copy", dst, sf, inc=VT)
                            warset("sgr", VT, last_v, sgs)
                    elif gamma is not None:
                        last_v = bld.op("v", "scalar_tensor_tensor", dst, xa,
                                        gamma[:, k:k + 1], ps_pair[0][:, :],
                                        op0=ALU.mult, op1=ALU.mult, inc=VT)
                    else:
                        last_v = bld.op("v", "tensor_mul", dst, xa,
                                        ps_pair[0][:, :], inc=VT)
            if h_fused is not None:
                dstt, base, gam = h_fused
                sem, val = bc_war[0]
                bld.wait("s", sem, val)
                bld.wait("s", ST, r1s)
                cb3 = bld.op("s", "copy", BCR[:, :], STA[row:row + 1, :], inc=ST)
                bld.wait("t", ST, cb3)
                bld.wait("t", VT, last_v)
                b3 = bld.mm(ps_pair[0][:, :], ONER_F[0:1, 0:128], BCR[:, :],
                            start=True, stop=True, inc=PT)
                bc_war[0] = (PT, bld.cnt[PT])
                bld.wait("v", PT, b3)
                for k in range(NC):
                    xa = RES[:, k * T + tg * 512: k * T + tg * 512 + 512]
                    dst = dstt[:, base + k * T + tg * 512:
                               base + k * T + tg * 512 + 512]
                    last_v = bld.op("v", "scalar_tensor_tensor", dst, xa,
                                    gam[:, k:k + 1], ps_pair[0][:, :],
                                    op0=ALU.mult, op1=ALU.mult, inc=VT)
            ps_free[0][0] = (VT, last_v)
            if b2 is not None:
                ps_free[1][0] = (VT, last_v)
        return bld.cnt[VT]

    h_done = feature_ln(True, None, None, True, ps_pair=PB, ps_free=pb_free,
                        h_fused=(XBBF, 0, G1))
    tap_dump("x1", RES[:, :])
    tap_dump("h", XBBF[:, :])

    if upto == "A":
        actx.close(); atc.close(); aoctx.close(); cglobal.close(); abctx.close()
        _finish(bld)
        return nc, sorted(tap_out.keys())

    actx.close()
    a_end_pt = bld.cnt[PT]
    a_end_vt = bld.cnt[VT]
    bctx = ExitStack()
    KVL = bld.sb("KVL", [128, 2 * T], BF16, stack=bctx)
    QL = bld.sb("QL", [128, 2 * T], BF16, stack=bctx)
    QR2 = bld.sb("QR2", [128, T], BF16, stack=bctx)
    KR2 = bld.sb("KR2", [128, T], BF16, stack=bctx)
    VR2 = bld.sb("VR2", [128, NTT * 130], BF16, stack=bctx)
    WKU = bld.sb("WKU", [128, 256], BF16, stack=bctx)
    WQU = bld.sb("WQU", [128, 256], BF16, stack=bctx)
    WVU = bld.sb("WVU", [128, 256], BF16, stack=bctx)
    WRQ = bld.sb("WRQ", [128, 128], BF16, stack=bctx)
    WRK = bld.sb("WRK", [128, 512], BF16, stack=bctx)
    CS = bld.sb("CS", [128, 512], F32, stack=bctx)
    RSC = SGR
    RTB = bld.sb("RTB", [128, 2 * 256], BF16, stack=bctx)
    PB2 = [bld.ps("PB2%d" % i, [128, 512], F32, stack=bctx) for i in range(5)]
    PCAV = [bld.ps("PCAVa", [128, 512], F32, stack=bctx),
            bld.ps("PCAVb", [128, 512], F32, stack=bctx)]
    PRBF = bld.ps("PRBF", [128, 1024], BF16, stack=bctx)

    pb2_free = [[(PT, 0)] for _ in range(5)]
    pcav_free = [[(PT, 0)], [(PT, 0)]]
    prbf_free = [(PT, 0)]

    WKVD = QR2
    WQD = KR2
    kvd_ready = None
    a_guard = [(PT, a_end_pt), (VT, a_end_vt)]
    for kk in range(2):
        stage_load(dr["kvdT"][kk * 512:(kk + 1) * 512, :]
                   .rearrange("(k p) f -> p k f", p=128), 1024,
                   WKVD[:, kk * 1024:(kk + 1) * 1024]
                   .rearrange("p (k f) -> p k f", k=4), pre_gwait=a_guard)
        kvd_ready = stage_load(dr["qdT"][kk * 512:(kk + 1) * 512, :]
                               .rearrange("(k p) f -> p k f", p=128), 1024,
                               WQD[:, kk * 1024:(kk + 1) * 1024]
                               .rearrange("p (k f) -> p k f", k=4), pre_gwait=a_guard)

    bld.wait("t", VT, h_done)
    bld.wait("t", GT, kvd_ready)
    lat_done = None
    for (buf, wbuf) in ((KVL, WKVD), (QL, WQD)):
        for lc in range(2):
            for tg in range(NTG):
                pi = tg % 2
                sem, val = pb2_free[pi][0]
                bld.wait("t", sem, val)
                for k in range(NC):
                    pv = bld.mm(PB2[pi][:, :],
                                wbuf[:, k * 256 + lc * 128: k * 256 + (lc + 1) * 128],
                                XBBF[:, k * T + tg * 512: k * T + tg * 512 + 512],
                                start=(k == 0), stop=(k == NC - 1),
                                inc=PT if k == NC - 1 else None)
                bld.wait("s", PT, pv)
                lat_done = bld.op("s", "copy",
                                  buf[:, lc * T + tg * 512: lc * T + tg * 512 + 512],
                                  PB2[pi][:, :], inc=ST)
                pb2_free[pi][0] = (ST, lat_done)
    tap_dump("lat", KVL[:, :])

    ring2_free = {"q": [(PT, bld.cnt[PT])], "k": [(PT, bld.cnt[PT])],
                  "v": [(PT, a_end_pt)]}
    war["cs"][0][0] = (VT, a_end_vt)
    war["rtb"][0][0] = (PT, a_end_pt)
    war["rtb"][1][0] = (PT, a_end_pt)
    up_freed = [0, 0]

    lu_cast_done = [0]

    def load_up(c):
        pg = [(PT, max(up_freed[0], a_end_pt)), (VT, a_end_vt)]
        srcs = [
            (dr["kuT"][:, c * 128:(c + 1) * 128]
             .rearrange("(k p) f -> p k f", p=128), 0, 256,
             WKU[:, :].rearrange("p (k f) -> p k f", k=2)),
            (dr["quT"][:, c * 128:(c + 1) * 128]
             .rearrange("(k p) f -> p k f", p=128), 256, 256,
             WQU[:, :].rearrange("p (k f) -> p k f", k=2)),
            (dr["vuT"][:, c * 128:(c + 1) * 128]
             .rearrange("(k p) f -> p k f", p=128), 512, 256,
             WVU[:, :].rearrange("p (k f) -> p k f", k=2)),
            (dr["rqT"][:, c * 64:(c + 1) * 64]
             .rearrange("(k p) f -> p k f", p=128), 768, 128,
             WRQ[:, :].rearrange("p (k f) -> p k f", k=2)),
            (dr["rkT"][:, c * 64:(c + 1) * 64]
             .rearrange("(k p) f -> p k f", p=128), 896, 512,
             WRK[:, :].rearrange("p (k f) -> p k f", k=NC)),
        ]
        bld.wait("y", GT, max(lu_cast_done[0], stg_free[0], stg_free[1]))
        for src_ap, off, cols, _dst in srcs:
            bld.op("y", "dma_start", inc=LU, incby=16,
                   out=STG[:, off:off + cols], in_=src_ap)
        full = bld.cnt[LU]
        out = []
        for _src, off, cols, dst in srcs:
            for sem, val in pg:
                bld.wait("g", sem, val)
            bld.wait("g", LU, full)
            gv = bld.op("g", "tensor_copy", dst, STG[:, off:off + cols], inc=GT)
            out.append(gv)
        lu_cast_done[0] = out[-1]
        return tuple(out)

    def mla_qkv(c, wready):
        def up_group(mat, tg):
            wbuf = WQU if mat == "q" else WKU
            ring = QR2 if mat == "q" else KR2
            src = QL if mat == "q" else KVL
            pi = 2 + tg % 2
            sem, val = pb2_free[pi][0]
            bld.wait("t", sem, val)
            for lc in range(2):
                pv = bld.mm(PB2[pi][:, :],
                            wbuf[:, lc * 128:(lc + 1) * 128],
                            src[:, lc * T + tg * 512: lc * T + tg * 512 + 512],
                            start=(lc == 0), stop=(lc == 1),
                            inc=PT if lc == 1 else None)
            bld.wait("s", PT, pv)
            ev = bld.op("s", "copy", ring[:, tg * 512: tg * 512 + 512],
                        PB2[pi][:, :], inc=ST)
            pb2_free[pi][0] = (ST, ev)

        def vup_tb(tb, mz):
            pi = 2 + tb % 2
            sem, val = pb2_free[pi][0]
            bld.wait("t", sem, val)
            for lc in range(2):
                pv = bld.mm(PB2[pi][:, 0:128],
                            KVL[:, lc * T + tb * 128: lc * T + (tb + 1) * 128],
                            WVU[:, lc * 128:(lc + 1) * 128],
                            start=(lc == 0), stop=(lc == 1),
                            inc=PT if lc == 1 else None)
            bld.wait("s", PT, pv)
            bld.wait("s", GT, mz)
            base = tb * 130
            bld.op("s", "copy", VR2[:, base: base + 64], PB2[pi][:, 0:64])
            ev = bld.op("s", "copy", VR2[:, base + 65: base + 129],
                        PB2[pi][:, 64:128], inc=ST)
            pb2_free[pi][0] = (ST, ev)

        def rope_group(mat, ring, wr, g):
            tb0 = g * 4
            warwait("g", "cs", 0)
            bld.op("g", "dma_start", inc=CSLD, incby=16,
                   out=CS[:, 0:256].rearrange("p (j f) -> p j f", j=4),
                   in_=dr["cosE"][tb0 * 128:(tb0 + 4) * 128, :]
                   .rearrange("(j p) f -> p j f", p=128))
            sv_ = bld.op("g", "dma_start", inc=CSLD, incby=16,
                         out=CS[:, 256:512].rearrange("p (j f) -> p j f", j=4),
                         in_=dr["sinE"][tb0 * 128:(tb0 + 4) * 128, :]
                         .rearrange("(j p) f -> p j f", p=128))
            pi = g % 2
            sem, val = pb2_free[pi][0]
            bld.wait("t", sem, val)
            bld.wait("t", GT, wr)
            for j in range(4):
                tb = tb0 + j
                if mat == "q":
                    for lc in range(2):
                        pv = bld.mm(PB2[pi][:, j * 64:(j + 1) * 64],
                                    QL[:, lc * T + tb * 128: lc * T + (tb + 1) * 128],
                                    WRQ[:, lc * 64:(lc + 1) * 64],
                                    start=(lc == 0), stop=(lc == 1),
                                    inc=PT if (lc == 1 and j == 3) else None)
                else:
                    for k in range(NC):
                        pv = bld.mm(PB2[pi][:, j * 64:(j + 1) * 64],
                                    XBBF[:, k * T + tb * 128: k * T + (tb + 1) * 128],
                                    WRK[:, k * 64:(k + 1) * 64],
                                    start=(k == 0), stop=(k == NC - 1),
                                    inc=PT if (k == NC - 1 and j == 3) else None)
            bld.wait("v", PT, pv)
            bld.wait("v", CSLD, sv_)
            bld.op("v", "tensor_scalar_mul",
                   RSC[0:128, 0:256].rearrange("p (h d) -> p h d", d=32)[:, :, 0:16],
                   PB2[pi][0:128, 0:256].rearrange("p (h d) -> p h d", d=32)[:, :, 16:32],
                   -1.0)
            x2v = bld.op("v", "tensor_scalar_mul",
                         RSC[0:128, 0:256].rearrange("p (h d) -> p h d", d=32)[:, :, 16:32],
                         PB2[pi][0:128, 0:256].rearrange("p (h d) -> p h d", d=32)[:, :, 0:16],
                         1.0, inc=VT)
            bld.op("v", "tensor_mul", RSC[:, 256:512], PB2[pi][:, 0:256],
                   CS[:, 0:256])
            bld.wait("v", VT, x2v)
            x3v = bld.op("v", "tensor_mul", RSC[:, 0:256], RSC[:, 0:256],
                         CS[:, 256:512], inc=VT)
            warset("cs", VT, x3v, 0)
            rs = g % 2
            warwait("v", "rtb2", rs)
            bld.wait("v", VT, x3v)
            x4v = bld.op("v", "tensor_add", RTB[:, rs * 256:rs * 256 + 256],
                         RSC[:, 256:512], RSC[:, 0:256], inc=VT)
            pb2_free[pi][0] = (VT, x4v)
            for jp in range(2):
                tbA = tb0 + 2 * jp
                sem, val = prbf_free[0]
                bld.wait("t", sem, val)
                bld.wait("t", VT, x4v)
                tv = bld.op("t", "transpose", PRBF[0:128, 0:128],
                            RTB[:, rs * 256 + jp * 128: rs * 256 + (jp + 1) * 128],
                            IDB[:, :], inc=PT)
                if jp == 1:
                    warset("rtb2", PT, tv, rs)
                bld.wait("s", PT, tv)
                bld.op("s", "copy", ring[0:32, tbA * 128:(tbA + 1) * 128],
                       PRBF[0:32, 0:128])
                bld.op("s", "copy", ring[64:96, tbA * 128:(tbA + 1) * 128],
                       PRBF[32:64, 0:128])
                bld.op("s", "copy", ring[0:32, (tbA + 1) * 128:(tbA + 2) * 128],
                       PRBF[64:96, 0:128])
                e2 = bld.op("s", "copy",
                            ring[64:96, (tbA + 1) * 128:(tbA + 2) * 128],
                            PRBF[96:128, 0:128], inc=ST)
                prbf_free[0] = (ST, e2)

        bld.wait("t", GT, wready[1])
        sem, val = ring2_free["q"][0]
        bld.wait("t", sem, val)
        bld.wait("t", ST, lat_done)
        for tg in range(NTG):
            up_group("q", tg)
        bld.wait("t", GT, wready[0])
        sem, val = ring2_free["k"][0]
        bld.wait("t", sem, val)
        for g in range(NTG):
            up_group("k", g)
            rope_group("q", QR2, wready[3], g)
        bld.wait("t", GT, wready[2])
        sem, val = ring2_free["v"][0]
        bld.wait("g", sem, val)
        mz = bld.op("g", "memset",
                    VR2[0:128, :].rearrange("p (t c) -> p t c", c=130)[:, :, 64:130:65],
                    1.0, inc=GT)
        for g in range(NTG):
            for tb in range(4 * g, 4 * g + 4):
                vup_tb(tb, mz)
            rope_group("k", KR2, wready[4], g)
        if c == 3 and "rtb3" in tap_out:
            bld.wait("s", VT, bld.cnt[VT])
            bld.op("s", "dma_start", out=tap_out["rtb3"][:, :], in_=RTB[:, :],
                   inc=LO, incby=16)
            bld.op("s", "dma_start", out=tap_out["wrk3"][:, :], in_=WRK[:, :],
                   inc=LO, incby=16)
        emit_bound(QR2, KR2, PB2[0], pb2_free[0])
        return bld.cnt[ST]

    def attn_inner_dual(c, ck):
        bld.wait("t", ST, ck)
        yv = None
        seq = []
        for qg in range(NTG):
            for kt in range(4 * qg + 4):
                for hh in (0, 1):
                    seq.append((qg, kt, hh))
        L = len(seq)

        def emit_s2(i):
            qg, kt, hh = seq[i]
            pi = i % 5
            sem, val = pb2_free[pi][0]
            bld.wait("t", sem, val)
            r0 = hh * 64
            r = kt - 4 * qg
            if r < 0:
                return bld.mm(PB2[pi][:, :],
                              KR2[r0:r0 + 64, kt * 128:(kt + 1) * 128],
                              QR2[r0:r0 + 64, qg * 512:(qg + 1) * 512],
                              start=True, stop=True, inc=PT)
            bld.mm(PB2[pi][:, :],
                   KR2[r0:r0 + 64, kt * 128:(kt + 1) * 128],
                   QR2[r0:r0 + 64, qg * 512:(qg + 1) * 512],
                   start=True, stop=False)
            return bld.mm(PB2[pi][:, :], IDB[:, :],
                          MBI[:, 384 - r * 128: 896 - r * 128],
                          start=False, stop=True, inc=PT)

        DEPTH = 4
        pend = []
        s2pv = {}
        for i in range(min(DEPTH, L)):
            s2pv[i] = emit_s2(i)
        for i, (qg, kt, hh) in enumerate(seq):
            pi = i % 5
            if i + DEPTH < L:
                s2pv[i + DEPTH] = emit_s2(i + DEPTH)
            nkt = 4 * qg + 4
            pts = i % 3
            warwait("s", "ptr", pts)
            bld.wait("s", PT, s2pv[i])
            ex = bld.op("s", "activation", PTRB[:, pts * 512:(pts + 1) * 512],
                        PB2[pi][:, :], AF.Exp, bias=MBB[:, hh:hh + 1],
                        scale=0.125, inc=ST)
            pb2_free[pi][0] = (ST, ex)
            while pend and (i - pend[0][0] >= 2 or i == L - 1):
                _, tqg, thh, tysc, trv, tevv = pend.pop(0)
                pi2 = i % 5
                sem, val = pb2_free[pi2][0]
                bld.wait("t", sem, val)
                bld.wait("t", ST, trv)
                bld.wait("t", VT, tevv)
                bv = bld.op("t", "matmul", PB2[pi2][0:64, 0:512],
                            ONER_BF[32:33, 0:64],
                            PTSB[32:33, thh * 512:(thh + 1) * 512],
                            start=True, stop=True, inc=PT)
                warset("rrow", PT, bv, thh)
                bld.wait("v", PT, bv)
                yv = bld.op("v", "tensor_mul",
                            AOBF[thh * 64:thh * 64 + 64,
                                 c * T + tqg * 512: c * T + (tqg + 1) * 512],
                            YS[0:64, tysc:tysc + 512], PB2[pi2][0:64, 0:512],
                            inc=VT)
                pb2_free[pi2][0] = (VT, yv)
                if i < L - 1:
                    break
            if kt == 0:
                sem, val = pcav_free[hh][0]
                bld.wait("t", sem, val)
            bld.wait("t", ST, ex)
            av = bld.mm(PCAV[hh][0:65, 0:512],
                        VR2[:, kt * 130 + hh * 65: kt * 130 + hh * 65 + 65],
                        PTRB[:, pts * 512:(pts + 1) * 512],
                        start=(kt == 0), stop=(kt == nkt - 1), inc=PT)
            warset("ptr", PT, av, pts)
            if kt == nkt - 1:
                r0 = hh * 64
                ysc = ((qg % 2) * 2 + hh) * 512
                bld.wait("v", PT, av)
                evv = bld.op("v", "tensor_copy", YS[:, ysc:ysc + 512],
                             PCAV[hh][0:65, 0:512], inc=VT)
                pcav_free[hh][0] = (VT, evv)
                bld.wait("s", VT, evv)
                l1 = bld.op("s", "activation", YS[64:65, ysc:ysc + 512],
                            YS[64:65, ysc:ysc + 512], AF.Ln, inc=ST)
                bld.wait("s", ST, l1)
                warwait("s", "rrow", hh)
                rv = bld.op("s", "activation",
                            PTSB[32:33, hh * 512:(hh + 1) * 512],
                            YS[64:65, ysc:ysc + 512], AF.Exp,
                            scale=-1.0, inc=ST)
                pend.append((i, qg, hh, ysc, rv, evv))
                if i == L - 1:
                    while pend:
                        _, tqg, thh, tysc, trv, tevv = pend.pop(0)
                        pi2 = i % 5
                        sem, val = pb2_free[pi2][0]
                        bld.wait("t", sem, val)
                        bld.wait("t", ST, trv)
                        bld.wait("t", VT, tevv)
                        bv = bld.op("t", "matmul", PB2[pi2][0:64, 0:512],
                                    ONER_BF[32:33, 0:64],
                                    PTSB[32:33, thh * 512:(thh + 1) * 512],
                                    start=True, stop=True, inc=PT)
                        warset("rrow", PT, bv, thh)
                        bld.wait("v", PT, bv)
                        yv = bld.op("v", "tensor_mul",
                                    AOBF[thh * 64:thh * 64 + 64,
                                         c * T + tqg * 512: c * T + (tqg + 1) * 512],
                                    YS[0:64, tysc:tysc + 512],
                                    PB2[pi2][0:64, 0:512], inc=VT)
                        pb2_free[pi2][0] = (VT, yv)
        return yv

    b_last = None
    h_dead_pt = [0]
    wready = load_up(0)
    for c in range(NC):
        ck = mla_qkv(c, wready)
        up_freed[0] = bld.cnt[PT]
        if c + 1 < NC:
            wready = load_up(c + 1)
        if c == NC - 1:
            h_dead_pt[0] = bld.cnt[PT]
        tap_dump("kc%d" % c, KR2[:, :])
        tap_dump("mb%d" % c, MBB[:, :])
        b_last = attn_inner_dual(c, ck)
        for mat in ("q", "k", "v"):
            ring2_free[mat][0] = (PT, bld.cnt[PT])
        up_freed[0] = bld.cnt[PT]
    tap_dump("yraw2", YS[0:64, :]); tap_dump("sums2", YS[64:65, :])
    tap_dump("sumsb2", YS[64:65, :]); tap_dump("mbb2", MBB[:, :])
    tap_dump("yb", AOBF[:, :])

    stg_free[0] = max(stg_free[0], lu_cast_done[0])
    stg_free[1] = max(stg_free[1], lu_cast_done[0])
    ow_ready = []
    for k in range(NC):
        gv = stage_load(dr["owT2"][k * 128:(k + 1) * 128, :], 1024,
                        XBBF[:, k * 1024:(k + 1) * 1024],
                        pre_gwait=(PT, h_dead_pt[0]))
        ow_ready.append(gv)
    bld.wait("t", VT, b_last)
    proj_to_res(0, AOBF, PB2, pb2_free, ow_ready)
    tap_dump("x2", RES[:, :])

    if upto == "B":
        bctx.close(); atc.close(); aoctx.close(); cglobal.close(); abctx.close()
        _finish(bld)
        return nc, sorted(tap_out.keys())

    bctx.close()
    atc.close()

    cps = ExitStack()
    PRT = [bld.ps("PRT%d" % i, [128, 512], F32, stack=cps) for i in range(NTG)]
    PB3 = [bld.ps("PB3a", [128, 512], F32, stack=cps),
           bld.ps("PB3b", [128, 512], F32, stack=cps)]
    pb3_free = [[(PT, 0)], [(PT, 0)]]

    rctx = {"psum": PRT, "mark": [0] * NTG}
    h2_done = feature_ln(True, G2, (XBBF, 0), False, f32_sink=True,
                         ps_pair=PB3, ps_free=pb3_free, router_ctx=rctx)
    tap_dump("h2", XBBF[:, :])
    lgt_war = [(PT, 0)]
    for tg in range(NTG):
        bld.wait("t", PT, rctx["mark"][tg])
        bv = bld.mm(PRT[tg][0:7, 0:512], RBIAS[:, :], ONER_F[0:1, 0:512],
                    start=False, stop=True, inc=PT)
        sem, val = lgt_war[0]
        bld.wait("s", sem, val)
        bld.wait("s", PT, bv)
        lv = bld.op("s", "copy", LGT[0:7, :], PRT[tg][0:7, 0:512], inc=ST)
        for j in range(4):
            tb = tg * 4 + j
            pi = tb % 2
            sem, val = pb3_free[pi][0]
            bld.wait("t", sem, val)
            bld.wait("t", ST, lv)
            tv = bld.op("t", "transpose", PB3[pi][0:128, 0:7],
                        LGT[0:7, j * 128:(j + 1) * 128], IDF[0:7, 0:7], inc=PT)
            bld.wait("s", PT, tv)
            sg = bld.op("s", "activation", PRBp[:, tb * 8: tb * 8 + 7],
                        PB3[pi][0:128, 0:7], AF.Sigmoid, inc=ST)
            pb3_free[pi][0] = (ST, sg)
            lgt_war[0] = (PT, tv)
    prb_done = bld.cnt[ST]
    bld.wait("v", ST, prb_done)
    for tb in range(NTT):
        P7 = PRBp[:, tb * 8: tb * 8 + 7]
        W7 = WGT[:, tb * 8: tb * 8 + 7]
        v1 = bld.op("v", "reduce_max", TOPS[:, 0:1], P7, axis=AX.X, inc=VT)
        bld.wait("v", VT, v1)
        v2 = bld.op("v", "tensor_single_scalar", W7, P7, TOPS[:, 0:1],
                    op=ALU.is_ge, inc=VT)
        bld.wait("v", VT, v2)
        v3 = bld.op("v", "scalar_tensor_tensor", W7, W7, -1e9, P7,
                    op0=ALU.mult, op1=ALU.add, inc=VT)
        bld.wait("v", VT, v3)
        v4 = bld.op("v", "reduce_max", TOPS[:, 1:2], W7, axis=AX.X, inc=VT)
        bld.wait("v", VT, v4)
        v5 = bld.op("v", "tensor_add", TOPS[:, 2:3], TOPS[:, 0:1], TOPS[:, 1:2], inc=VT)
        bld.wait("v", VT, v5)
        v6 = bld.op("v", "reciprocal", TOPS[:, 3:4], TOPS[:, 2:3], inc=VT)
        bld.wait("v", VT, v6)
        v7 = bld.op("v", "tensor_single_scalar", W7, P7, TOPS[:, 1:2],
                    op=ALU.is_ge, inc=VT)
        bld.wait("v", VT, v7)
        v8 = bld.op("v", "tensor_mul", W7, W7, P7, inc=VT)
        bld.wait("v", VT, v8)
        bld.op("v", "tensor_scalar_mul", W7, W7, TOPS[:, 3:4], inc=VT)
    wgt_done = bld.cnt[VT]
    tap_dump("wgt", WGT[:, :])
    tap_dump("prb", PRBp[:, :])

    def sigma(p):
        tb, k = p // 8, p % 8
        return ((tb % 2) * 8 + k) * 8 + tb // 2

    bld.wait("t", VT, bld.cnt[VT])
    seen = set()
    tp_ping = [0]
    for p0 in range(128):
        if p0 in seen:
            continue
        cyc = [p0]
        seen.add(p0)
        qq_ = sigma(p0)
        while qq_ != p0:
            cyc.append(qq_)
            seen.add(qq_)
            qq_ = sigma(qq_)
        held = None
        for i, p in enumerate(cyc):
            pi = tp_ping[0] % 2
            tp_ping[0] += 1
            sem, val = pb3_free[pi][0]
            bld.wait("t", sem, val)
            tb, k = p // 8, p % 8
            tv = bld.op("t", "transpose", PB3[pi][:, 0:128],
                        RES[:, k * T + tb * 128: k * T + (tb + 1) * 128],
                        IDF[:, :], inc=PT)
            if held is not None:
                hpi, hv = held
                bld.wait("s", PT, tv)
                ev = bld.op("s", "activation",
                            RES[:, k * T + tb * 128: k * T + (tb + 1) * 128],
                            PB3[hpi][:, 0:128], AF.Copy, bias=0.0, scale=2.0, inc=ST)
                pb3_free[hpi][0] = (ST, ev)
            held = (pi, tv)
        tb, k = cyc[0] // 8, cyc[0] % 8
        hpi, hv = held
        bld.wait("s", PT, hv)
        ev = bld.op("s", "activation",
                    RES[:, k * T + tb * 128: k * T + (tb + 1) * 128],
                    PB3[hpi][:, 0:128], AF.Copy, bias=0.0, scale=2.0, inc=ST)
        pb3_free[hpi][0] = (ST, ev)
    x2t_done = bld.cnt[ST]
    X2TOK = RES

    p2ctx = ExitStack()
    OW2B = bld.sb("OW2B", [128, NC * 1024], BF16, stack=p2ctx)
    ow2b_ready = []
    for k in range(NC):
        gv = stage_load(dr["owT2"][k * 128:(k + 1) * 128, :], 1024,
                        OW2B[:, k * 1024:(k + 1) * 1024],
                        pre_gwait=(ST, x2t_done) if k == 0 else None)
        ow2b_ready.append(gv)
    bld.wait("t", ST, x2t_done)
    for tb in range(NTT):
        for no in range(2):
            pi = (tb * 2 + no) % 2
            sem, val = pb3_free[pi][0]
            bld.wait("t", sem, val)
            for k in range(NC):
                bld.wait("t", GT, ow2b_ready[k])
                pv = bld.mm(PB3[pi][:, :],
                            AOBF[:, k * T + tb * 128: k * T + (tb + 1) * 128],
                            OW2B[:, k * 1024 + no * 512: k * 1024 + (no + 1) * 512],
                            start=(k == 0), stop=(k == NC - 1),
                            inc=PT if k == NC - 1 else None)
            bld.wait("v", PT, pv)
            rv = bld.op("v", "tensor_sub",
                        X2TOK[:, tb * D + no * 512: tb * D + (no + 1) * 512],
                        X2TOK[:, tb * D + no * 512: tb * D + (no + 1) * 512],
                        PB3[pi][:, :], inc=VT)
            pb3_free[pi][0] = (VT, rv)
    x2t_done = bld.cnt[VT]
    p2ctx.close()
    cps.close()
    aoctx.close()

    cmain = ExitStack()
    WS = [bld.sb("WS%d" % i, [128, 4 * 1024], BF16, stack=cmain) for i in range(5)]
    ZB = bld.sb("ZB", [128, NC * T], BF16, stack=cmain)
    PG = [bld.ps("PGa", [128, 512], F32, stack=cmain),
          bld.ps("PGb", [128, 512], F32, stack=cmain)]
    PU = [bld.ps("PUa", [128, 512], F32, stack=cmain),
          bld.ps("PUb", [128, 512], F32, stack=cmain)]
    PD = [bld.ps("PDa", [128, 512], F32, stack=cmain),
          bld.ps("PDb", [128, 512], F32, stack=cmain)]
    pg_free = [[(PT, 0)], [(PT, 0)]]
    pu_free = [[(PT, 0)], [(PT, 0)]]
    pd_free = [[(PT, 0)], [(PT, 0)]]
    ws_free = [[(PT, 0)] for _ in range(5)]
    ws_first = [True] * 5
    slot_rr = [0]
    sgr2_war = [[(PT, 0)]]

    def load_piece(e, half, kind, sub2):
        s = slot_rr[0] % 5
        slot_rr[0] += 1
        sem, val = ws_free[s][0]
        gv = None
        for sub in range(4):
            if kind in ("g", "u"):
                dram3 = dr["gatesT"] if kind == "g" else dr["upsT"]
                src = dram3[e, sub * 256:(sub + 1) * 256,
                            half * 1024 + sub2 * 512: half * 1024 + (sub2 + 1) * 512] \
                    .rearrange("(k p) f -> p k f", p=128)
                dst = WS[s][:, sub * 1024:(sub + 1) * 1024] \
                    .rearrange("p (k f) -> p k f", k=2)
            else:
                src = dr["downsT"][e, half * 1024 + sub2 * 512 + sub * 128:
                                   half * 1024 + sub2 * 512 + (sub + 1) * 128, :]
                dst = WS[s][:, sub * 1024:(sub + 1) * 1024]
            pre = (sem, val) if sub == 0 else None
            if ws_first[s] and sub == 0:
                bld.wait("g", VT, x2t_done)
            gv = stage_load(src, 1024, dst, pre_gwait=pre)
        ws_first[s] = False
        return s, gv

    def expert(e):
        for half in range(2):
            sg1, gv_g1 = load_piece(e, half, "g", 0)
            sg2, gv_g2 = load_piece(e, half, "g", 1)
            su1, gv_u1 = load_piece(e, half, "u", 0)
            su2, gv_u2 = load_piece(e, half, "u", 1)
            sd1 = sd2 = gv_d1 = gv_d2 = None
            bld.wait("t", VT, h2_done)
            if e == 0 and half == 0:
                bld.wait("v", VT, x2t_done)
            z_last = None
            for ic in range(NC):
                gsl = sg1 if ic < 4 else sg2
                usl = su1 if ic < 4 else su2
                icl = ic % 4
                if ic == 0:
                    bld.wait("t", GT, gv_g1)
                    bld.wait("t", GT, gv_u1)
                    sd1, gv_d1 = load_piece(e, half, "d", 0)
                if ic == 4:
                    bld.wait("t", GT, gv_g2)
                    bld.wait("t", GT, gv_u2)
                for tg in range(NTG):
                    gi = (ic * NTG + tg) % 2
                    sem, val = pg_free[gi][0]
                    bld.wait("t", sem, val)
                    sem, val = pu_free[gi][0]
                    bld.wait("t", sem, val)
                    for hc in range(NC):
                        bld.mm(PG[gi][:, :],
                               WS[gsl][:, hc * 512 + icl * 128: hc * 512 + (icl + 1) * 128],
                               XBBF[:, hc * T + tg * 512: hc * T + tg * 512 + 512],
                               start=(hc == 0), stop=(hc == NC - 1))
                    for hc in range(NC):
                        pv = bld.mm(PU[gi][:, :],
                                    WS[usl][:, hc * 512 + icl * 128: hc * 512 + (icl + 1) * 128],
                                    XBBF[:, hc * T + tg * 512: hc * T + tg * 512 + 512],
                                    start=(hc == 0), stop=(hc == NC - 1),
                                    inc=PT if hc == NC - 1 else None)
                    sem, val = sgr2_war[0][0]
                    bld.wait("s", sem, val)
                    bld.wait("s", PT, pv)
                    sl = bld.op("s", "activation", SGR[:, 0:512],
                                PG[gi][:, :], AF.Silu, inc=ST)
                    pg_free[gi][0] = (ST, sl)
                    bld.wait("v", ST, sl)
                    zv = bld.op("v", "tensor_mul",
                                ZB[:, ic * T + tg * 512: ic * T + tg * 512 + 512],
                                SGR[:, 0:512], PU[gi][:, :], inc=VT)
                    sgr2_war[0][0] = (VT, zv)
                    pu_free[gi][0] = (VT, zv)
                    z_last = zv
                if ic == 3:
                    ws_free[sg1][0] = (PT, bld.cnt[PT])
                    ws_free[su1][0] = (PT, bld.cnt[PT])
                    sd2, gv_d2 = load_piece(e, half, "d", 1)
            ws_free[sg2][0] = (PT, bld.cnt[PT])
            ws_free[su2][0] = (PT, bld.cnt[PT])
            bld.wait("t", GT, gv_d1)
            bld.wait("t", GT, gv_d2)
            bld.wait("t", VT, z_last)
            for tb in range(NTT):
                for hhh in range(2):
                    di = (tb * 2 + hhh) % 2
                    sem, val = pd_free[di][0]
                    bld.wait("t", sem, val)
                    for ic in range(NC):
                        dsl = sd1 if ic < 4 else sd2
                        icl = ic % 4
                        pv = bld.mm(PD[di][:, :],
                                    ZB[:, ic * T + tb * 128: ic * T + (tb + 1) * 128],
                                    WS[dsl][:, icl * 1024 + hhh * 512:
                                            icl * 1024 + (hhh + 1) * 512],
                                    start=(ic == 0), stop=(ic == NC - 1),
                                    inc=PT if ic == NC - 1 else None)
                    bld.wait("v", PT, pv)
                    dst = X2TOK[:, tb * D + hhh * 512: tb * D + (hhh + 1) * 512]
                    scal = ONEC_F[:, 0:1] if e == 0 else WGT[:, tb * 8 + e - 1: tb * 8 + e]
                    av = bld.op("v", "scalar_tensor_tensor", dst, PD[di][:, :], scal, dst,
                                op0=ALU.mult, op1=ALU.add, inc=VT)
                    pd_free[di][0] = (VT, av)
            ws_free[sd1][0] = (PT, bld.cnt[PT])
            ws_free[sd2][0] = (PT, bld.cnt[PT])

    bld.wait("v", VT, wgt_done)
    for e in range(NEXP):
        expert(e)
    tap_dump("x3", X2TOK[:, :])
    cmain.close()

    fctx = ExitStack()
    OBUF = bld.sb("OBUF", [128, 2 * D], F32, stack=fctx)
    FSQ = bld.sb("FSQ", [128, D], F32, stack=fctx)
    FST = bld.sb("FST", [128, 8], F32, stack=fctx)
    moe_done = bld.cnt[VT]
    bld.wait("s", VT, moe_done)
    obuf_war = [[(LO, 0)], [(LO, 0)]]
    for tb in range(NTT):
        xt = X2TOK[:, tb * D:(tb + 1) * D]
        v1 = bld.op("v", "reduce_sum", FST[:, 0:1], xt, axis=AX.X, inc=VT)
        s1 = bld.op("s", "activation", FSQ[:, :], xt, AF.Square,
                    accum_out=FST[:, 1:2], inc=ST)
        bld.wait("v", VT, v1)
        bld.wait("v", ST, s1)
        v2 = bld.op("v", "tensor_scalar_mul", FST[:, 2:3], FST[:, 0:1], 1.0 / D, inc=VT)
        bld.wait("v", VT, v2)
        v3 = bld.op("v", "tensor_mul", FST[:, 3:4], FST[:, 2:3], FST[:, 2:3], inc=VT)
        bld.wait("v", VT, v3)
        v4 = bld.op("v", "scalar_tensor_tensor", FST[:, 4:5], FST[:, 1:2], 1.0 / D,
                    FST[:, 3:4], op0=ALU.mult, op1=ALU.subtract, inc=VT)
        bld.wait("v", VT, v4)
        v5 = bld.op("v", "tensor_scalar_add", FST[:, 4:5], FST[:, 4:5], EPS, inc=VT)
        bld.wait("s", VT, v5)
        s2 = bld.op("s", "activation", FST[:, 5:6], FST[:, 4:5], AF.Ln, inc=ST)
        bld.wait("s", ST, s2)
        s3 = bld.op("s", "activation", FST[:, 5:6], FST[:, 5:6], AF.Exp,
                    scale=-0.5, inc=ST)
        bld.wait("v", ST, s3)
        v6 = bld.op("v", "scalar_tensor_tensor", FST[:, 6:7], FST[:, 2:3], -1.0,
                    FST[:, 5:6], op0=ALU.mult, op1=ALU.mult, inc=VT)
        bld.wait("v", VT, v6)
        ob = tb % 2
        sem, val = obuf_war[ob][0]
        bld.wait("v", sem, val)
        v7 = bld.op("v", "tensor_scalar", OBUF[:, ob * D:(ob + 1) * D], xt,
                    FST[:, 5:6], FST[:, 6:7], op0=ALU.mult, op1=ALU.add, inc=VT)
        bld.wait("s", VT, v7)
        lo = bld.op("s", "dma_start", inc=LO, incby=16,
                    out=out[tb * 128:(tb + 1) * 128, :],
                    in_=OBUF[:, ob * D:(ob + 1) * D])
        obuf_war[ob][0] = (LO, lo)

    fctx.close(); cglobal.close(); abctx.close()
    _finish(bld)
    return nc, sorted(tap_out.keys())




def _finish(bld):
    bld.wait("y", bld.LO, bld.cnt[bld.LO])
    bld.wait("y", bld.LG, bld.cnt[bld.LG])
    _emit_block(bld)
    bld.ctx.close()


def kernel(**inputs):
    key = (BUILD_UPTO, tuple(sorted(TAPS)))
    if key not in _CACHE:
        _CACHE[key] = build_program(upto=BUILD_UPTO, taps=TAPS)
    nc, tap_names = _CACHE[key]
    w = _host_prep(inputs)
    c = _host_constants()
    x = np.asarray(inputs["L0_attn_x"], dtype=np.float32)
    base = {}
    base.update(w)
    base.update(c)
    cores = CORES if CORES is not None else list(range(8))
    in_maps = []
    for core in cores:
        m = dict(base)
        m["x"] = np.ascontiguousarray(x[core].T)
        in_maps.append(m)
    res = run_bass_kernel_spmd(nc, in_maps, core_ids=cores)
    outs = np.stack([np.asarray(res.results[i]["out"]) for i in range(len(cores))],
                    axis=0)
    if tap_names:
        kernel.last_taps = [
            {t: np.asarray(res.results[i]["tap_" + t]) for t in tap_names}
            for i in range(len(cores))]
    if len(cores) < 8:
        full = np.zeros((8, T, D), np.float32)
        full[np.array(cores)] = outs
        return full
    return outs


# version 23
# speedup vs baseline: 1.0457x; 1.0027x over previous
import numpy as np
import ml_dtypes
from contextlib import ExitStack

import concourse.bass as bass
import concourse.mybir as mybir
from concourse.bass_utils import run_bass_kernel_spmd

F32 = mybir.dt.float32
BF16 = mybir.dt.bfloat16
AF = mybir.ActivationFunctionType
ALU = mybir.AluOpType
AX = mybir.AxisListType

T = 2048
D = 1024
NC = 8
LAT = 256
RD = 32
NEXP = 8
NTT = 16
NTG = 4
BIGNEG = -1e30
EPS = 1e-5

BUILD_UPTO = "C"
TAPS = ()
CORES = None

_CACHE = {}


class B:
    def __init__(self, nc):
        self.nc = nc
        self.q = {k: [] for k in "tvsgy"}
        self.cnt = {}
        self.ctx = ExitStack()

    def sem(self, name):
        h = self.ctx.enter_context(self.nc.semaphore(name))
        self.cnt[h] = 0
        return h

    def sb(self, name, shape, dt, stack=None):
        return (stack or self.ctx).enter_context(self.nc.sbuf_tensor(name, shape, dt))

    def ps(self, name, shape, dt=F32, stack=None):
        return (stack or self.ctx).enter_context(self.nc.psum_tensor(name, shape, dt))

    def bump(self, sem, by=1):
        self.cnt[sem] += by
        return self.cnt[sem]

    def emit(self, eng, fn):
        self.q[eng].append(fn)

    def wait(self, eng, sem, val):
        if val and val > 0:
            self.emit(eng, lambda e, s=sem, v=val: e.wait_ge(s, v))

    def op(self, eng, meth, *args, inc=None, incby=1, **kw):
        v = self.bump(inc, incby) if inc is not None else None
        def f(e, m=meth, a=args, k=kw, i=inc, ib=incby):
            r = getattr(e, m)(*a, **k)
            if i is not None:
                r.then_inc(i, ib)
        self.emit(eng, f)
        return v

    def mm(self, out, lhsT, rhs, start, stop, inc=None):
        return self.op("t", "matmul", out, lhsT, rhs, start=start, stop=stop, inc=inc)


def _emit_block(bld):
    with bld.nc.Block() as block:
        @block.tensor
        def _(e):
            for f in bld.q["t"]:
                f(e)
        @block.vector
        def _(e):
            for f in bld.q["v"]:
                f(e)
        @block.scalar
        def _(e):
            for f in bld.q["s"]:
                f(e)
        @block.gpsimd
        def _(e):
            for f in bld.q["g"]:
                f(e)
        @block.sync
        def _(e):
            for f in bld.q["y"]:
                f(e)


def _host_constants():
    c = {}
    p = np.arange(128)[:, None]
    j = np.arange(128)[None, :]
    mA = np.where(p >= j, 0.0, BIGNEG).astype(np.float32)
    mB = np.where(p <= j, 0.0, BIGNEG).astype(np.float32)
    c["mask_band"] = np.concatenate([mA, mB, mA, mB], axis=1).astype(ml_dtypes.bfloat16)
    c["mask_inner"] = np.concatenate(
        [np.full((128, 384), BIGNEG, np.float32), mB,
         np.zeros((128, 512), np.float32)], axis=1).astype(ml_dtypes.bfloat16)
    ms = np.full((32, 16, 128), BIGNEG, np.float32)
    qv = np.arange(128)
    for qb in range(16):
        for jj in range(32):
            ms[jj, qb, :] = np.where((64 * jj) < (128 * qb + qv - 128), 0.0, BIGNEG)
    c["mask_str"] = ms.reshape(32, 16 * 128).astype(ml_dtypes.bfloat16)
    c["mask_allneg"] = np.full((128, 256), BIGNEG, np.float32).astype(ml_dtypes.bfloat16)
    c["iden_bf"] = np.eye(128, dtype=ml_dtypes.bfloat16)
    c["iden_f"] = np.eye(128, dtype=np.float32)
    c["ones_row_bf"] = np.ones((33, 128), dtype=ml_dtypes.bfloat16)
    c["ones_blk_bf"] = np.ones((65, 128), dtype=ml_dtypes.bfloat16)
    s2m = np.zeros((128, 2), np.float32)
    s2m[0:64, 0] = 1.0
    s2m[64:128, 1] = 1.0
    c["sel2"] = s2m.astype(ml_dtypes.bfloat16)
    c["ones_col_bf"] = np.ones((128, 1), dtype=ml_dtypes.bfloat16)
    c["ones_row_f"] = np.ones((65, 512), dtype=np.float32)
    c["ones_col_f"] = np.ones((128, 1), dtype=np.float32)
    sel = np.zeros((128, NTT, 32), np.float32)
    for tile in range(NTT):
        for i in range(2):
            sel[i * 64, tile, tile * 2 + i] = 1.0
    c["vsel"] = sel.reshape(128, NTT * 32).astype(ml_dtypes.bfloat16)
    inv = 1.0 / (10000.0 ** (np.arange(0, RD, 2, dtype=np.float64) / RD))
    tpos = np.arange(T, dtype=np.float64)
    fr = tpos[:, None] * inv[None, :]
    emb = np.concatenate([fr, fr], axis=-1)
    cos = np.cos(emb).astype(np.float32)
    sin = np.sin(emb).astype(np.float32)
    c["cosE"] = np.ascontiguousarray(np.concatenate([cos, cos], axis=1))
    c["sinE"] = np.ascontiguousarray(np.concatenate([sin, sin], axis=1))
    return c


def _host_prep(inputs):
    t = lambda a: np.ascontiguousarray(np.asarray(a, dtype=np.float32).T)
    w = {}
    wqkv = np.asarray(inputs["L0_attn_w_qkv"], dtype=np.float32)
    w["wqT"] = t(wqkv[0:D])
    w["wkT"] = t(wqkv[D:2 * D])
    w["wvT"] = t(wqkv[2 * D:3 * D])
    w["woT"] = t(inputs["L0_attn_w_out"])
    w["g1"] = np.ascontiguousarray(
        np.asarray(inputs["L0_ffn_ln1_w"], dtype=np.float32).reshape(NC, 128).T)
    w["g2"] = np.ascontiguousarray(
        np.asarray(inputs["L0_ffn_ln2_w"], dtype=np.float32).reshape(NC, 128).T)
    w["kvdT"] = t(inputs["L0_ffn_kv_d"])
    w["qdT"] = t(inputs["L0_ffn_q_d"])
    w["kuT"] = t(inputs["L0_ffn_k_u"])
    w["quT"] = t(inputs["L0_ffn_q_u"])
    w["vuT"] = t(inputs["L0_ffn_v_u"])
    w["rqT"] = t(inputs["L0_ffn_rope_q_w"])
    w["rkT"] = t(inputs["L0_ffn_rope_k_w"])
    w["owT2"] = t(inputs["L0_ffn_o_w"])
    w["rwT"] = t(inputs["L0_ffn_router_w"])
    w["rbias"] = np.asarray(inputs["L0_ffn_routing_bias"], dtype=np.float32).reshape(1, 7)
    sg = np.asarray(inputs["L0_ffn_shared_gate"], dtype=np.float32)
    su = np.asarray(inputs["L0_ffn_shared_up"], dtype=np.float32)
    sd = np.asarray(inputs["L0_ffn_shared_down"], dtype=np.float32)
    rg = np.asarray(inputs["L0_ffn_routed_gate"], dtype=np.float32)
    ru = np.asarray(inputs["L0_ffn_routed_up"], dtype=np.float32)
    rd = np.asarray(inputs["L0_ffn_routed_down"], dtype=np.float32)
    w["gatesT"] = np.ascontiguousarray(np.concatenate([sg, rg], 0).transpose(0, 2, 1))
    w["upsT"] = np.ascontiguousarray(np.concatenate([su, ru], 0).transpose(0, 2, 1))
    w["downsT"] = np.ascontiguousarray(np.concatenate([sd, rd], 0).transpose(0, 2, 1))
    return w


def build_program(upto="C", taps=()):
    nc = bass.Bass("TRN2", target_bir_lowering=False)
    bld = B(nc)
    taps = set(taps)

    dr = {}
    def din(name, shape, dt=F32):
        dr[name] = nc.dram_tensor(name, list(shape), dt, kind="ExternalInput").ap()
    din("x", [D, T])
    din("wqT", [D, D]); din("wkT", [D, D]); din("wvT", [D, D]); din("woT", [D, D])
    din("g1", [128, NC]); din("g2", [128, NC])
    din("kvdT", [D, LAT]); din("qdT", [D, LAT])
    din("kuT", [LAT, D]); din("quT", [LAT, D]); din("vuT", [LAT, D])
    din("rqT", [LAT, 512]); din("rkT", [D, 512])
    din("owT2", [D, D])
    din("rwT", [D, 7]); din("rbias", [1, 7])
    din("gatesT", [NEXP, D, 2048]); din("upsT", [NEXP, D, 2048])
    din("downsT", [NEXP, 2048, D])
    din("cosE", [T, 64]); din("sinE", [T, 64])
    din("mask_band", [128, 512], BF16)
    din("mask_inner", [128, 1024], BF16)
    din("mask_str", [32, 16 * 128], BF16)
    din("mask_allneg", [128, 256], BF16)
    din("iden_bf", [128, 128], BF16); din("iden_f", [128, 128], F32)
    din("ones_row_bf", [33, 128], BF16); din("ones_col_bf", [128, 1], BF16)
    din("ones_blk_bf", [65, 128], BF16)
    din("sel2", [128, 2], BF16)
    din("ones_row_f", [65, 512], F32); din("ones_col_f", [128, 1], F32)
    din("vsel", [128, NTT * 32], BF16)

    out = nc.dram_tensor("out", [T, D], F32, kind="ExternalOutput").ap()
    tap_out = {}
    def mktap(name, shape, dt=F32):
        if name in taps:
            tap_out[name] = nc.dram_tensor("tap_" + name, list(shape), dt,
                                           kind="ExternalOutput").ap()
    mktap("x1", [128, NC * T]); mktap("h", [128, NC * T], BF16)
    mktap("x2", [128, NC * T]); mktap("h2", [128, NC * T], BF16)
    mktap("wgt", [128, NTT * 8]); mktap("x3", [128, NTT * D])
    mktap("prb", [128, NTT * 8])
    mktap("ao", [128, NC * T], BF16); mktap("yb", [128, NC * T], BF16)
    mktap("yraw", [64, T]); mktap("sums", [1, T])
    mktap("sumsb", [1, T]); mktap("mbb", [128, 2])
    mktap("yraw2", [64, T]); mktap("sums2", [1, T])
    for _c in range(NC):
        mktap("kc%d" % _c, [128, T], BF16)
        mktap("mb%d" % _c, [128, 2])
    mktap("rtb3", [128, 512], BF16)
    mktap("wrk3", [128, 512], BF16)
    mktap("sumsb2", [1, T]); mktap("mbb2", [128, 2])
    mktap("q0", [128, T], BF16); mktap("k0", [128, T], BF16)
    mktap("v0", [128, NTT * 130], BF16)
    mktap("lat", [128, 2 * T], BF16)

    LD = bld.LD = bld.sem("LD")
    LG = bld.LG = bld.sem("LG")
    LO = bld.LO = bld.sem("LO")
    PT = bld.PT = bld.sem("PT")
    VT = bld.VT = bld.sem("VT")
    ST = bld.ST = bld.sem("ST")
    GT = bld.GT = bld.sem("GT")
    LU = bld.LU = bld.sem("LU")
    CSLD = bld.CSLD = bld.sem("CSLD")
    XLD = bld.XLD = bld.sem("XLD")

    XBBF = bld.sb("XBBF", [128, NC * T], BF16)
    STG = bld.sb("STG", [128, 2 * 1024], F32)
    IDF = bld.sb("IDF", [128, 128], F32)
    ONER_BF = bld.sb("ONER_BF", [33, 128], BF16)
    ONEC_BF = bld.sb("ONEC_BF", [128, 1], BF16)
    ONER_F = bld.sb("ONER_F", [65, 512], F32)
    ONEC_F = bld.sb("ONEC_F", [128, 1], F32)
    G1 = bld.sb("G1", [128, NC], F32)
    G2 = bld.sb("G2", [128, NC], F32)
    STA = bld.sb("STA", [128, 512], F32)
    STB = bld.sb("STB", [128, 512], F32)
    STC = bld.sb("STC", [128, 512], F32)
    BCR = bld.sb("BCR", [1, 512], F32)
    LNSB = bld.sb("LNSB", [128, 512], BF16)
    LNSQ = bld.sb("LNSQ", [128, 512], BF16)
    RWT = bld.sb("RWT", [128, NC * 7], F32)
    RBIAS = bld.sb("RBIAS", [1, 7], F32)
    SGR = bld.sb("SGR", [128, 512], F32)

    abctx = ExitStack()
    RES = bld.sb("RES", [128, NC * T], F32, stack=abctx)
    cglobal = ExitStack()
    LGT = bld.sb("LGT", [7, 512], F32, stack=cglobal)
    PRBp = bld.sb("PRBp", [128, NTT * 8], F32, stack=cglobal)
    WGT = bld.sb("WGT", [128, NTT * 8], F32, stack=cglobal)
    TOPS = bld.sb("TOPS", [128, 64], F32, stack=cglobal)
    aoctx = ExitStack()
    AOBF = bld.sb("AOBF", [128, NC * T], BF16, stack=aoctx)
    atc = ExitStack()
    IDB = bld.sb("IDB", [128, 128], BF16, stack=atc)
    PTRB = bld.sb("PTRB", [128, 3 * 512], BF16, stack=atc)
    MBAND2 = bld.sb("MBAND2", [128, 512], BF16, stack=atc)
    MBI = bld.sb("MBI", [128, 1024], BF16, stack=atc)
    YS = bld.sb("YS", [65, T], F32, stack=atc)
    PTSB = bld.sb("PTSB", [33, 2 * 512], BF16, stack=atc)
    MBB = bld.sb("MBB", [128, 2], F32, stack=atc)
    MBS = bld.sb("MBS", [1, 2], F32, stack=atc)
    BND = bld.sb("BND", [2, 16], F32, stack=atc)
    SEL2 = bld.sb("SEL2", [128, 2], BF16, stack=atc)

    war = {
        "negc": [(PT, 0)], "negt": [(LG, 0)], "negm": [(PT, 0)],
        "ptr": [[(PT, 0)] for _ in range(3)],
        "ptr2": [[(PT, 0)], [(PT, 0)]],
        "rrow": [[(PT, 0)], [(PT, 0)]],
        "rtb2": [[(PT, 0)], [(PT, 0)]],
        "lnsb": [(PT, 0)], "lnsq": [(PT, 0)],
        "sgr": [[(PT, 0)], [(PT, 0)]],
        "cs": [[(VT, 0)]],
        "rtb": [[(PT, 0)], [(PT, 0)]],
        "yraw": [(VT, 0)], "sums": [(PT, 0)],
    }

    def warwait(eng, key, slot=None):
        cell = war[key] if slot is None else war[key][slot]
        sem, val = cell[0]
        bld.wait(eng, sem, val)

    def warset(key, sem, val, slot=None):
        cell = war[key] if slot is None else war[key][slot]
        cell[0] = (sem, val)

    stg_free = [0, 0]
    stg_next = [0]

    def emit_bound(qring, kring, ps, ps_free):
        for mi, ring in ((0, qring), (1, kring)):
            for tg in range(NTG):
                warwait("s", "lnsq")
                sq = bld.op("s", "activation", LNSQ[:, :],
                            ring[:, tg * 512:(tg + 1) * 512], AF.Square, inc=ST)
                sem, val = ps_free[0]
                bld.wait("t", sem, val)
                bld.wait("t", ST, sq)
                pv = bld.mm(ps[0:2, 0:512], SEL2[:, :], LNSQ[:, :],
                            start=True, stop=True, inc=PT)
                warset("lnsq", PT, pv)
                bld.wait("v", PT, pv)
                rv = bld.op("v", "reduce_max", BND[0:2, mi * 4 + tg: mi * 4 + tg + 1],
                            ps[0:2, 0:512], axis=AX.X, inc=VT)
                ps_free[0] = (VT, rv)
        bld.wait("v", VT, bld.cnt[VT])
        r1 = bld.op("v", "reduce_max", BND[0:2, 8:9], BND[0:2, 0:4], axis=AX.X, inc=VT)
        bld.wait("v", VT, r1)
        r2 = bld.op("v", "reduce_max", BND[0:2, 9:10], BND[0:2, 4:8], axis=AX.X, inc=VT)
        bld.wait("v", VT, r2)
        r3 = bld.op("v", "tensor_add", BND[0:2, 10:11], BND[0:2, 8:9],
                    BND[0:2, 9:10], inc=VT)
        bld.wait("v", VT, r3)
        r4 = bld.op("v", "tensor_scalar_mul", BND[0:2, 10:11], BND[0:2, 10:11],
                    -0.0625, inc=VT)
        bld.wait("v", VT, r4)
        r5 = bld.op("v", "tensor_scalar_add", BND[0:2, 10:11], BND[0:2, 10:11],
                    36.0, inc=VT)
        bld.wait("v", VT, r5)
        r6 = bld.op("v", "tensor_scalar_min", BND[0:2, 10:11], BND[0:2, 10:11],
                    0.0, inc=VT)
        sem, val = ps_free[0]
        bld.wait("t", sem, val)
        bld.wait("t", VT, r6)
        tv = bld.op("t", "transpose", ps[0:1, 0:2], BND[0:2, 10:11],
                    IDF[0:2, 0:2], inc=PT)
        bld.wait("s", PT, tv)
        me = bld.op("s", "copy", MBS[0:1, 0:2], ps[0:1, 0:2], inc=ST)
        for hh in range(2):
            sem, val = ps_free[0]
            bld.wait("t", sem, val)
            bld.wait("t", ST, me)
            bv = bld.mm(ps[0:128, 0:1], ONER_F[0:1, 0:128],
                        MBS[0:1, hh:hh + 1], start=True, stop=True, inc=PT)
            bld.wait("s", PT, bv)
            mbe = bld.op("s", "copy", MBB[:, hh:hh + 1], ps[0:128, 0:1], inc=ST)
            ps_free[0] = (ST, mbe)

    def stage_load(dram_ap, cols, cast_dst, pre_gwait=None, eng="g"):
        s = stg_next[0] % 2
        stg_next[0] += 1
        bld.wait("y", GT, stg_free[s])
        lv = bld.op("y", "dma_start", inc=LD, incby=16,
                    out=STG[:, s * 1024: s * 1024 + cols], in_=dram_ap)
        if pre_gwait is not None:
            pws = pre_gwait if isinstance(pre_gwait, list) else [pre_gwait]
            for sem, val in pws:
                bld.wait(eng, sem, val)
        bld.wait(eng, LD, lv)
        meth = "tensor_copy" if eng != "s" else "copy"
        gv = bld.op(eng, meth, cast_dst,
                    STG[:, s * 1024: s * 1024 + cols], inc=GT)
        stg_free[s] = gv
        return gv

    def tap_dump(name, src_ap):
        if name in tap_out:
            for sem in (PT, VT, ST, GT):
                bld.wait("s", sem, bld.cnt[sem])
            bld.op("s", "dma_start", inc=LO, incby=16,
                   out=tap_out[name][:, :], in_=src_ap)

    actx = ExitStack()
    MSTR = bld.sb("MSTR", [32, 16 * 128], BF16, stack=actx)
    VSEL = bld.sb("VSEL", [128, NTT * 32], BF16, stack=actx)
    QRG = bld.sb("QRG", [128, T], BF16, stack=actx)
    KRG = bld.sb("KRG", [128, T], BF16, stack=actx)
    VRG = bld.sb("VRG", [128, NTT * 130], BF16, stack=actx)
    VSB = bld.sb("VSB", [32, 130], BF16, stack=actx)
    WQ = bld.sb("WQ", [128, NC * 128], BF16, stack=actx)
    WK = bld.sb("WK", [128, NC * 128], BF16, stack=actx)
    WV = bld.sb("WV", [128, NC * 128], BF16, stack=actx)
    PA = [bld.ps("PA0", [128, 512], F32, stack=actx),
          bld.ps("PA1", [128, 512], F32, stack=actx)]
    PB = [bld.ps("PB0", [128, 512], F32, stack=actx),
          bld.ps("PB1", [128, 512], F32, stack=actx),
          bld.ps("PB2x", [128, 512], F32, stack=actx)]
    PC = [bld.ps("PC0", [128, 512], F32, stack=actx),
          bld.ps("PC1", [128, 512], F32, stack=actx)]
    PM = bld.ps("PM", [128, 512], F32, stack=actx)

    for dst, src in [
        (MBAND2, dr["mask_band"][:, :]), (MBI, dr["mask_inner"][:, :]), (MSTR, dr["mask_str"][:, :]),
        (IDB, dr["iden_bf"][:, :]),
        (IDF, dr["iden_f"][:, :]), (ONER_BF, dr["ones_row_bf"][:, :]),
        (ONEC_BF, dr["ones_col_bf"][:, :]), (ONER_F, dr["ones_row_f"][:, :]),
        (ONEC_F, dr["ones_col_f"][:, :]), (G1, dr["g1"][:, :]), (G2, dr["g2"][:, :]),
        (SEL2, dr["sel2"][:, :]),
        (VSEL, dr["vsel"][:, :]), (RBIAS, dr["rbias"][:, :]),
    ]:
        bld.op("y", "dma_start", out=dst[:, :], in_=src, inc=LD, incby=16)
    bld.op("y", "dma_start", out=RWT[:, :].rearrange("p (k f) -> p k f", k=NC),
           in_=dr["rwT"][:, :].rearrange("(k p) f -> p k f", p=128), inc=LD, incby=16)
    bld.op("g", "memset", STA[:, :], 0.0)
    bld.op("g", "memset", STB[:, :], 0.0)
    zmem = bld.op("g", "memset", STC[:, :], 0.0, inc=GT)

    xld = bld.op("y", "dma_start", inc=XLD, incby=16,
                 out=RES[:, :].rearrange("p (k f) -> p k f", k=NC),
                 in_=dr["x"][:, :].rearrange("(k p) f -> p k f", p=128))

    pa_free = [[(PT, 0)], [(PT, 0)]]
    pb_free = [[(PT, 0)], [(PT, 0)], [(PT, 0)]]
    pc_free = [[(PT, 0)], [(PT, 0)]]
    pm_free = [(PT, 0)]
    vs_free = [(PT, 0)]

    bld.wait("v", XLD, xld)
    xb_cast = []
    for k in range(NC):
        v = bld.op("v", "tensor_copy", XBBF[:, k * T:(k + 1) * T],
                   RES[:, k * T:(k + 1) * T], inc=VT)
        xb_cast.append(v)

    wbuf_ready = {}
    wbuf_freed = {"q": [0], "k": [0], "v": [0]}
    ring_free = {"q": [(PT, 0)], "k": [(PT, 0)], "v": [(PT, 0)]}
    ring_ready = {}

    def load_wchunk(mat, dram, c):
        dst = {"q": WQ, "k": WK, "v": WV}[mat]
        gv = stage_load(
            dram[:, c * 128:(c + 1) * 128].rearrange("(k p) f -> p k f", p=128),
            1024,
            dst[:, :].rearrange("p (k f) -> p k f", k=NC),
            pre_gwait=(PT, wbuf_freed[mat][0]))
        wbuf_ready[(mat, c)] = gv

    def qkv_chunk(c):
        for mat, wbuf, ring in (("q", WQ, QRG), ("k", WK, KRG)):
            bld.wait("t", GT, wbuf_ready[(mat, c)])
            sem, val = ring_free[mat][0]
            bld.wait("t", sem, val)
            ev = None
            for tg in range(NTG):
                pi = tg % 2
                sem, val = pa_free[pi][0]
                bld.wait("t", sem, val)
                for k in range(NC):
                    bld.wait("t", VT, xb_cast[k])
                    pv = bld.mm(PA[pi][:, :],
                                wbuf[:, k * 128:(k + 1) * 128],
                                XBBF[:, k * T + tg * 512: k * T + tg * 512 + 512],
                                start=(k == 0), stop=(k == NC - 1),
                                inc=PT if k == NC - 1 else None)
                bld.wait("s", PT, pv)
                ev = bld.op("s", "copy", ring[:, tg * 512: tg * 512 + 512],
                            PA[pi][:, :], inc=ST)
                pa_free[pi][0] = (ST, ev)
            ring_ready[mat] = ev
        bld.wait("t", GT, wbuf_ready[("v", c)])
        sem, val = ring_free["v"][0]
        bld.wait("g", sem, val)
        mz = bld.op("g", "memset",
                    VRG[0:128, :].rearrange("p (t c) -> p t c", c=130)[:, :, 64:130:65],
                    1.0, inc=GT)
        ev = None
        for tb in range(NTT):
            pi = tb % 2
            sem, val = pa_free[pi][0]
            bld.wait("t", sem, val)
            for k in range(NC):
                pv = bld.mm(PA[pi][:, 0:128],
                            XBBF[:, k * T + tb * 128: k * T + (tb + 1) * 128],
                            WV[:, k * 128:(k + 1) * 128],
                            start=(k == 0), stop=(k == NC - 1),
                            inc=PT if k == NC - 1 else None)
            bld.wait("s", PT, pv)
            bld.wait("s", GT, mz)
            base = tb * 130
            bld.op("s", "copy", VRG[:, base: base + 64], PA[pi][:, 0:64])
            ev = bld.op("s", "copy", VRG[:, base + 65: base + 129], PA[pi][:, 64:128],
                        inc=ST)
            pa_free[pi][0] = (ST, ev)
        ring_ready["v"] = ev
        bld.wait("t", ST, ev)
        sem, val = vs_free[0]
        bld.wait("t", sem, val)
        sem, val = pm_free[0]
        bld.wait("t", sem, val)
        for tile in range(NTT):
            pv = bld.mm(PM[0:32, 0:130],
                        VSEL[:, tile * 32:(tile + 1) * 32],
                        VRG[:, tile * 130:(tile + 1) * 130],
                        start=(tile == 0), stop=(tile == NTT - 1),
                        inc=PT if tile == NTT - 1 else None)
        bld.wait("s", PT, pv)
        vs_ev = bld.op("s", "copy", VSB[0:32, 0:130], PM[0:32, 0:130], inc=ST)
        pm_free[0] = (ST, vs_ev)
        return vs_ev

    def attn_outer_dual(c, vs_ev):
        bld.wait("t", ST, ring_ready["q"])
        bld.wait("t", ST, ring_ready["k"])
        bld.wait("t", ST, vs_ev)
        units = [(bb, hh) for bb in range(NTG) for hh in (0, 1)]

        def emit_pair(gp):
            u, j2 = gp // 2, gp % 2
            bb, hh = units[u]
            r0 = hh * 64
            pp = 2 * bb + j2
            pi = gp % 3
            sem, val = pb_free[pi][0]
            bld.wait("t", sem, val)
            first = True
            for j in range(2):
                qb = 2 * pp + j
                qs = (qb * 128, (qb + 1) * 128)
                co = j * 256
                if qb > 0:
                    bld.mm(PB[pi][:, co:co + 128],
                           KRG[r0:r0 + 64, (qb - 1) * 128: qb * 128],
                           QRG[r0:r0 + 64, qs[0]:qs[1]], start=first, stop=False)
                    first = False
                bld.mm(PB[pi][:, co + 128:co + 256],
                       KRG[r0:r0 + 64, qs[0]:qs[1]],
                       QRG[r0:r0 + 64, qs[0]:qs[1]], start=first, stop=False)
                first = False
            return bld.mm(PB[pi][:, 0:512], IDB[:, :], MBAND2[:, :],
                          start=False, stop=True, inc=PT)

        def emit_sstr(u):
            bb, hh = units[u]
            r0 = hh * 64
            sem, val = pm_free[0]
            bld.wait("t", sem, val)
            bld.mm(PM[0:32, 0:512],
                   KRG[r0:r0 + 64, 0:T:64],
                   QRG[r0:r0 + 64, bb * 512:(bb + 1) * 512],
                   start=True, stop=False)
            return bld.mm(PM[0:32, 0:512], IDB[0:32, 0:32],
                          MSTR[0:32, bb * 512:(bb + 1) * 512],
                          start=False, stop=True, inc=PT)

        s2pv = {0: emit_pair(0), 1: emit_pair(1)}
        sstrpv = {0: emit_sstr(0)}
        exs = {}
        pend = []
        yv = None

        def flush_norm():
            nonlocal yv
            while pend:
                dbb, dhh, dysc, du, drv, devv = pend.pop(0)
                pa_i = du % 2
                sem, val = pa_free[pa_i][0]
                bld.wait("t", sem, val)
                bld.wait("t", ST, drv)
                bld.wait("t", VT, devv)
                bv = bld.op("t", "matmul", PA[pa_i][0:64, 0:512],
                            ONER_BF[32:33, 0:64],
                            PTSB[32:33, dhh * 512:(dhh + 1) * 512],
                            start=True, stop=True, inc=PT)
                warset("rrow", PT, bv, dhh)
                bld.wait("v", PT, bv)
                yv = bld.op("v", "tensor_mul",
                            AOBF[dhh * 64:dhh * 64 + 64,
                                 c * T + dbb * 512: c * T + (dbb + 1) * 512],
                            YS[0:64, dysc:dysc + 512], PA[pa_i][0:64, 0:512],
                            inc=VT)
                pa_free[pa_i][0] = (VT, yv)

        for u, (bb, hh) in enumerate(units):
            r0 = hh * 64
            for j2 in range(2):
                gp = 2 * u + j2
                if gp + 2 < 16:
                    s2pv[gp + 2] = emit_pair(gp + 2)
                pi = gp % 3
                pts = gp % 3
                warwait("s", "ptr", pts)
                bld.wait("s", PT, s2pv[gp])
                ex = bld.op("s", "activation", PTRB[:, pts * 512:(pts + 1) * 512],
                            PB[pi][:, 0:512], AF.Exp, bias=MBB[:, hh:hh + 1],
                            scale=0.125, inc=ST)
                pb_free[pi][0] = (ST, ex)
                exs[gp] = ex
            flush_norm()
            pts2 = u % 2
            warwait("s", "ptr2", pts2)
            bld.wait("s", PT, sstrpv[u])
            exstr = bld.op("s", "activation",
                           PTSB[0:32, pts2 * 512:(pts2 + 1) * 512],
                           PM[0:32, 0:512], AF.Exp, bias=MBB[0:32, hh:hh + 1],
                           scale=0.125, inc=ST)
            pm_free[0] = (ST, exstr)
            sem, val = pc_free[hh][0]
            bld.wait("t", sem, val)
            first = True
            for j in range(4):
                qb = 4 * bb + j
                gp = 2 * u + j // 2
                pts = gp % 3
                co = (qb % 2) * 256
                if j % 2 == 0:
                    bld.wait("t", ST, exs[gp])
                if qb > 0:
                    bld.mm(PC[hh][0:65, j * 128:(j + 1) * 128],
                           VRG[:, (qb - 1) * 130 + hh * 65:
                               (qb - 1) * 130 + hh * 65 + 65],
                           PTRB[:, pts * 512 + co: pts * 512 + co + 128],
                           start=first, stop=False)
                    first = False
                lastmm = bld.mm(PC[hh][0:65, j * 128:(j + 1) * 128],
                                VRG[:, qb * 130 + hh * 65: qb * 130 + hh * 65 + 65],
                                PTRB[:, pts * 512 + co + 128: pts * 512 + co + 256],
                                start=first, stop=False,
                                inc=PT if j % 2 == 1 else None)
                first = False
                if j % 2 == 1:
                    warset("ptr", PT, lastmm, pts)
            bld.wait("t", ST, exstr)
            av = bld.mm(PC[hh][0:65, 0:512], VSB[0:32, hh * 65: hh * 65 + 65],
                        PTSB[0:32, pts2 * 512:(pts2 + 1) * 512],
                        start=False, stop=True, inc=PT)
            warset("ptr2", PT, av, pts2)
            if u + 1 < len(units):
                sstrpv[u + 1] = emit_sstr(u + 1)
            ysc = ((bb % 2) * 2 + hh) * 512
            bld.wait("v", PT, av)
            evv = bld.op("v", "tensor_copy", YS[:, ysc:ysc + 512],
                         PC[hh][0:65, 0:512], inc=VT)
            pc_free[hh][0] = (VT, evv)
            bld.wait("s", VT, evv)
            l1 = bld.op("s", "activation", YS[64:65, ysc:ysc + 512],
                        YS[64:65, ysc:ysc + 512], AF.Ln, inc=ST)
            bld.wait("s", ST, l1)
            warwait("s", "rrow", hh)
            rv = bld.op("s", "activation", PTSB[32:33, hh * 512:(hh + 1) * 512],
                        YS[64:65, ysc:ysc + 512], AF.Exp,
                        scale=-1.0, inc=ST)
            pend.append((bb, hh, ysc, u, rv, evv))
        flush_norm()
        return yv

    a_last = None
    xb_dead_pt = [0]
    load_wchunk("q", dr["wqT"], 0)
    load_wchunk("k", dr["wkT"], 0)
    load_wchunk("v", dr["wvT"], 0)
    for c in range(NC):
        vs_ev = qkv_chunk(c)
        wbuf_freed["q"][0] = bld.cnt[PT]
        wbuf_freed["k"][0] = bld.cnt[PT]
        wbuf_freed["v"][0] = bld.cnt[PT]
        if c + 1 < NC:
            load_wchunk("q", dr["wqT"], c + 1)
            load_wchunk("k", dr["wkT"], c + 1)
            load_wchunk("v", dr["wvT"], c + 1)
        if c == NC - 1:
            xb_dead_pt[0] = bld.cnt[PT]
        bld.wait("s", ST, ring_ready["q"])
        bld.wait("s", ST, ring_ready["k"])
        emit_bound(QRG, KRG, PM, pm_free)
        a_last = attn_outer_dual(c, vs_ev)
        ring_free["q"][0] = (PT, bld.cnt[PT])
        ring_free["k"][0] = (PT, bld.cnt[PT])
        ring_free["v"][0] = (PT, bld.cnt[PT])
        vs_free[0] = (PT, bld.cnt[PT])
    tap_dump("ao", AOBF[:, :])
    tap_dump("yraw", YS[0:64, :]); tap_dump("sums", YS[64:65, :])
    tap_dump("sumsb", YS[64:65, :]); tap_dump("mbb", MBB[:, :])

    wo_ready = []
    for k in range(NC):
        gv = stage_load(dr["woT"][k * 128:(k + 1) * 128, :], 1024,
                        XBBF[:, 8192 + k * 1024: 8192 + (k + 1) * 1024],
                        pre_gwait=(PT, xb_dead_pt[0]))
        wo_ready.append(gv)

    def proj_to_res(wbase, src_bf, ps_pair, ps_free, wready):
        for oc in range(NC):
            for tg in range(NTG):
                pi = (oc * NTG + tg) % 2
                sem, val = ps_free[pi][0]
                bld.wait("t", sem, val)
                for k in range(NC):
                    bld.wait("t", GT, wready[k])
                    pv = bld.mm(ps_pair[pi][:, :],
                                XBBF[:, wbase + k * 1024 + oc * 128:
                                     wbase + k * 1024 + (oc + 1) * 128],
                                src_bf[:, k * T + tg * 512: k * T + tg * 512 + 512],
                                start=(k == 0), stop=(k == NC - 1),
                                inc=PT if k == NC - 1 else None)
                bld.wait("v", PT, pv)
                rv = bld.op("v", "tensor_add",
                            RES[:, oc * T + tg * 512: oc * T + tg * 512 + 512],
                            RES[:, oc * T + tg * 512: oc * T + tg * 512 + 512],
                            ps_pair[pi][:, :], inc=VT)
                ps_free[pi][0] = (VT, rv)

    bld.wait("t", VT, a_last)
    proj_to_res(8192, AOBF, PA, pa_free, wo_ready)

    def feature_ln(with_mean, gamma, out_bf, inplace, f32_sink=False,
                   ps_pair=None, ps_free=None, router_ctx=None, h_fused=None):
        bld.wait("s", VT, bld.cnt[VT])
        bld.wait("t", VT, bld.cnt[VT])
        for tg in range(NTG):
            for pi in range(2):
                sem, val = ps_free[pi][0]
                bld.wait("t", sem, val)
            pv_sum = None
            pv_sq = None
            for k in range(NC):
                if with_mean:
                    warwait("v", "lnsb")
                    cv = bld.op("v", "tensor_copy", LNSB[:, :],
                                RES[:, k * T + tg * 512: k * T + tg * 512 + 512], inc=VT)
                    bld.wait("t", VT, cv)
                    pv_sum = bld.mm(ps_pair[0][0:1, 0:512], ONEC_BF[:, :], LNSB[:, :],
                                    start=(k == 0), stop=(k == NC - 1), inc=PT)
                    warset("lnsb", PT, pv_sum)
                warwait("s", "lnsq")
                sv = bld.op("s", "activation", LNSQ[:, :],
                            RES[:, k * T + tg * 512: k * T + tg * 512 + 512],
                            AF.Square, inc=ST)
                bld.wait("t", ST, sv)
                pv_sq = bld.mm(ps_pair[1][0:1, 0:512], ONEC_BF[:, :], LNSQ[:, :],
                               start=(k == 0), stop=(k == NC - 1), inc=PT)
                warset("lnsq", PT, pv_sq)
            row = tg * 32
            if with_mean:
                bld.wait("s", PT, pv_sum)
                e1 = bld.op("s", "copy", STA[row:row + 1, :],
                            ps_pair[0][0:1, 0:512], inc=ST)
                ps_free[0][0] = (ST, e1)
            bld.wait("s", PT, pv_sq)
            e2 = bld.op("s", "copy", STB[row:row + 1, :],
                        ps_pair[1][0:1, 0:512], inc=ST)
            ps_free[1][0] = (ST, e2)
        bld.wait("v", ST, bld.cnt[ST])
        bld.wait("v", GT, zmem)
        if with_mean:
            v1 = bld.op("v", "tensor_scalar_mul", STA[:, :], STA[:, :],
                        1.0 / D, inc=VT)
            bld.wait("v", VT, v1)
            v2 = bld.op("v", "tensor_mul", STC[:, :], STA[:, :], STA[:, :], inc=VT)
            bld.wait("v", VT, v2)
            v3 = bld.op("v", "scalar_tensor_tensor", STB[:, :], STB[:, :],
                        1.0 / D, STC[:, :], op0=ALU.mult, op1=ALU.subtract, inc=VT)
        else:
            v3 = bld.op("v", "tensor_scalar_mul", STB[:, :], STB[:, :],
                        1.0 / D, inc=VT)
        bld.wait("v", VT, v3)
        v4 = bld.op("v", "tensor_scalar_add", STB[:, :], STB[:, :], EPS, inc=VT)
        bld.wait("s", VT, v4)
        s1 = bld.op("s", "activation", STB[:, :], STB[:, :], AF.Ln, inc=ST)
        bld.wait("s", ST, s1)
        s2 = bld.op("s", "activation", STB[:, :], STB[:, :], AF.Exp,
                    scale=-0.5, inc=ST)
        v5 = None
        if with_mean:
            bld.wait("v", ST, s2)
            v5 = bld.op("v", "tensor_mul", STC[:, :], STA[:, :], STB[:, :], inc=VT)
        r1s = None
        if h_fused is not None:
            bld.wait("v", VT, v5)
            v6 = bld.op("v", "tensor_mul", STA[:, :], STB[:, :], STB[:, :], inc=VT)
            bld.wait("v", VT, v6)
            v7 = bld.op("v", "tensor_scalar", STA[:, :], STA[:, :],
                        -EPS, 1.0 + EPS, op0=ALU.mult, op1=ALU.add, inc=VT)
            bld.wait("s", VT, v7)
            s3 = bld.op("s", "activation", STA[:, :], STA[:, :], AF.Ln, inc=ST)
            bld.wait("s", ST, s3)
            r1s = bld.op("s", "activation", STA[:, :], STA[:, :], AF.Exp,
                         scale=-0.5, inc=ST)
        bc_war = getattr(feature_ln, "_bc_war", None)
        if bc_war is None:
            bc_war = [(PT, 0)]
            feature_ln._bc_war = bc_war
        for tg in range(NTG):
            row = tg * 32
            sem, val = ps_free[0][0]
            bld.wait("t", sem, val)
            sem, val = bc_war[0]
            bld.wait("s", sem, val)
            bld.wait("s", ST, s2)
            if with_mean:
                bld.wait("s", VT, v5)
            cb1 = bld.op("s", "copy", BCR[:, :], STB[row:row + 1, :], inc=ST)
            bld.wait("t", ST, cb1)
            rbv = bld.mm(ps_pair[0][:, :], ONER_F[0:1, 0:128],
                         BCR[:, :], start=True, stop=True, inc=PT)
            b2 = None
            if with_mean:
                sem, val = ps_free[1][0]
                bld.wait("t", sem, val)
                bld.wait("s", PT, rbv)
                cb2 = bld.op("s", "copy", BCR[:, :], STC[row:row + 1, :], inc=ST)
                bld.wait("t", ST, cb2)
                b2 = bld.mm(ps_pair[1][:, :], ONER_F[0:1, 0:128],
                            BCR[:, :], start=True, stop=True, inc=PT)
            bc_war[0] = (PT, bld.cnt[PT])
            bld.wait("v", PT, bld.cnt[PT])
            last_v = None
            for k in range(NC):
                xa = RES[:, k * T + tg * 512: k * T + tg * 512 + 512]
                if inplace:
                    bld.op("v", "tensor_mul", xa, xa, ps_pair[0][:, :])
                    last_v = bld.op("v", "tensor_sub", xa, xa, ps_pair[1][:, :], inc=VT)
                    continue
                if out_bf is not None:
                    dstt, base = out_bf
                    dst = dstt[:, base + k * T + tg * 512: base + k * T + tg * 512 + 512]
                    if with_mean:
                        sgs = 0
                        warwait("v", "sgr", sgs)
                        sf = SGR[:, 0:512]
                        bld.op("v", "tensor_mul", sf, xa, ps_pair[0][:, :])
                        fv0 = bld.op("v", "tensor_sub", sf, sf, ps_pair[1][:, :], inc=VT)
                        bld.wait("v", VT, fv0)
                        if f32_sink:
                            fv = bld.op("v", "tensor_scalar_mul", sf, sf,
                                        gamma[:, k:k + 1], inc=VT)
                            bld.wait("v", VT, fv)
                            last_v = bld.op("v", "tensor_copy", dst, sf, inc=VT)
                            bld.wait("t", VT, fv)
                            rpv = bld.mm(router_ctx["psum"][tg][0:7, 0:512],
                                         RWT[:, k * 7:(k + 1) * 7], sf,
                                         start=(k == 0), stop=(k == NC - 1), inc=PT)
                            router_ctx["mark"][tg] = rpv
                            warset("sgr", PT, rpv, sgs)
                        else:
                            if gamma is not None:
                                last_v = bld.op("v", "tensor_scalar_mul", dst, sf,
                                                gamma[:, k:k + 1], inc=VT)
                            else:
                                last_v = bld.op("v", "tensor_copy", dst, sf, inc=VT)
                            warset("sgr", VT, last_v, sgs)
                    elif gamma is not None:
                        last_v = bld.op("v", "scalar_tensor_tensor", dst, xa,
                                        gamma[:, k:k + 1], ps_pair[0][:, :],
                                        op0=ALU.mult, op1=ALU.mult, inc=VT)
                    else:
                        last_v = bld.op("v", "tensor_mul", dst, xa,
                                        ps_pair[0][:, :], inc=VT)
            if h_fused is not None:
                dstt, base, gam = h_fused
                sem, val = bc_war[0]
                bld.wait("s", sem, val)
                bld.wait("s", ST, r1s)
                cb3 = bld.op("s", "copy", BCR[:, :], STA[row:row + 1, :], inc=ST)
                bld.wait("t", ST, cb3)
                bld.wait("t", VT, last_v)
                b3 = bld.mm(ps_pair[0][:, :], ONER_F[0:1, 0:128], BCR[:, :],
                            start=True, stop=True, inc=PT)
                bc_war[0] = (PT, bld.cnt[PT])
                bld.wait("v", PT, b3)
                for k in range(NC):
                    xa = RES[:, k * T + tg * 512: k * T + tg * 512 + 512]
                    dst = dstt[:, base + k * T + tg * 512:
                               base + k * T + tg * 512 + 512]
                    last_v = bld.op("v", "scalar_tensor_tensor", dst, xa,
                                    gam[:, k:k + 1], ps_pair[0][:, :],
                                    op0=ALU.mult, op1=ALU.mult, inc=VT)
            ps_free[0][0] = (VT, last_v)
            if b2 is not None:
                ps_free[1][0] = (VT, last_v)
        return bld.cnt[VT]

    h_done = feature_ln(True, None, None, True, ps_pair=PB, ps_free=pb_free,
                        h_fused=(XBBF, 0, G1))
    tap_dump("x1", RES[:, :])
    tap_dump("h", XBBF[:, :])

    if upto == "A":
        actx.close(); atc.close(); aoctx.close(); cglobal.close(); abctx.close()
        _finish(bld)
        return nc, sorted(tap_out.keys())

    actx.close()
    a_end_pt = bld.cnt[PT]
    a_end_vt = bld.cnt[VT]
    bctx = ExitStack()
    KVL = bld.sb("KVL", [128, 2 * T], BF16, stack=bctx)
    QL = bld.sb("QL", [128, 2 * T], BF16, stack=bctx)
    QR2 = bld.sb("QR2", [128, T], BF16, stack=bctx)
    KR2 = bld.sb("KR2", [128, T], BF16, stack=bctx)
    VR2 = bld.sb("VR2", [128, NTT * 130], BF16, stack=bctx)
    WKU = bld.sb("WKU", [128, 256], BF16, stack=bctx)
    WQU = bld.sb("WQU", [128, 256], BF16, stack=bctx)
    WVU = bld.sb("WVU", [128, 256], BF16, stack=bctx)
    WRQ = bld.sb("WRQ", [128, 128], BF16, stack=bctx)
    WRK = bld.sb("WRK", [128, 512], BF16, stack=bctx)
    CS = bld.sb("CS", [128, 512], F32, stack=bctx)
    RSC = SGR
    RTB = bld.sb("RTB", [128, 2 * 256], BF16, stack=bctx)
    PB2 = [bld.ps("PB2%d" % i, [128, 512], F32, stack=bctx) for i in range(5)]
    PCAV = [bld.ps("PCAVa", [128, 512], F32, stack=bctx),
            bld.ps("PCAVb", [128, 512], F32, stack=bctx)]
    PRBF = bld.ps("PRBF", [128, 1024], BF16, stack=bctx)

    pb2_free = [[(PT, 0)] for _ in range(5)]
    pcav_free = [[(PT, 0)], [(PT, 0)]]
    prbf_free = [(PT, 0)]

    WKVD = QR2
    WQD = KR2
    kvd_ready = None
    a_guard = [(PT, a_end_pt), (VT, a_end_vt)]
    for kk in range(2):
        stage_load(dr["kvdT"][kk * 512:(kk + 1) * 512, :]
                   .rearrange("(k p) f -> p k f", p=128), 1024,
                   WKVD[:, kk * 1024:(kk + 1) * 1024]
                   .rearrange("p (k f) -> p k f", k=4), pre_gwait=a_guard)
        kvd_ready = stage_load(dr["qdT"][kk * 512:(kk + 1) * 512, :]
                               .rearrange("(k p) f -> p k f", p=128), 1024,
                               WQD[:, kk * 1024:(kk + 1) * 1024]
                               .rearrange("p (k f) -> p k f", k=4), pre_gwait=a_guard)

    bld.wait("t", VT, h_done)
    bld.wait("t", GT, kvd_ready)
    lat_done = None
    for (buf, wbuf) in ((KVL, WKVD), (QL, WQD)):
        for lc in range(2):
            for tg in range(NTG):
                pi = tg % 2
                sem, val = pb2_free[pi][0]
                bld.wait("t", sem, val)
                for k in range(NC):
                    pv = bld.mm(PB2[pi][:, :],
                                wbuf[:, k * 256 + lc * 128: k * 256 + (lc + 1) * 128],
                                XBBF[:, k * T + tg * 512: k * T + tg * 512 + 512],
                                start=(k == 0), stop=(k == NC - 1),
                                inc=PT if k == NC - 1 else None)
                bld.wait("s", PT, pv)
                lat_done = bld.op("s", "copy",
                                  buf[:, lc * T + tg * 512: lc * T + tg * 512 + 512],
                                  PB2[pi][:, :], inc=ST)
                pb2_free[pi][0] = (ST, lat_done)
    tap_dump("lat", KVL[:, :])

    ring2_free = {"q": [(PT, bld.cnt[PT])], "k": [(PT, bld.cnt[PT])],
                  "v": [(PT, a_end_pt)]}
    war["cs"][0][0] = (VT, a_end_vt)
    war["rtb"][0][0] = (PT, a_end_pt)
    war["rtb"][1][0] = (PT, a_end_pt)
    up_freed = [0, 0]

    lu_cast_done = [0]

    def load_up(c):
        pg = [(PT, max(up_freed[0], a_end_pt)), (VT, a_end_vt)]
        srcs = [
            (dr["kuT"][:, c * 128:(c + 1) * 128]
             .rearrange("(k p) f -> p k f", p=128), 0, 256,
             WKU[:, :].rearrange("p (k f) -> p k f", k=2)),
            (dr["quT"][:, c * 128:(c + 1) * 128]
             .rearrange("(k p) f -> p k f", p=128), 256, 256,
             WQU[:, :].rearrange("p (k f) -> p k f", k=2)),
            (dr["vuT"][:, c * 128:(c + 1) * 128]
             .rearrange("(k p) f -> p k f", p=128), 512, 256,
             WVU[:, :].rearrange("p (k f) -> p k f", k=2)),
            (dr["rqT"][:, c * 64:(c + 1) * 64]
             .rearrange("(k p) f -> p k f", p=128), 768, 128,
             WRQ[:, :].rearrange("p (k f) -> p k f", k=2)),
            (dr["rkT"][:, c * 64:(c + 1) * 64]
             .rearrange("(k p) f -> p k f", p=128), 896, 512,
             WRK[:, :].rearrange("p (k f) -> p k f", k=NC)),
        ]
        bld.wait("y", GT, max(lu_cast_done[0], stg_free[0], stg_free[1]))
        for src_ap, off, cols, _dst in srcs:
            bld.op("y", "dma_start", inc=LU, incby=16,
                   out=STG[:, off:off + cols], in_=src_ap)
        full = bld.cnt[LU]
        out = []
        for _src, off, cols, dst in srcs:
            for sem, val in pg:
                bld.wait("g", sem, val)
            bld.wait("g", LU, full)
            gv = bld.op("g", "tensor_copy", dst, STG[:, off:off + cols], inc=GT)
            out.append(gv)
        lu_cast_done[0] = out[-1]
        return tuple(out)

    def mla_qkv(c, wready):
        def up_group(mat, tg):
            wbuf = WQU if mat == "q" else WKU
            ring = QR2 if mat == "q" else KR2
            src = QL if mat == "q" else KVL
            pi = 2 + tg % 2
            sem, val = pb2_free[pi][0]
            bld.wait("t", sem, val)
            for lc in range(2):
                pv = bld.mm(PB2[pi][:, :],
                            wbuf[:, lc * 128:(lc + 1) * 128],
                            src[:, lc * T + tg * 512: lc * T + tg * 512 + 512],
                            start=(lc == 0), stop=(lc == 1),
                            inc=PT if lc == 1 else None)
            bld.wait("s", PT, pv)
            ev = bld.op("s", "copy", ring[:, tg * 512: tg * 512 + 512],
                        PB2[pi][:, :], inc=ST)
            pb2_free[pi][0] = (ST, ev)

        def vup_tb(tb, mz):
            pi = 2 + tb % 2
            sem, val = pb2_free[pi][0]
            bld.wait("t", sem, val)
            for lc in range(2):
                pv = bld.mm(PB2[pi][:, 0:128],
                            KVL[:, lc * T + tb * 128: lc * T + (tb + 1) * 128],
                            WVU[:, lc * 128:(lc + 1) * 128],
                            start=(lc == 0), stop=(lc == 1),
                            inc=PT if lc == 1 else None)
            bld.wait("s", PT, pv)
            bld.wait("s", GT, mz)
            base = tb * 130
            bld.op("s", "copy", VR2[:, base: base + 64], PB2[pi][:, 0:64])
            ev = bld.op("s", "copy", VR2[:, base + 65: base + 129],
                        PB2[pi][:, 64:128], inc=ST)
            pb2_free[pi][0] = (ST, ev)

        def rope_group(mat, ring, wr, g):
            tb0 = g * 4
            warwait("g", "cs", 0)
            bld.op("g", "dma_start", inc=CSLD, incby=16,
                   out=CS[:, 0:256].rearrange("p (j f) -> p j f", j=4),
                   in_=dr["cosE"][tb0 * 128:(tb0 + 4) * 128, :]
                   .rearrange("(j p) f -> p j f", p=128))
            sv_ = bld.op("g", "dma_start", inc=CSLD, incby=16,
                         out=CS[:, 256:512].rearrange("p (j f) -> p j f", j=4),
                         in_=dr["sinE"][tb0 * 128:(tb0 + 4) * 128, :]
                         .rearrange("(j p) f -> p j f", p=128))
            pi = g % 2
            sem, val = pb2_free[pi][0]
            bld.wait("t", sem, val)
            bld.wait("t", GT, wr)
            for j in range(4):
                tb = tb0 + j
                if mat == "q":
                    for lc in range(2):
                        pv = bld.mm(PB2[pi][:, j * 64:(j + 1) * 64],
                                    QL[:, lc * T + tb * 128: lc * T + (tb + 1) * 128],
                                    WRQ[:, lc * 64:(lc + 1) * 64],
                                    start=(lc == 0), stop=(lc == 1),
                                    inc=PT if (lc == 1 and j == 3) else None)
                else:
                    for k in range(NC):
                        pv = bld.mm(PB2[pi][:, j * 64:(j + 1) * 64],
                                    XBBF[:, k * T + tb * 128: k * T + (tb + 1) * 128],
                                    WRK[:, k * 64:(k + 1) * 64],
                                    start=(k == 0), stop=(k == NC - 1),
                                    inc=PT if (k == NC - 1 and j == 3) else None)
            bld.wait("v", PT, pv)
            bld.wait("v", CSLD, sv_)
            bld.op("v", "tensor_scalar_mul",
                   RSC[0:128, 0:256].rearrange("p (h d) -> p h d", d=32)[:, :, 0:16],
                   PB2[pi][0:128, 0:256].rearrange("p (h d) -> p h d", d=32)[:, :, 16:32],
                   -1.0)
            x2v = bld.op("v", "tensor_scalar_mul",
                         RSC[0:128, 0:256].rearrange("p (h d) -> p h d", d=32)[:, :, 16:32],
                         PB2[pi][0:128, 0:256].rearrange("p (h d) -> p h d", d=32)[:, :, 0:16],
                         1.0, inc=VT)
            bld.op("v", "tensor_mul", RSC[:, 256:512], PB2[pi][:, 0:256],
                   CS[:, 0:256])
            bld.wait("v", VT, x2v)
            x3v = bld.op("v", "tensor_mul", RSC[:, 0:256], RSC[:, 0:256],
                         CS[:, 256:512], inc=VT)
            warset("cs", VT, x3v, 0)
            rs = g % 2
            warwait("v", "rtb2", rs)
            bld.wait("v", VT, x3v)
            x4v = bld.op("v", "tensor_add", RTB[:, rs * 256:rs * 256 + 256],
                         RSC[:, 256:512], RSC[:, 0:256], inc=VT)
            pb2_free[pi][0] = (VT, x4v)
            for jp in range(2):
                tbA = tb0 + 2 * jp
                sem, val = prbf_free[0]
                bld.wait("t", sem, val)
                bld.wait("t", VT, x4v)
                tv = bld.op("t", "transpose", PRBF[0:128, 0:128],
                            RTB[:, rs * 256 + jp * 128: rs * 256 + (jp + 1) * 128],
                            IDB[:, :], inc=PT)
                if jp == 1:
                    warset("rtb2", PT, tv, rs)
                bld.wait("s", PT, tv)
                bld.op("s", "copy", ring[0:32, tbA * 128:(tbA + 1) * 128],
                       PRBF[0:32, 0:128])
                bld.op("s", "copy", ring[64:96, tbA * 128:(tbA + 1) * 128],
                       PRBF[32:64, 0:128])
                bld.op("s", "copy", ring[0:32, (tbA + 1) * 128:(tbA + 2) * 128],
                       PRBF[64:96, 0:128])
                e2 = bld.op("s", "copy",
                            ring[64:96, (tbA + 1) * 128:(tbA + 2) * 128],
                            PRBF[96:128, 0:128], inc=ST)
                prbf_free[0] = (ST, e2)

        bld.wait("t", GT, wready[1])
        sem, val = ring2_free["q"][0]
        bld.wait("t", sem, val)
        bld.wait("t", ST, lat_done)
        for tg in range(NTG):
            up_group("q", tg)
        bld.wait("t", GT, wready[0])
        sem, val = ring2_free["k"][0]
        bld.wait("t", sem, val)
        for g in range(NTG):
            up_group("k", g)
            rope_group("q", QR2, wready[3], g)
        bld.wait("t", GT, wready[2])
        sem, val = ring2_free["v"][0]
        bld.wait("g", sem, val)
        mz = bld.op("g", "memset",
                    VR2[0:128, :].rearrange("p (t c) -> p t c", c=130)[:, :, 64:130:65],
                    1.0, inc=GT)
        for g in range(NTG):
            for tb in range(4 * g, 4 * g + 4):
                vup_tb(tb, mz)
            rope_group("k", KR2, wready[4], g)
        if c == 3 and "rtb3" in tap_out:
            bld.wait("s", VT, bld.cnt[VT])
            bld.op("s", "dma_start", out=tap_out["rtb3"][:, :], in_=RTB[:, :],
                   inc=LO, incby=16)
            bld.op("s", "dma_start", out=tap_out["wrk3"][:, :], in_=WRK[:, :],
                   inc=LO, incby=16)
        emit_bound(QR2, KR2, PB2[0], pb2_free[0])
        return bld.cnt[ST]

    def attn_inner_dual(c, ck):
        bld.wait("t", ST, ck)
        yv = None
        seq = []
        for qg in range(NTG):
            for kt in range(4 * qg + 4):
                for hh in (0, 1):
                    seq.append((qg, kt, hh))
        L = len(seq)

        def emit_s2(i):
            qg, kt, hh = seq[i]
            pi = i % 5
            sem, val = pb2_free[pi][0]
            bld.wait("t", sem, val)
            r0 = hh * 64
            r = kt - 4 * qg
            if r < 0:
                return bld.mm(PB2[pi][:, :],
                              KR2[r0:r0 + 64, kt * 128:(kt + 1) * 128],
                              QR2[r0:r0 + 64, qg * 512:(qg + 1) * 512],
                              start=True, stop=True, inc=PT)
            bld.mm(PB2[pi][:, :],
                   KR2[r0:r0 + 64, kt * 128:(kt + 1) * 128],
                   QR2[r0:r0 + 64, qg * 512:(qg + 1) * 512],
                   start=True, stop=False)
            return bld.mm(PB2[pi][:, :], IDB[:, :],
                          MBI[:, 384 - r * 128: 896 - r * 128],
                          start=False, stop=True, inc=PT)

        DEPTH = 4
        pend = []
        s2pv = {}
        for i in range(min(DEPTH, L)):
            s2pv[i] = emit_s2(i)
        for i, (qg, kt, hh) in enumerate(seq):
            pi = i % 5
            if i + DEPTH < L:
                s2pv[i + DEPTH] = emit_s2(i + DEPTH)
            nkt = 4 * qg + 4
            pts = i % 3
            warwait("s", "ptr", pts)
            bld.wait("s", PT, s2pv[i])
            ex = bld.op("s", "activation", PTRB[:, pts * 512:(pts + 1) * 512],
                        PB2[pi][:, :], AF.Exp, bias=MBB[:, hh:hh + 1],
                        scale=0.125, inc=ST)
            pb2_free[pi][0] = (ST, ex)
            while pend and (i - pend[0][0] >= 2 or i == L - 1):
                _, tqg, thh, tysc, trv, tevv = pend.pop(0)
                pi2 = i % 5
                sem, val = pb2_free[pi2][0]
                bld.wait("t", sem, val)
                bld.wait("t", ST, trv)
                bld.wait("t", VT, tevv)
                bv = bld.op("t", "matmul", PB2[pi2][0:64, 0:512],
                            ONER_BF[32:33, 0:64],
                            PTSB[32:33, thh * 512:(thh + 1) * 512],
                            start=True, stop=True, inc=PT)
                warset("rrow", PT, bv, thh)
                bld.wait("v", PT, bv)
                yv = bld.op("v", "tensor_mul",
                            AOBF[thh * 64:thh * 64 + 64,
                                 c * T + tqg * 512: c * T + (tqg + 1) * 512],
                            YS[0:64, tysc:tysc + 512], PB2[pi2][0:64, 0:512],
                            inc=VT)
                pb2_free[pi2][0] = (VT, yv)
                if i < L - 1:
                    break
            if kt == 0:
                sem, val = pcav_free[hh][0]
                bld.wait("t", sem, val)
            bld.wait("t", ST, ex)
            av = bld.mm(PCAV[hh][0:65, 0:512],
                        VR2[:, kt * 130 + hh * 65: kt * 130 + hh * 65 + 65],
                        PTRB[:, pts * 512:(pts + 1) * 512],
                        start=(kt == 0), stop=(kt == nkt - 1), inc=PT)
            warset("ptr", PT, av, pts)
            if kt == nkt - 1:
                r0 = hh * 64
                ysc = ((qg % 2) * 2 + hh) * 512
                bld.wait("v", PT, av)
                evv = bld.op("v", "tensor_copy", YS[:, ysc:ysc + 512],
                             PCAV[hh][0:65, 0:512], inc=VT)
                pcav_free[hh][0] = (VT, evv)
                bld.wait("s", VT, evv)
                l1 = bld.op("s", "activation", YS[64:65, ysc:ysc + 512],
                            YS[64:65, ysc:ysc + 512], AF.Ln, inc=ST)
                bld.wait("s", ST, l1)
                warwait("s", "rrow", hh)
                rv = bld.op("s", "activation",
                            PTSB[32:33, hh * 512:(hh + 1) * 512],
                            YS[64:65, ysc:ysc + 512], AF.Exp,
                            scale=-1.0, inc=ST)
                pend.append((i, qg, hh, ysc, rv, evv))
                if i == L - 1:
                    while pend:
                        _, tqg, thh, tysc, trv, tevv = pend.pop(0)
                        pi2 = i % 5
                        sem, val = pb2_free[pi2][0]
                        bld.wait("t", sem, val)
                        bld.wait("t", ST, trv)
                        bld.wait("t", VT, tevv)
                        bv = bld.op("t", "matmul", PB2[pi2][0:64, 0:512],
                                    ONER_BF[32:33, 0:64],
                                    PTSB[32:33, thh * 512:(thh + 1) * 512],
                                    start=True, stop=True, inc=PT)
                        warset("rrow", PT, bv, thh)
                        bld.wait("v", PT, bv)
                        yv = bld.op("v", "tensor_mul",
                                    AOBF[thh * 64:thh * 64 + 64,
                                         c * T + tqg * 512: c * T + (tqg + 1) * 512],
                                    YS[0:64, tysc:tysc + 512],
                                    PB2[pi2][0:64, 0:512], inc=VT)
                        pb2_free[pi2][0] = (VT, yv)
        return yv

    b_last = None
    h_dead_pt = [0]
    wready = load_up(0)
    for c in range(NC):
        ck = mla_qkv(c, wready)
        up_freed[0] = bld.cnt[PT]
        if c + 1 < NC:
            wready = load_up(c + 1)
        if c == NC - 1:
            h_dead_pt[0] = bld.cnt[PT]
        tap_dump("kc%d" % c, KR2[:, :])
        tap_dump("mb%d" % c, MBB[:, :])
        b_last = attn_inner_dual(c, ck)
        for mat in ("q", "k", "v"):
            ring2_free[mat][0] = (PT, bld.cnt[PT])
        up_freed[0] = bld.cnt[PT]
    tap_dump("yraw2", YS[0:64, :]); tap_dump("sums2", YS[64:65, :])
    tap_dump("sumsb2", YS[64:65, :]); tap_dump("mbb2", MBB[:, :])
    tap_dump("yb", AOBF[:, :])

    stg_free[0] = max(stg_free[0], lu_cast_done[0])
    stg_free[1] = max(stg_free[1], lu_cast_done[0])
    ow_ready = []
    for k in range(NC):
        gv = stage_load(dr["owT2"][k * 128:(k + 1) * 128, :], 1024,
                        XBBF[:, k * 1024:(k + 1) * 1024],
                        pre_gwait=(PT, h_dead_pt[0]))
        ow_ready.append(gv)
    bld.wait("t", VT, b_last)
    proj_to_res(0, AOBF, PB2, pb2_free, ow_ready)
    tap_dump("x2", RES[:, :])

    if upto == "B":
        bctx.close(); atc.close(); aoctx.close(); cglobal.close(); abctx.close()
        _finish(bld)
        return nc, sorted(tap_out.keys())

    bctx.close()
    atc.close()

    cps = ExitStack()
    PRT = [bld.ps("PRT%d" % i, [128, 512], F32, stack=cps) for i in range(NTG)]
    PB3 = [bld.ps("PB3a", [128, 512], F32, stack=cps),
           bld.ps("PB3b", [128, 512], F32, stack=cps)]
    pb3_free = [[(PT, 0)], [(PT, 0)]]

    rctx = {"psum": PRT, "mark": [0] * NTG}
    h2_done = feature_ln(True, G2, (XBBF, 0), False, f32_sink=True,
                         ps_pair=PB3, ps_free=pb3_free, router_ctx=rctx)
    tap_dump("h2", XBBF[:, :])
    lgt_war = [(PT, 0)]
    for tg in range(NTG):
        bld.wait("t", PT, rctx["mark"][tg])
        bv = bld.mm(PRT[tg][0:7, 0:512], RBIAS[:, :], ONER_F[0:1, 0:512],
                    start=False, stop=True, inc=PT)
        sem, val = lgt_war[0]
        bld.wait("s", sem, val)
        bld.wait("s", PT, bv)
        lv = bld.op("s", "copy", LGT[0:7, :], PRT[tg][0:7, 0:512], inc=ST)
        for j in range(4):
            tb = tg * 4 + j
            pi = tb % 2
            sem, val = pb3_free[pi][0]
            bld.wait("t", sem, val)
            bld.wait("t", ST, lv)
            tv = bld.op("t", "transpose", PB3[pi][0:128, 0:7],
                        LGT[0:7, j * 128:(j + 1) * 128], IDF[0:7, 0:7], inc=PT)
            bld.wait("s", PT, tv)
            sg = bld.op("s", "activation", PRBp[:, tb * 8: tb * 8 + 7],
                        PB3[pi][0:128, 0:7], AF.Sigmoid, inc=ST)
            pb3_free[pi][0] = (ST, sg)
            lgt_war[0] = (PT, tv)
    prb_done = bld.cnt[ST]

    def emit_top2():
        bld.wait("v", ST, prb_done)
        def P7(tb):
            return PRBp[:, tb * 8: tb * 8 + 7]
        def W7(tb):
            return WGT[:, tb * 8: tb * 8 + 7]
        for tb in range(NTT):
            bld.op("v", "reduce_max", TOPS[:, tb:tb + 1], P7(tb),
                   axis=AX.X, inc=VT)
        bld.wait("v", VT, bld.cnt[VT])
        for tb in range(NTT):
            bld.op("v", "tensor_single_scalar", W7(tb), P7(tb),
                   TOPS[:, tb:tb + 1], op=ALU.is_ge, inc=VT)
        bld.wait("v", VT, bld.cnt[VT])
        for tb in range(NTT):
            bld.op("v", "scalar_tensor_tensor", W7(tb), W7(tb), -1e9, P7(tb),
                   op0=ALU.mult, op1=ALU.add, inc=VT)
        bld.wait("v", VT, bld.cnt[VT])
        for tb in range(NTT):
            bld.op("v", "reduce_max", TOPS[:, 16 + tb:17 + tb], W7(tb),
                   axis=AX.X, inc=VT)
        bld.wait("v", VT, bld.cnt[VT])
        for tb in range(NTT):
            bld.op("v", "tensor_add", TOPS[:, 32 + tb:33 + tb],
                   TOPS[:, tb:tb + 1], TOPS[:, 16 + tb:17 + tb], inc=VT)
        bld.wait("v", VT, bld.cnt[VT])
        bld.op("v", "reciprocal", TOPS[:, 48:64], TOPS[:, 32:48], inc=VT)
        bld.wait("v", VT, bld.cnt[VT])
        for tb in range(NTT):
            bld.op("v", "tensor_single_scalar", W7(tb), P7(tb),
                   TOPS[:, 16 + tb:17 + tb], op=ALU.is_ge, inc=VT)
        bld.wait("v", VT, bld.cnt[VT])
        for tb in range(NTT):
            bld.op("v", "tensor_mul", W7(tb), W7(tb), P7(tb), inc=VT)
        bld.wait("v", VT, bld.cnt[VT])
        for tb in range(NTT):
            bld.op("v", "tensor_scalar_mul", W7(tb), W7(tb),
                   TOPS[:, 48 + tb:49 + tb], inc=VT)
        return bld.cnt[VT]

    def sigma(p):
        tb, k = p // 8, p % 8
        return ((tb % 2) * 8 + k) * 8 + tb // 2

    bld.wait("t", VT, bld.cnt[VT])
    seen = set()
    tp_ping = [0]
    for p0 in range(128):
        if p0 in seen:
            continue
        cyc = [p0]
        seen.add(p0)
        qq_ = sigma(p0)
        while qq_ != p0:
            cyc.append(qq_)
            seen.add(qq_)
            qq_ = sigma(qq_)
        held = None
        for i, p in enumerate(cyc):
            pi = tp_ping[0] % 2
            tp_ping[0] += 1
            sem, val = pb3_free[pi][0]
            bld.wait("t", sem, val)
            tb, k = p // 8, p % 8
            tv = bld.op("t", "transpose", PB3[pi][:, 0:128],
                        RES[:, k * T + tb * 128: k * T + (tb + 1) * 128],
                        IDF[:, :], inc=PT)
            if held is not None:
                hpi, hv = held
                bld.wait("s", PT, tv)
                ev = bld.op("s", "activation",
                            RES[:, k * T + tb * 128: k * T + (tb + 1) * 128],
                            PB3[hpi][:, 0:128], AF.Copy, bias=0.0, scale=2.0, inc=ST)
                pb3_free[hpi][0] = (ST, ev)
            held = (pi, tv)
        tb, k = cyc[0] // 8, cyc[0] % 8
        hpi, hv = held
        bld.wait("s", PT, hv)
        ev = bld.op("s", "activation",
                    RES[:, k * T + tb * 128: k * T + (tb + 1) * 128],
                    PB3[hpi][:, 0:128], AF.Copy, bias=0.0, scale=2.0, inc=ST)
        pb3_free[hpi][0] = (ST, ev)
    x2t_done = bld.cnt[ST]
    X2TOK = RES

    p2ctx = ExitStack()
    OW2B = bld.sb("OW2B", [128, NC * 1024], BF16, stack=p2ctx)
    ow2b_ready = []
    for k in range(NC):
        gv = stage_load(dr["owT2"][k * 128:(k + 1) * 128, :], 1024,
                        OW2B[:, k * 1024:(k + 1) * 1024],
                        pre_gwait=(ST, x2t_done) if k == 0 else None)
        ow2b_ready.append(gv)
    bld.wait("t", ST, x2t_done)
    for tb in range(NTT):
        for no in range(2):
            pi = (tb * 2 + no) % 2
            sem, val = pb3_free[pi][0]
            bld.wait("t", sem, val)
            for k in range(NC):
                bld.wait("t", GT, ow2b_ready[k])
                pv = bld.mm(PB3[pi][:, :],
                            AOBF[:, k * T + tb * 128: k * T + (tb + 1) * 128],
                            OW2B[:, k * 1024 + no * 512: k * 1024 + (no + 1) * 512],
                            start=(k == 0), stop=(k == NC - 1),
                            inc=PT if k == NC - 1 else None)
            bld.wait("v", PT, pv)
            rv = bld.op("v", "tensor_sub",
                        X2TOK[:, tb * D + no * 512: tb * D + (no + 1) * 512],
                        X2TOK[:, tb * D + no * 512: tb * D + (no + 1) * 512],
                        PB3[pi][:, :], inc=VT)
            pb3_free[pi][0] = (VT, rv)
    wgt_done = emit_top2()
    tap_dump("wgt", WGT[:, :])
    tap_dump("prb", PRBp[:, :])
    x2t_done = bld.cnt[VT]
    p2ctx.close()
    cps.close()
    aoctx.close()

    cmain = ExitStack()
    WS = [bld.sb("WS%d" % i, [128, 4 * 1024], BF16, stack=cmain) for i in range(5)]
    ZB = bld.sb("ZB", [128, NC * T], BF16, stack=cmain)
    PG = [bld.ps("PGa", [128, 512], F32, stack=cmain),
          bld.ps("PGb", [128, 512], F32, stack=cmain)]
    PU = [bld.ps("PUa", [128, 512], F32, stack=cmain),
          bld.ps("PUb", [128, 512], F32, stack=cmain)]
    PD = [bld.ps("PDa", [128, 512], F32, stack=cmain),
          bld.ps("PDb", [128, 512], F32, stack=cmain)]
    pg_free = [[(PT, 0)], [(PT, 0)]]
    pu_free = [[(PT, 0)], [(PT, 0)]]
    pd_free = [[(PT, 0)], [(PT, 0)]]
    ws_free = [[(PT, 0)] for _ in range(5)]
    ws_first = [True] * 5
    slot_rr = [0]
    sgr2_war = [[(PT, 0)]]

    def load_piece(e, half, kind, sub2):
        s = slot_rr[0] % 5
        slot_rr[0] += 1
        sem, val = ws_free[s][0]
        gv = None
        for sub in range(4):
            if kind in ("g", "u"):
                dram3 = dr["gatesT"] if kind == "g" else dr["upsT"]
                src = dram3[e, sub * 256:(sub + 1) * 256,
                            half * 1024 + sub2 * 512: half * 1024 + (sub2 + 1) * 512] \
                    .rearrange("(k p) f -> p k f", p=128)
                dst = WS[s][:, sub * 1024:(sub + 1) * 1024] \
                    .rearrange("p (k f) -> p k f", k=2)
            else:
                src = dr["downsT"][e, half * 1024 + sub2 * 512 + sub * 128:
                                   half * 1024 + sub2 * 512 + (sub + 1) * 128, :]
                dst = WS[s][:, sub * 1024:(sub + 1) * 1024]
            ramp = e == 0 and half == 0
            eng = ("s", "v", "g")[sub % 3] if ramp else "g"
            pre = (sem, val) if (sub == 0 or ramp) else None
            if ws_first[s] and (sub == 0 or ramp):
                bld.wait(eng, VT, x2t_done)
            gv = stage_load(src, 1024, dst, pre_gwait=pre, eng=eng)
        ws_first[s] = False
        return s, gv

    def expert(e):
        for half in range(2):
            sg1, gv_g1 = load_piece(e, half, "g", 0)
            sg2, gv_g2 = load_piece(e, half, "g", 1)
            su1, gv_u1 = load_piece(e, half, "u", 0)
            su2, gv_u2 = load_piece(e, half, "u", 1)
            sd1 = sd2 = gv_d1 = gv_d2 = None
            bld.wait("t", VT, h2_done)
            if e == 0 and half == 0:
                bld.wait("v", VT, x2t_done)
            z_last = None
            for ic in range(NC):
                gsl = sg1 if ic < 4 else sg2
                usl = su1 if ic < 4 else su2
                icl = ic % 4
                if ic == 0:
                    bld.wait("t", GT, gv_g1)
                    bld.wait("t", GT, gv_u1)
                    sd1, gv_d1 = load_piece(e, half, "d", 0)
                if ic == 4:
                    bld.wait("t", GT, gv_g2)
                    bld.wait("t", GT, gv_u2)
                for tg in range(NTG):
                    gi = (ic * NTG + tg) % 2
                    sem, val = pg_free[gi][0]
                    bld.wait("t", sem, val)
                    sem, val = pu_free[gi][0]
                    bld.wait("t", sem, val)
                    for hc in range(NC):
                        bld.mm(PG[gi][:, :],
                               WS[gsl][:, hc * 512 + icl * 128: hc * 512 + (icl + 1) * 128],
                               XBBF[:, hc * T + tg * 512: hc * T + tg * 512 + 512],
                               start=(hc == 0), stop=(hc == NC - 1))
                    for hc in range(NC):
                        pv = bld.mm(PU[gi][:, :],
                                    WS[usl][:, hc * 512 + icl * 128: hc * 512 + (icl + 1) * 128],
                                    XBBF[:, hc * T + tg * 512: hc * T + tg * 512 + 512],
                                    start=(hc == 0), stop=(hc == NC - 1),
                                    inc=PT if hc == NC - 1 else None)
                    sem, val = sgr2_war[0][0]
                    bld.wait("s", sem, val)
                    bld.wait("s", PT, pv)
                    sl = bld.op("s", "activation", SGR[:, 0:512],
                                PG[gi][:, :], AF.Silu, inc=ST)
                    pg_free[gi][0] = (ST, sl)
                    bld.wait("v", ST, sl)
                    zv = bld.op("v", "tensor_mul",
                                ZB[:, ic * T + tg * 512: ic * T + tg * 512 + 512],
                                SGR[:, 0:512], PU[gi][:, :], inc=VT)
                    sgr2_war[0][0] = (VT, zv)
                    pu_free[gi][0] = (VT, zv)
                    z_last = zv
                if ic == 3:
                    ws_free[sg1][0] = (PT, bld.cnt[PT])
                    ws_free[su1][0] = (PT, bld.cnt[PT])
                    sd2, gv_d2 = load_piece(e, half, "d", 1)
            ws_free[sg2][0] = (PT, bld.cnt[PT])
            ws_free[su2][0] = (PT, bld.cnt[PT])
            bld.wait("t", GT, gv_d1)
            bld.wait("t", GT, gv_d2)
            bld.wait("t", VT, z_last)
            for tb in range(NTT):
                for hhh in range(2):
                    di = (tb * 2 + hhh) % 2
                    sem, val = pd_free[di][0]
                    bld.wait("t", sem, val)
                    for ic in range(NC):
                        dsl = sd1 if ic < 4 else sd2
                        icl = ic % 4
                        pv = bld.mm(PD[di][:, :],
                                    ZB[:, ic * T + tb * 128: ic * T + (tb + 1) * 128],
                                    WS[dsl][:, icl * 1024 + hhh * 512:
                                            icl * 1024 + (hhh + 1) * 512],
                                    start=(ic == 0), stop=(ic == NC - 1),
                                    inc=PT if ic == NC - 1 else None)
                    bld.wait("v", PT, pv)
                    dst = X2TOK[:, tb * D + hhh * 512: tb * D + (hhh + 1) * 512]
                    scal = ONEC_F[:, 0:1] if e == 0 else WGT[:, tb * 8 + e - 1: tb * 8 + e]
                    av = bld.op("v", "scalar_tensor_tensor", dst, PD[di][:, :], scal, dst,
                                op0=ALU.mult, op1=ALU.add, inc=VT)
                    pd_free[di][0] = (VT, av)
            ws_free[sd1][0] = (PT, bld.cnt[PT])
            ws_free[sd2][0] = (PT, bld.cnt[PT])

    bld.wait("v", VT, wgt_done)
    for e in range(NEXP):
        expert(e)
    tap_dump("x3", X2TOK[:, :])
    cmain.close()

    fctx = ExitStack()
    OBUF = bld.sb("OBUF", [128, 2 * D], F32, stack=fctx)
    FSQ = bld.sb("FSQ", [128, D], F32, stack=fctx)
    FST = bld.sb("FST", [128, 8], F32, stack=fctx)
    moe_done = bld.cnt[VT]
    bld.wait("s", VT, moe_done)
    obuf_war = [[(LO, 0)], [(LO, 0)]]
    for tb in range(NTT):
        xt = X2TOK[:, tb * D:(tb + 1) * D]
        v1 = bld.op("v", "reduce_sum", FST[:, 0:1], xt, axis=AX.X, inc=VT)
        s1 = bld.op("s", "activation", FSQ[:, :], xt, AF.Square,
                    accum_out=FST[:, 1:2], inc=ST)
        bld.wait("v", VT, v1)
        bld.wait("v", ST, s1)
        v2 = bld.op("v", "tensor_scalar_mul", FST[:, 2:3], FST[:, 0:1], 1.0 / D, inc=VT)
        bld.wait("v", VT, v2)
        v3 = bld.op("v", "tensor_mul", FST[:, 3:4], FST[:, 2:3], FST[:, 2:3], inc=VT)
        bld.wait("v", VT, v3)
        v4 = bld.op("v", "scalar_tensor_tensor", FST[:, 4:5], FST[:, 1:2], 1.0 / D,
                    FST[:, 3:4], op0=ALU.mult, op1=ALU.subtract, inc=VT)
        bld.wait("v", VT, v4)
        v5 = bld.op("v", "tensor_scalar_add", FST[:, 4:5], FST[:, 4:5], EPS, inc=VT)
        bld.wait("s", VT, v5)
        s2 = bld.op("s", "activation", FST[:, 5:6], FST[:, 4:5], AF.Ln, inc=ST)
        bld.wait("s", ST, s2)
        s3 = bld.op("s", "activation", FST[:, 5:6], FST[:, 5:6], AF.Exp,
                    scale=-0.5, inc=ST)
        bld.wait("v", ST, s3)
        v6 = bld.op("v", "scalar_tensor_tensor", FST[:, 6:7], FST[:, 2:3], -1.0,
                    FST[:, 5:6], op0=ALU.mult, op1=ALU.mult, inc=VT)
        bld.wait("v", VT, v6)
        ob = tb % 2
        sem, val = obuf_war[ob][0]
        bld.wait("v", sem, val)
        v7 = bld.op("v", "tensor_scalar", OBUF[:, ob * D:(ob + 1) * D], xt,
                    FST[:, 5:6], FST[:, 6:7], op0=ALU.mult, op1=ALU.add, inc=VT)
        bld.wait("s", VT, v7)
        lo = bld.op("s", "dma_start", inc=LO, incby=16,
                    out=out[tb * 128:(tb + 1) * 128, :],
                    in_=OBUF[:, ob * D:(ob + 1) * D])
        obuf_war[ob][0] = (LO, lo)

    fctx.close(); cglobal.close(); abctx.close()
    _finish(bld)
    return nc, sorted(tap_out.keys())




def _finish(bld):
    bld.wait("y", bld.LO, bld.cnt[bld.LO])
    bld.wait("y", bld.LG, bld.cnt[bld.LG])
    _emit_block(bld)
    bld.ctx.close()


def kernel(**inputs):
    key = (BUILD_UPTO, tuple(sorted(TAPS)))
    if key not in _CACHE:
        _CACHE[key] = build_program(upto=BUILD_UPTO, taps=TAPS)
    nc, tap_names = _CACHE[key]
    w = _host_prep(inputs)
    c = _host_constants()
    x = np.asarray(inputs["L0_attn_x"], dtype=np.float32)
    base = {}
    base.update(w)
    base.update(c)
    cores = CORES if CORES is not None else list(range(8))
    in_maps = []
    for core in cores:
        m = dict(base)
        m["x"] = np.ascontiguousarray(x[core].T)
        in_maps.append(m)
    res = run_bass_kernel_spmd(nc, in_maps, core_ids=cores)
    outs = np.stack([np.asarray(res.results[i]["out"]) for i in range(len(cores))],
                    axis=0)
    if tap_names:
        kernel.last_taps = [
            {t: np.asarray(res.results[i]["tap_" + t]) for t in tap_names}
            for i in range(len(cores))]
    if len(cores) < 8:
        full = np.zeros((8, T, D), np.float32)
        full[np.array(cores)] = outs
        return full
    return outs
